# version 01
# baseline (speedup 1.0000x reference)
import math
import numpy as np
import concourse.bass as bass
import concourse.mybir as mybir
from concourse.bass_utils import run_bass_kernel_spmd

F32 = mybir.dt.float32
AF = mybir.ActivationFunctionType
ALU = mybir.AluOpType

SEQ = 4096
D = 1024
T = 512
NCH = SEQ // T
NDC = D // 128
INNER = 512
DS = 8
NH = 8
DH = 64
HID = 2730
NJ = 22
HIDP = NJ * 128
NEG = -1.0e30
TWO_PI = 2.0 * math.pi

CFG = {"stages": 6, "debug_spills": False, "serialize": False}



def _tile_lhsT_nmajor(m, kc, ncnt, nw=128):
    K, N = m.shape
    assert K == kc * 128
    pad = ncnt * nw - N
    if pad:
        m = np.concatenate([m, np.zeros((K, pad), m.dtype)], axis=1)
    return np.ascontiguousarray(
        m.reshape(kc, 128, ncnt, nw).transpose(1, 2, 0, 3).reshape(128, -1))


def _cols(v, nt):
    return np.ascontiguousarray(v.reshape(nt, 128).T)


def host_prep(inputs):
    f = np.float32
    g = {}
    g["w_in"] = _tile_lhsT_nmajor(np.ascontiguousarray(inputs["L0_attn_W_in"].astype(f).T), 8, 8)
    g["w_out"] = _tile_lhsT_nmajor(np.ascontiguousarray(inputs["L0_attn_W_out"].astype(f).T), 8, 8)
    g["w_xp"] = _tile_lhsT_nmajor(np.ascontiguousarray(inputs["L0_attn_W_xproj"].astype(f).T), 4, 1, nw=80)
    g["w_dt"] = np.ascontiguousarray(inputs["L0_attn_W_dt"].astype(f).T)
    g["b_dt"] = _cols(inputs["L0_attn_b_dt"].astype(f), 4)
    g["d_col"] = _cols(inputs["L0_attn_D"].astype(f), 4)
    al = inputs["L0_attn_A_log"].astype(f)
    g["a_cols"] = np.ascontiguousarray(al.reshape(4, 128, DS).transpose(1, 0, 2).reshape(128, 4 * DS))
    g["kx"] = np.ascontiguousarray(inputs["L0_attn_Kx"][:, 0, :].astype(f).reshape(4, 128, 3).transpose(1, 0, 2).reshape(128, 12))
    g["kz"] = np.ascontiguousarray(inputs["L0_attn_Kz"][:, 0, :].astype(f).reshape(4, 128, 3).transpose(1, 0, 2).reshape(128, 12))
    for l in range(2):
        g[f"wqkv{l}"] = _tile_lhsT_nmajor(inputs["L0_ffn_wqkv"][l].astype(f), 8, 12)
        g[f"wo{l}"] = _tile_lhsT_nmajor(inputs["L0_ffn_wo"][l].astype(f), 4, 8)
        w1 = inputs["L0_ffn_w1"][l].astype(f)
        blocks = []
        for jc in range(NJ):
            a = w1[:, jc * 128:min((jc + 1) * 128, HID)]
            gg = w1[:, HID + jc * 128: HID + min((jc + 1) * 128, HID)]
            if a.shape[1] < 128:
                a = np.concatenate([a, np.zeros((D, 128 - a.shape[1]), f)], 1)
                gg = np.concatenate([gg, np.zeros((D, 128 - gg.shape[1]), f)], 1)
            blk = np.concatenate([a, gg], 1)
            blocks.append(blk.reshape(8, 128, 256).transpose(1, 0, 2).reshape(128, 2048))
        g[f"w1_{l}"] = np.ascontiguousarray(np.concatenate(blocks, 1))
        w2 = inputs["L0_ffn_w2"][l].astype(f)
        w2p = np.concatenate([w2, np.zeros((HIDP - HID, D), f)], 0)
        w2r = w2p.reshape(NJ, 128, 8, 128).transpose(1, 0, 2, 3).reshape(128, -1)
        g[f"w2_{l}"] = np.ascontiguousarray(w2r)
        b1 = inputs["L0_ffn_b1"][l].astype(f)
        g[f"b1a{l}"] = _cols(np.concatenate([b1[:HID], np.zeros(HIDP - HID, f)]), NJ)
        g[f"b1g{l}"] = _cols(np.concatenate([b1[HID:], np.zeros(HIDP - HID, f)]), NJ)
        g[f"b2_{l}"] = _cols(inputs["L0_ffn_b2"][l].astype(f), 8)
        g[f"lg{l}"] = _cols(inputs["L0_ffn_ln_g"][l].astype(f), 8)
        g[f"lb{l}"] = _cols(inputs["L0_ffn_ln_b"][l].astype(f), 8)
        g[f"qs{l}"] = np.ascontiguousarray(np.tile(inputs["L0_ffn_q_scale"][l].astype(f), 2)[:, None])
        g[f"ks{l}"] = np.ascontiguousarray(np.tile(inputs["L0_ffn_k_scale"][l].astype(f), 2)[:, None])
    g["ones1"] = np.ones((128, 1), f)
    o2 = np.zeros((128, 2), f)
    o2[:64, 0] = 1.0
    o2[64:, 1] = 1.0
    g["ones2"] = o2
    perm = np.zeros((128, 128), f)
    for i in range(128):
        perm[i ^ 32, i] = 1.0
    g["perm"] = perm
    mc = np.zeros((128, 128), f)
    mp = np.zeros((128, 128), f)
    for k in range(128):
        for q in range(128):
            if k > q:
                mc[k, q] = NEG
            if k < q:
                mp[k, q] = NEG
    g["mask_cur"] = np.concatenate([np.full((128, 384), NEG, f), mc], 1)
    g["mask_prev"] = np.concatenate([mp, np.full((128, 384), NEG, f)], 1)
    g["trow"] = np.arange(SEQ, dtype=f)[None, :]
    g["hrow"] = np.tile(np.arange(0, DH, 2, dtype=f), 2)[None, :]
    sel16 = np.zeros((16, 16 * 128), f)
    for sv in range(16):
        sel16[sv, sv * 128:(sv + 1) * 128] = 1.0
    g["sel16"] = sel16
    for nm, row in (("sel0", 0), ("sel64", 64), ("sel96", 96)):
        m = np.zeros((128, 128), f)
        m[row, :] = 1.0
        g[nm] = m
    selrn = np.zeros((128, 128), f)
    selrn[0, 0:64] = 1.0
    selrn[1, 64:128] = 1.0
    g["selrn"] = selrn
    return g


IN_SHAPES = {
    "x": (D, SEQ),
    "w_in": (128, 8192), "w_out": (128, 8192), "w_xp": (128, 320),
    "w_dt": (64, 512), "b_dt": (128, 4), "d_col": (128, 4), "a_cols": (128, 32),
    "kx": (128, 12), "kz": (128, 12), "ones1": (128, 1), "ones2": (128, 2),
    "perm": (128, 128), "mask_cur": (128, 512), "mask_prev": (128, 512),
    "trow": (1, SEQ), "hrow": (1, 64),
    "sel16": (16, 2048), "sel0": (128, 128), "sel64": (128, 128),
    "sel96": (128, 128), "selrn": (128, 128),
}
for _l in range(2):
    IN_SHAPES.update({
        f"wqkv{_l}": (128, 12288), f"wo{_l}": (128, 4096),
        f"w1_{_l}": (128, NJ * 2048),
        f"b1a{_l}": (128, NJ), f"b1g{_l}": (128, NJ), f"b2_{_l}": (128, 8),
        f"lg{_l}": (128, 8), f"lb{_l}": (128, 8), f"qs{_l}": (128, 1), f"ks{_l}": (128, 1),
        f"w2_{_l}": (128, NJ * 1024),
    })



ENGS = ("sync", "tensor", "vector", "scalar", "gpsimd")
QUEUES = {"qw": "sync", "qg": "gpsimd"}


class _Op:
    __slots__ = ("stream", "emit", "flag", "val", "waits", "isdma", "dmaidx", "snap")

    def __init__(self, stream, emit, isdma=False):
        self.stream = stream
        self.emit = emit
        self.flag = False
        self.val = None
        self.waits = []
        self.isdma = isdma
        self.dmaidx = None
        self.snap = None


class Buf:
    __slots__ = ("name", "writer", "readers")

    def __init__(self, name):
        self.name = name
        self.writer = None
        self.readers = {}


class KB:
    def __init__(self, nc, serialize=False):
        self.nc = nc
        self.streams = {e: [] for e in ENGS}
        self.dmacount = {q: 0 for q in QUEUES}
        self.allops = []
        self.serialize = serialize
        self.lastop = {}

    def buf(self, name):
        return Buf(name)

    def _add(self, stream, emit, reads, writes, isdma=False):
        eng = QUEUES[stream] if stream in QUEUES else stream
        op = _Op(stream, emit, isdma=isdma)
        op.snap = dict(self.dmacount)
        if isdma:
            op.dmaidx = self.dmacount[stream]
            self.dmacount[stream] += 1
        deps = []
        for b in reads:
            if b.writer is not None:
                deps.append(b.writer)
        for b in writes:
            if b.writer is not None:
                deps.append(b.writer)
            deps.extend(b.readers.values())
        if self.serialize:
            deps.extend(self.lastop.values())
        for (cs, ps_) in CFG.get("ser_pairs", ()):
            if stream == cs and ps_ in self.lastop:
                deps.append(self.lastop[ps_])
        seen = set()
        for d in deps:
            if id(d) in seen:
                continue
            seen.add(id(d))
            if d.stream == stream:
                continue
            op.waits.append(d)
        self.streams[eng].append(op)
        self.allops.append(op)
        self.lastop[stream] = op
        for b in reads:
            b.readers[stream] = op
        for b in writes:
            b.writer = op
            b.readers = {}
        return op

    def op(self, eng, emit, reads=(), writes=()):
        return self._add(eng, emit, reads, writes, isdma=False)

    def dma(self, q, out_ap, in_ap, reads=(), writes=()):
        def emit(eng, _o=out_ap, _i=in_ap):
            return eng.dma_start(out=_o, in_=_i)
        return self._add(q, emit, reads, writes, isdma=True)

    def finalize_and_emit(self, sems):
        for op in self.allops:
            for d in op.waits:
                if not d.isdma:
                    d.flag = True
        for e in ENGS:
            v = 0
            for op in self.streams[e]:
                if op.isdma:
                    continue
                if op.flag:
                    v += 1
                    op.val = v

        def tgt(d, consumer):
            if d.isdma:
                return (d.stream, 16 * max(d.dmaidx + 1, consumer.snap.get(d.stream, 0)))
            return (d.stream, d.val)

        emitted = {e: [] for e in ENGS}
        for e in ENGS:
            hi = {}
            for op in self.streams[e]:
                for d in op.waits:
                    s, v = tgt(d, op)
                    assert v is not None
                    if hi.get(s, 0) >= v:
                        continue
                    hi[s] = v
                    emitted[e].append(("w", s, v))
                inc = (op.stream, 16) if op.isdma else ((op.stream, 1) if op.flag else None)
                emitted[e].append(("o", op.emit, inc))
        nc = self.nc
        with nc.Block() as block:
            def mk(e):
                def run(eng):
                    for item in emitted[e]:
                        if item[0] == "w":
                            eng.wait_ge(sems[item[1]], item[2])
                        else:
                            h = item[1](eng)
                            if item[2] is not None:
                                h.then_inc(sems[item[2][0]], item[2][1])
                return run
            block.sync(mk("sync"))
            block.tensor(mk("tensor"))
            block.vector(mk("vector"))
            block.scalar(mk("scalar"))
            block.gpsimd(mk("gpsimd"))



def build_program(debug_spills=False, stages=6):
    nc = bass.Bass("TRN2", target_bir_lowering=False)
    f = F32

    dr = {n: nc.dram_tensor(n, list(s), f, kind="ExternalInput").ap()
          for n, s in IN_SHAPES.items()}
    scr = {}
    for n in ("xn", "attnb", "xa", "xb"):
        kind = {"kind": "ExternalOutput"} if debug_spills else {}
        scr[n] = nc.dram_tensor(n, [D, SEQ], f, **kind).ap()
    rotm = nc.dram_tensor("rotm", [256, SEQ], f).ap()
    dbg = {}
    if debug_spills:
        for n in ("dbg_xz", "dbg_dy", "dbg_h", "dbg_bc"):
            dbg[n] = nc.dram_tensor(n, [1024, SEQ], f, kind="ExternalOutput").ap()
        for n in ("dbg_dec", "dbg_drv", "dbg_st"):
            dbg[n] = nc.dram_tensor(n, [128, 4104], f, kind="ExternalOutput").ap()
    out_h = nc.dram_tensor("out", [D, SEQ], f, kind="ExternalOutput").ap()

    kb = KB(nc, serialize=CFG.get("serialize", False))
    hbuf = {n: [kb.buf(f"{n}{c}") for c in range(NCH)] for n in
            ("x", "xn", "attnb", "xa", "xb", "out")}
    rotmB = [kb.buf(f"rotm{i}") for i in range(4)]
    wbuf = {n: kb.buf("W" + n) for n in IN_SHAPES}

    import contextlib
    sb = {}
    bb = {}

    with contextlib.ExitStack() as stack:
        def alloc(name, cols, parts=128, dtype=F32):
            h = stack.enter_context(nc.sbuf_tensor("sb_" + name, [parts, cols], dtype))
            sb[name] = h
            bb[name] = kb.buf(name)
            return h

        for i in range(4):
            alloc(f"PW{i}", 4104)
        alloc("PX", 4160)
        alloc("PH", 4160)
        alloc("PO", 4104)
        alloc("PP1", 4096)
        alloc("PP2", 4096)
        alloc("PP3", 4160)
        alloc("WDB0", 2048)
        alloc("WDB1", 2048)
        for n in ("tmpA", "tmpB", "bc0", "bc1"):
            alloc(n, T)
        for n in ("sel0", "sel64", "sel96", "selrn"):
            alloc(n, 128)
        alloc("rows", T, parts=128)
        alloc("proj", T, parts=80)
        alloc("bct", T, parts=16)
        alloc("carry", 32)
        alloc("tint", T, parts=64, dtype=mybir.dt.int32)
        for n in ("ones1", "ones2", "perm", "mask_cur", "mask_prev", "b_dt",
                  "d_col", "a_cols", "kx", "kz", "w_xp", "hrow"):
            alloc(n, IN_SHAPES[n][1], parts=IN_SHAPES[n][0])
        alloc("w_dt", 512, parts=64)
        for l in range(2):
            for n in (f"b1a{l}", f"b1g{l}", f"b2_{l}", f"lg{l}", f"lb{l}", f"qs{l}", f"ks{l}"):
                alloc(n, IN_SHAPES[n][1], parts=IN_SHAPES[n][0])
        alloc("invf", 64, parts=1)
        alloc("logb", 64, parts=1)

        ps, psB = [], []
        for i in range(8):
            h = stack.enter_context(nc.psum_tensor(f"pb{i}", [128, 512], f))
            ps.append(h)
            psB.append(kb.buf(f"pb{i}"))

        sems = {}
        for e in ENGS:
            sems[e] = stack.enter_context(nc.semaphore("s_" + e))
        for q in QUEUES:
            sems[q] = stack.enter_context(nc.semaphore("s_" + q))

        A = lambda n: sb[n][:]
        B = lambda n: bb[n]

        def ld_const(n):
            kb.dma("qw", A(n), dr[n], reads=[wbuf[n]], writes=[B(n)])

        for n in ("ones1", "ones2", "perm", "mask_cur", "mask_prev", "b_dt",
                  "d_col", "a_cols", "kx", "kz", "w_xp", "w_dt", "hrow",
                  "sel0", "sel64", "sel96", "selrn"):
            ld_const(n)
        trow = sb["PP1"][0:1, 0:SEQ]
        prow = sb["PP2"][0:1, 0:SEQ]
        kb.dma("qw", trow, dr["trow"], reads=[wbuf["trow"]], writes=[B("PP1")])
        for l in range(2):
            for n in (f"b1a{l}", f"b1g{l}", f"b2_{l}", f"lg{l}", f"lb{l}", f"qs{l}", f"ks{l}"):
                ld_const(n)

        kb.op("vector", lambda e: e.tensor_scalar(A("a_cols"), A("a_cols"), 10.0, None, op0=ALU.min),
              reads=[B("a_cols")], writes=[B("a_cols")])
        kb.op("scalar", lambda e: e.activation(A("a_cols"), A("a_cols"), AF.Exp),
              reads=[B("a_cols")], writes=[B("a_cols")])
        kb.op("scalar", lambda e: e.activation(A("a_cols"), A("a_cols"), AF.Ln, bias=1.0),
              reads=[B("a_cols")], writes=[B("a_cols")])
        kb.op("vector", lambda e: e.tensor_scalar(A("a_cols"), A("a_cols"), 1e-4, None, op0=ALU.add),
              reads=[B("a_cols")], writes=[B("a_cols")])

        def s0_tables():
            inv_c = -math.log(10000.0) / DH
            kb.op("scalar", lambda e: e.activation(A("invf"), A("hrow"), AF.Exp, scale=inv_c),
                  reads=[B("hrow")], writes=[B("invf")])
            kb.op("vector", lambda e: e.tensor_scalar(A("logb"), A("hrow"), 1.0 / (1.4 * DH), 0.4 / 1.4, op0=ALU.mult, op1=ALU.add),
                  reads=[B("hrow")], writes=[B("logb")])
            kb.op("scalar", lambda e: e.activation(A("logb"), A("logb"), AF.Ln),
                  reads=[B("logb")], writes=[B("logb")])
            kb.op("vector", lambda e: e.tensor_scalar(prow, trow, 1.0 / 256.0, -8.0, op0=ALU.mult, op1=ALU.add),
                  reads=[B("PP1")], writes=[B("PP2")])
            u = sb["PX"][0:64, 0:512]
            fr = sb["PX"][0:64, 520:1032]
            sc3 = sb["PX"][0:64, 1040:1552]
            sn = sb["PH"][0:64, 0:512]
            cs = sb["PH"][0:64, 520:1032]
            scp = sb["PH"][0:64, 1040:1552]
            scm = sb["PH"][0:64, 1560:2072]
            ti = sb["tint"][0:64, :]

            def wrap(dst, src, rb, wb):
                kb.op("vector", lambda e: e.tensor_copy(ti, src), reads=[rb], writes=[B("tint")])
                kb.op("vector", lambda e: e.tensor_copy(sc3, ti), reads=[B("tint")], writes=[B("PX")])
                kb.op("vector", lambda e: e.tensor_tensor(dst, src, sc3, op=ALU.subtract),
                      reads=[rb, B("PX")], writes=[wb])
                kb.op("vector", lambda e: e.tensor_scalar(sc3, dst, 0.5, None, op0=ALU.is_ge),
                      reads=[wb], writes=[B("PX")])
                kb.op("vector", lambda e: e.tensor_tensor(dst, dst, sc3, op=ALU.subtract),
                      reads=[wb, B("PX")], writes=[wb])
                kb.op("vector", lambda e: e.tensor_scalar(sc3, dst, -0.5, None, op0=ALU.is_lt),
                      reads=[wb], writes=[B("PX")])
                kb.op("vector", lambda e: e.tensor_tensor(dst, dst, sc3, op=ALU.add),
                      reads=[wb, B("PX")], writes=[wb])

            for tc in range(8):
                sl = slice(tc * 512, (tc + 1) * 512)
                kb.op("tensor", lambda e, sl=sl: e.matmul(ps[0][0:64, :], A("invf"), trow[:, sl], start=True, stop=True),
                      reads=[B("invf"), B("PP1")], writes=[psB[0]])
                kb.op("vector", lambda e: e.tensor_scalar(u, ps[0][0:64, :], 1.0 / TWO_PI, None, op0=ALU.mult),
                      reads=[psB[0]], writes=[B("PX")])
                kb.op("tensor", lambda e, sl=sl: e.matmul(ps[1][0:64, :], A("logb"), prow[:, sl], start=True, stop=True),
                      reads=[B("logb"), B("PP2")], writes=[psB[1]])
                kb.op("scalar", lambda e: e.activation(scp, ps[1][0:64, :], AF.Exp),
                      reads=[psB[1]], writes=[B("PH")])
                kb.op("scalar", lambda e: e.activation(scm, ps[1][0:64, :], AF.Exp, scale=-1.0),
                      reads=[psB[1]], writes=[B("PH")])
                wrap(fr, u, B("PX"), B("PX"))
                kb.op("scalar", lambda e: e.activation(sn, fr, AF.Sin, scale=TWO_PI),
                      reads=[B("PX")], writes=[B("PH")])
                kb.op("vector", lambda e: e.tensor_scalar(u, fr, 0.25, None, op0=ALU.add),
                      reads=[B("PX")], writes=[B("PX")])
                wrap(fr, u, B("PX"), B("PX"))
                kb.op("scalar", lambda e: e.activation(cs, fr, AF.Sin, scale=TWO_PI),
                      reads=[B("PX")], writes=[B("PH")])
                for (ri, trg, sgn, scl) in ((0, cs, 1.0, scp), (64, sn, 1.0, scp),
                                            (128, cs, 1.0, scm), (192, sn, 1.0, scm)):
                    dst = sb["PX"][0:64, 1560:2072]
                    kb.op("vector", lambda e, trg=trg, scl=scl: e.tensor_tensor(dst, trg, scl, op=ALU.mult),
                          reads=[B("PH")], writes=[B("PX")])
                    if ri in (64, 192):
                        kb.op("vector", lambda e: e.tensor_scalar(sb["PX"][0:32, 1560:2072], sb["PX"][0:32, 1560:2072], -1.0, None, op0=ALU.mult),
                              reads=[B("PX")], writes=[B("PX")])
                    kb.dma("qg", rotm[ri:ri + 64, sl], dst,
                           reads=[B("PX")], writes=[rotmB[ri // 64]])

        s0_tables()

        def s1_chunk(c):
            t0 = c * T
            kb.dma("qw", sb["PX"][:, 0:NDC * T],
                   dr["x"][:, t0:t0 + T].rearrange("(dc p) t -> p dc t", p=128),
                   reads=[hbuf["x"][c]], writes=[B("PX")])
            for dc in range(NDC):
                xt = sb["PX"][:, dc * T:(dc + 1) * T]
                kb.op("tensor", lambda e, xt=xt, dc=dc: e.matmul(
                    ps[0][0:1, :], A("ones1"), xt, start=(dc == 0), stop=(dc == NDC - 1)),
                    reads=[B("PX"), B("ones1")], writes=[psB[0]])
            for dc in range(NDC):
                xt = sb["PX"][:, dc * T:(dc + 1) * T]
                kb.op("scalar", lambda e, xt=xt: e.activation(A("tmpA"), xt, AF.Square),
                      reads=[B("PX")], writes=[B("tmpA")])
                kb.op("tensor", lambda e, dc=dc: e.matmul(
                    ps[1][0:1, :], A("ones1"), A("tmpA"), start=(dc == 0), stop=(dc == NDC - 1)),
                    reads=[B("tmpA"), B("ones1")], writes=[psB[1]])
            r = sb["rows"]
            rm, rm2, rv, ri = r[0:1, :], r[32:33, :], r[64:65, :], r[96:97, :]
            kb.op("vector", lambda e: e.tensor_scalar(rm, ps[0][0:1, :], 1.0 / D, None, op0=ALU.mult),
                  reads=[psB[0]], writes=[B("rows")])
            kb.op("vector", lambda e: e.tensor_tensor(rm2, rm, rm, op=ALU.mult),
                  reads=[B("rows")], writes=[B("rows")])
            kb.op("vector", lambda e: e.scalar_tensor_tensor(rv, ps[1][0:1, :], 1.0 / D, rm2,
                                                             op0=ALU.mult, op1=ALU.subtract),
                  reads=[psB[1], B("rows")], writes=[B("rows")])
            kb.op("vector", lambda e: e.tensor_scalar(rv, rv, 1e-5, None, op0=ALU.add),
                  reads=[B("rows")], writes=[B("rows")])
            kb.op("scalar", lambda e: e.activation(ri, rv, AF.Ln),
                  reads=[B("rows")], writes=[B("rows")])
            kb.op("scalar", lambda e: e.activation(ri, ri, AF.Exp, scale=-0.5),
                  reads=[B("rows")], writes=[B("rows")])
            kb.op("tensor", lambda e: e.matmul(ps[2][:, :], A("sel0"), r[:], start=True, stop=True),
                  reads=[B("sel0"), B("rows")], writes=[psB[2]])
            kb.op("tensor", lambda e: e.matmul(ps[3][:, :], A("sel96"), r[:], start=True, stop=True),
                  reads=[B("sel96"), B("rows")], writes=[psB[3]])
            for dc in range(NDC):
                xt = sb["PX"][:, dc * T:(dc + 1) * T]
                ot = sb["PO"][:, dc * T:(dc + 1) * T]
                kb.op("vector", lambda e, xt=xt, ot=ot: e.tensor_tensor(ot, xt, ps[2][:, :], op=ALU.subtract),
                      reads=[B("PX"), psB[2]], writes=[B("PO")])
                kb.op("vector", lambda e, ot=ot: e.tensor_tensor(ot, ot, ps[3][:, :], op=ALU.mult),
                      reads=[B("PO"), psB[3]], writes=[B("PO")])
            kb.dma("qg", scr["xn"][:, t0:t0 + T].rearrange("(dc p) t -> p dc t", p=128),
                   sb["PO"][:, 0:NDC * T], reads=[B("PO")], writes=[hbuf["xn"][c]])

        def s2_prep():
            kb.op("vector", lambda e: e.memset(A("carry"), 0.0), reads=[], writes=[B("carry")])
            kb.op("vector", lambda e: e.memset(sb["PW1"][:, 0:8], 0.0), reads=[], writes=[B("PW1")])
            kb.dma("qw", sb["PW3"][0:16, 0:2048], dr["sel16"],
                   reads=[wbuf["sel16"]], writes=[B("PW3")])

        def s2_chunk(c):
            t0 = c * T
            PX, PH = sb["PX"], sb["PH"]
            lo = max(t0 - 1, 0)
            hi = min(t0 + T + 1, SEQ)
            ncol = hi - lo
            coff = 1 if c == 0 else 0
            rd = [hbuf["xn"][c]]
            if c > 0:
                rd.append(hbuf["xn"][c - 1])
            if c < NCH - 1:
                rd.append(hbuf["xn"][c + 1])
            kb.dma("qw", PX[:].rearrange("p (dc j) -> p dc j", dc=NDC)[:, :, coff:coff + ncol],
                   scr["xn"][:, lo:hi].rearrange("(dc p) t -> p dc t", p=128),
                   reads=rd, writes=[B("PX")])
            for ec in range(NDC):
                wb = "WDB0" if ec % 2 == 0 else "WDB1"
                half = (ec // 2) % 2
                wsl = sb[wb][:, half * 1024:(half + 1) * 1024]
                kb.dma("qw", wsl, dr["w_in"][:, ec * 1024:(ec + 1) * 1024],
                       reads=[wbuf["w_in"]], writes=[B(wb)])
                pk = ec % 2
                j0 = coff
                j1 = 514 if c < NCH - 1 else 513
                m0 = j0
                m1 = min(j0 + 512, j1)
                for dc in range(NDC):
                    kb.op("tensor", lambda e, ec=ec, dc=dc, wsl=wsl, m0=m0, m1=m1, pk=pk: e.matmul(
                        ps[pk][:, 0:m1 - m0], wsl[:, dc * 128:(dc + 1) * 128],
                        PX[:, dc * 520 + m0:dc * 520 + m1], start=(dc == 0), stop=(dc == NDC - 1)),
                        reads=[B(wb), B("PX")], writes=[psB[pk]])
                kb.op("scalar", lambda e, ec=ec, pk=pk, m0=m0, m1=m1: e.copy(
                    PH[:, ec * 520 + m0:ec * 520 + m1], ps[pk][:, 0:m1 - m0]),
                    reads=[psB[pk]], writes=[B("PH")])
                if m1 < j1:
                    for dc in range(NDC):
                        kb.op("tensor", lambda e, ec=ec, dc=dc, wsl=wsl, m1=m1, j1=j1, pk=pk: e.matmul(
                            ps[2 + pk][:, 0:j1 - m1], wsl[:, dc * 128:(dc + 1) * 128],
                            PX[:, dc * 520 + m1:dc * 520 + j1], start=(dc == 0), stop=(dc == NDC - 1)),
                            reads=[B(wb), B("PX")], writes=[psB[2 + pk]])
                    kb.op("scalar", lambda e, ec=ec, pk=pk, m1=m1, j1=j1: e.copy(
                        PH[:, ec * 520 + m1:ec * 520 + j1], ps[2 + pk][:, 0:j1 - m1]),
                        reads=[psB[2 + pk]], writes=[B("PH")])
                if c == 0:
                    kb.op("vector", lambda e, ec=ec: e.memset(PH[:, ec * 520:ec * 520 + 1], 0.0),
                          reads=[], writes=[B("PH")])
                if c == NCH - 1:
                    kb.op("vector", lambda e, ec=ec: e.memset(PH[:, ec * 520 + 513:ec * 520 + 514], 0.0),
                          reads=[], writes=[B("PH")])
            for ec in range(NDC):
                kern = "kx" if ec < 4 else "kz"
                cb = ec % 4
                h0 = PH[:, ec * 520 + 0:ec * 520 + 512]
                h1 = PH[:, ec * 520 + 1:ec * 520 + 513]
                h2 = PH[:, ec * 520 + 2:ec * 520 + 514]
                kb.op("vector", lambda e, h1=h1, kern=kern, cb=cb: e.tensor_scalar(
                    A("tmpA"), h1, sb[kern][:, cb * 3 + 1:cb * 3 + 2], None, op0=ALU.mult),
                    reads=[B("PH"), B(kern)], writes=[B("tmpA")])
                kb.op("vector", lambda e, h0=h0, kern=kern, cb=cb: e.scalar_tensor_tensor(
                    A("tmpB"), h0, sb[kern][:, cb * 3 + 0:cb * 3 + 1], A("tmpA"), op0=ALU.mult, op1=ALU.add),
                    reads=[B("PH"), B(kern), B("tmpA")], writes=[B("tmpB")])
                kb.op("vector", lambda e, h2=h2, kern=kern, cb=cb: e.scalar_tensor_tensor(
                    A("tmpA"), h2, sb[kern][:, cb * 3 + 2:cb * 3 + 3], A("tmpB"), op0=ALU.mult, op1=ALU.add),
                    reads=[B("PH"), B(kern), B("tmpB")], writes=[B("tmpA")])
                kb.op("scalar", lambda e, ec=ec: e.activation(
                    sb["PP1"][:, ec * 512:(ec + 1) * 512], A("tmpA"), AF.Silu),
                    reads=[B("tmpA")], writes=[B("PP1")])
            for cb in range(4):
                kb.op("tensor", lambda e, cb=cb: e.matmul(
                    ps[4][0:80, :], sb["w_xp"][:, cb * 80:(cb + 1) * 80],
                    sb["PP1"][:, cb * 512:(cb + 1) * 512], start=(cb == 0), stop=(cb == 3)),
                    reads=[B("w_xp"), B("PP1")], writes=[psB[4]])
            kb.op("scalar", lambda e: e.copy(sb["proj"][0:80, :], ps[4][0:80, :]),
                  reads=[psB[4]], writes=[B("proj")])
            kb.op("scalar", lambda e: e.activation(A("bct"), sb["proj"][64:80, :], AF.Tanh),
                  reads=[B("proj")], writes=[B("bct")])
            for cb in range(4):
                kb.op("tensor", lambda e, cb=cb: e.matmul(
                    ps[5][:, :], sb["w_dt"][:, cb * 128:(cb + 1) * 128], sb["proj"][0:64, :],
                    start=True, stop=True),
                    reads=[B("w_dt"), B("proj")], writes=[psB[5]])
                kb.op("vector", lambda e, cb=cb: e.tensor_scalar(
                    A("tmpA"), ps[5][:, :], sb["b_dt"][:, cb:cb + 1], 10.0, op0=ALU.add, op1=ALU.min),
                    reads=[psB[5], B("b_dt")], writes=[B("tmpA")])
                kb.op("scalar", lambda e: e.activation(A("tmpB"), A("tmpA"), AF.Exp),
                      reads=[B("tmpA")], writes=[B("tmpB")])
                kb.op("scalar", lambda e: e.activation(A("tmpA"), A("tmpB"), AF.Ln, bias=1.0),
                      reads=[B("tmpB")], writes=[B("tmpA")])
                kb.op("vector", lambda e, cb=cb: e.tensor_scalar(
                    sb["PP2"][:, cb * 512:(cb + 1) * 512], A("tmpA"), 1e-4, 1.0, op0=ALU.max, op1=ALU.min),
                    reads=[B("tmpA")], writes=[B("PP2")])
            for s in range(DS):
                pk = 6 + s % 2
                kb.op("tensor", lambda e, s=s, pk=pk: e.matmul(
                    ps[pk][:, :], sb["PW3"][0:16, s * 128:(s + 1) * 128], A("bct"),
                    start=True, stop=True),
                    reads=[B("PW3"), B("bct")], writes=[psB[pk]])
                kb.op("scalar", lambda e, s=s, pk=pk: e.copy(
                    sb["PP3"][:, s * 512:(s + 1) * 512], ps[pk][:, :]),
                    reads=[psB[pk]], writes=[B("PP3")])
                kb.op("tensor", lambda e, s=s, pk=pk: e.matmul(
                    ps[pk][:, :], sb["PW3"][0:16, (8 + s) * 128:(9 + s) * 128], A("bct"),
                    start=True, stop=True),
                    reads=[B("PW3"), B("bct")], writes=[psB[pk]])
                kb.op("scalar", lambda e, s=s, pk=pk: e.copy(
                    sb["PH"][:, s * 512:(s + 1) * 512], ps[pk][:, :]),
                    reads=[psB[pk]], writes=[B("PH")])
            for cb in range(4):
                SA, SB_, SC = sb["PW0"], sb["PW1"], sb["PW2"]
                for s in range(DS):
                    dst = SA[:, 8:4104].rearrange("p (t s) -> p t s", s=8)[:, :, s]
                    kb.op("vector", lambda e, dst=dst, cb=cb, s=s: e.tensor_scalar(
                        dst, sb["PP2"][:, cb * 512:(cb + 1) * 512],
                        sb["a_cols"][:, cb * 8 + s:cb * 8 + s + 1], None, op0=ALU.mult),
                        reads=[B("PP2"), B("a_cols")], writes=[B("PW0")])
                kb.op("scalar", lambda e: e.activation(SB_[:, 8:4104], SA[:, 8:4104], AF.Exp, scale=-1.0),
                      reads=[B("PW0")], writes=[B("PW1")])
                kb.op("vector", lambda e: e.tensor_scalar(SB_[:, 8:4104], SB_[:, 8:4104], 1e-4, 1.0, op0=ALU.max, op1=ALU.min),
                      reads=[B("PW1")], writes=[B("PW1")])
                xbrep = sb["PP1"][:, cb * 512:(cb + 1) * 512].unsqueeze(2).broadcast_to([128, 512, 8])
                bview = sb["PP3"][:, 0:4096].rearrange("p (s t) -> p t s", s=8)
                kb.op("vector", lambda e, xbrep=xbrep, bview=bview: e.tensor_tensor(
                    SA[:, 8:4104].rearrange("p (t s) -> p t s", s=8),
                    bview, xbrep, op=ALU.mult),
                    reads=[B("PP3"), B("PP1")], writes=[B("PW0")])
                kb.op("scalar", lambda e, cb=cb: e.copy(SC[:, 0:8], sb["carry"][:, cb * 8:cb * 8 + 8]),
                      reads=[B("carry")], writes=[B("PW2")])
                kb.op("vector", lambda e: e.scalar_tensor_tensor(
                    SC[:, 8:4104], SB_[:, 8:4104], 1.0, SA[:, 8:4104], op0=ALU.subtract, op1=ALU.mult),
                    reads=[B("PW1"), B("PW0")], writes=[B("PW2")])
                for s in range(DS):
                    d0 = SB_[:, 0:4104].rearrange("p (j s) -> p j s", s=8)[:, :, s]
                    d1 = SC[:, 0:4104].rearrange("p (j s) -> p j s", s=8)[:, :, s]
                    oo = SA[:, 0:4104].rearrange("p (j s) -> p j s", s=8)[:, :, s]
                    kb.op("vector", lambda e, d0=d0, d1=d1, oo=oo: e.tensor_tensor_scan(
                        oo, d0, d1, 0.0, op0=ALU.mult, op1=ALU.add),
                        reads=[B("PW1"), B("PW2")], writes=[B("PW0")])
                if debug_spills and c == 0 and cb == 0:
                    kb.dma("qg", dbg["dbg_dec"], SB_[:, 0:4104], reads=[B("PW1")], writes=[hbuf["out"][0]])
                    kb.dma("qg", dbg["dbg_drv"], SC[:, 0:4104], reads=[B("PW2")], writes=[hbuf["out"][0]])
                    kb.dma("qg", dbg["dbg_st"], SA[:, 0:4104], reads=[B("PW0")], writes=[hbuf["out"][0]])
                cview = sb["PH"][:, 0:4096].rearrange("p (s t) -> p t s", s=8)
                kb.op("vector", lambda e, cview=cview: e.tensor_tensor(
                    SC[:, 8:4104].rearrange("p (t s) -> p t s", s=8),
                    SA[:, 8:4104].rearrange("p (t s) -> p t s", s=8),
                    cview, op=ALU.mult),
                    reads=[B("PW0"), B("PH")], writes=[B("PW2")])
                kb.op("vector", lambda e, cb=cb: e.tensor_reduce(
                    A("tmpA"),
                    SC[:, 8:4104].rearrange("p (t s) -> p t s", s=8),
                    mybir.AxisListType.X, ALU.add, negate=True),
                    reads=[B("PW2")], writes=[B("tmpA")])
                kb.op("scalar", lambda e, cb=cb: e.copy(
                    sb["carry"][:, cb * 8:cb * 8 + 8], SA[:, 8 + 511 * 8: 8 + 512 * 8]),
                    reads=[B("PW0")], writes=[B("carry")])
                kb.op("vector", lambda e, cb=cb: e.scalar_tensor_tensor(
                    sb["PP2"][:, 2048 + cb * 512:2048 + (cb + 1) * 512],
                    sb["PP1"][:, cb * 512:(cb + 1) * 512], sb["d_col"][:, cb:cb + 1],
                    A("tmpA"), op0=ALU.mult, op1=ALU.add),
                    reads=[B("PP1"), B("d_col"), B("tmpA")], writes=[B("PP2")])
            for ec in range(NDC):
                wb = "WDB0" if ec % 2 == 0 else "WDB1"
                half = (ec // 2) % 2
                wsl = sb[wb][:, half * 1024:(half + 1) * 1024]
                kb.dma("qw", wsl, dr["w_out"][:, ec * 1024:(ec + 1) * 1024],
                       reads=[wbuf["w_out"]], writes=[B(wb)])
                pk = ec % 2
                for dc in range(NDC):
                    if dc < 4:
                        rhs = sb["PP2"][:, 2048 + dc * 512:2048 + (dc + 1) * 512]
                        rb = B("PP2")
                    else:
                        rhs = sb["PP1"][:, 2048 + (dc - 4) * 512:2048 + (dc - 3) * 512]
                        rb = B("PP1")
                    kb.op("tensor", lambda e, wsl=wsl, dc=dc, rhs=rhs, pk=pk: e.matmul(
                        ps[pk][:, :], wsl[:, dc * 128:(dc + 1) * 128], rhs,
                        start=(dc == 0), stop=(dc == NDC - 1)),
                        reads=[B(wb), rb], writes=[psB[pk]])
                kb.op("scalar", lambda e, ec=ec, pk=pk: e.copy(
                    sb["PO"][:, ec * 512:(ec + 1) * 512], ps[pk][:, :]),
                    reads=[psB[pk]], writes=[B("PO")])
            kb.dma("qg", scr["attnb"][:, t0:t0 + T].rearrange("(dc p) t -> p dc t", p=128),
                   sb["PO"][:, 0:NDC * T], reads=[B("PO")], writes=[hbuf["attnb"][c]])
            if debug_spills:
                kb.dma("qg", dbg["dbg_xz"][:, t0:t0 + T].rearrange("(dc p) t -> p dc t", p=128),
                       sb["PP1"][:, 0:4096], reads=[B("PP1")], writes=[hbuf["out"][c]])
                kb.dma("qg", dbg["dbg_dy"][:, t0:t0 + T].rearrange("(dc p) t -> p dc t", p=128),
                       sb["PP2"][:, 0:4096], reads=[B("PP2")], writes=[hbuf["out"][c]])
                kb.dma("qg", dbg["dbg_h"][:, t0:t0 + T].rearrange("(dc p) t -> p dc t", p=128),
                       sb["PH"][:].rearrange("p (dc j) -> p dc j", dc=NDC)[:, :, 1:513],
                       reads=[B("PH")], writes=[hbuf["out"][c]])
                kb.dma("qg", dbg["dbg_bc"][:, t0:t0 + T].rearrange("(dc p) t -> p dc t", p=128),
                       sb["PP3"][:, 0:4096], reads=[B("PP3")], writes=[hbuf["out"][c]])

        def s3_prep(l):
            for q in range(3):
                kb.dma("qw", sb[f"PW{q}"][:, 0:4096], dr[f"wqkv{l}"][:, q * 4096:(q + 1) * 4096],
                       reads=[wbuf[f"wqkv{l}"]], writes=[B(f"PW{q}")])
            kb.dma("qw", sb["PW3"][:, 0:4096], dr[f"wo{l}"],
                   reads=[wbuf[f"wo{l}"]], writes=[B("PW3")])

        def wqkv_tile(es, dc):
            q = es // 4
            off = (es % 4) * 1024 + dc * 128
            return sb[f"PW{q}"][:, off:off + 128], B(f"PW{q}")

        def s3_chunk(l, c, src, dst):
            t0 = c * T
            PX, PH, PO = sb["PX"], sb["PH"], sb["PO"]
            qsn, ksn = f"qs{l}", f"ks{l}"
            vb_cur = (c % 2) * 2080
            vb_prv = ((c + 1) % 2) * 2080
            kp_cur = "PP1" if c % 2 == 0 else "PP2"
            kp_prv = "PP2" if c % 2 == 0 else "PP1"
            lo = max(t0 - 1, 0)
            ncol = t0 + T - lo
            coff = 1 if c == 0 else 0
            rd = [hbuf[src][c]] + ([hbuf[src][c - 1]] if c > 0 else [])
            kb.dma("qw", PX[:].rearrange("p (dc j) -> p dc j", dc=NDC)[:, :, coff:coff + ncol],
                   scr[src][:, lo:t0 + T].rearrange("(dc p) t -> p dc t", p=128),
                   reads=rd, writes=[B("PX")])
            if c == 0:
                kb.op("vector", lambda e: e.memset(
                    PX[:].rearrange("p (dc j) -> p dc j", dc=NDC)[:, :, 0:1], 0.0),
                    reads=[], writes=[B("PX")])
            wk = {"wkcsq": sb["PP1"][:, 2048:2560], "wksnq": sb["PP1"][:, 2560:3072],
                  "wkcsk": sb["PP1"][:, 3072:3584], "wksnk": sb["PP1"][:, 3584:4096]}
            for (nm, ri) in (("wkcsq", 0), ("wksnq", 64), ("wkcsk", 128), ("wksnk", 192)):
                kb.dma("qw", wk[nm][0:64, :], rotm[ri:ri + 64, t0:t0 + T],
                       reads=[rotmB[ri // 64]], writes=[B("PP1")])
                kb.dma("qw", wk[nm][64:128, :], rotm[ri:ri + 64, t0:t0 + T],
                       reads=[rotmB[ri // 64]], writes=[B("PP1")])

            def xsl(dc, j0, width):
                base = dc * 520 + (j0 + (1 if dc < 4 else 0))
                return PX[:, base:base + width]

            for es in range(8):
                pk = es % 2
                for dc in range(NDC):
                    w, wB = wqkv_tile(es, dc)
                    kb.op("tensor", lambda e, w=w, dc=dc, pk=pk: e.matmul(
                        ps[pk][:, :], w, xsl(dc, 0, 512), start=(dc == 0), stop=(dc == NDC - 1)),
                        reads=[wB, B("PX")], writes=[psB[pk]])
                kb.op("scalar", lambda e, es=es, pk=pk: e.copy(PH[:, es * 512:(es + 1) * 512], ps[pk][:, :]),
                      reads=[psB[pk]], writes=[B("PH")])
            for tt in range(4):
                pk = 4 + tt % 2
                first = True
                for dc in range(NDC):
                    for esv in range(4):
                        w, wB = wqkv_tile(8 + esv, dc)
                        kb.op("tensor", lambda e, w=w, dc=dc, tt=tt, esv=esv, pk=pk, first=first: e.matmul(
                            ps[pk][:, esv * 128:(esv + 1) * 128],
                            xsl(dc, tt * 128, 128), w,
                            start=first, stop=(dc == NDC - 1 and esv == 3)),
                            reads=[wB, B("PX")], writes=[psB[pk]])
                        first = False
                dstv = sb["PP3"][:, vb_cur + tt * 520: vb_cur + tt * 520 + 520]
                kb.op("scalar", lambda e, dstv=dstv, pk=pk: e.copy(
                    dstv.rearrange("p (h d) -> p h d", d=65)[:, :, 0:64], ps[pk][:, :]),
                    reads=[psB[pk]], writes=[B("PP3")])
            kb.op("vector", lambda e: e.memset(
                sb["PP3"][:, vb_cur:vb_cur + 2080].rearrange("p (tt h d) -> p tt h d", h=8, d=65)[:, :, :, 64:65], 1.0),
                reads=[], writes=[B("PP3")])
            for es in range(8):
                phs = PH[:, es * 512:(es + 1) * 512]
                kb.op("scalar", lambda e, phs=phs: e.activation(A("tmpA"), phs, AF.Square),
                      reads=[B("PH")], writes=[B("tmpA")])
                kb.op("tensor", lambda e: e.matmul(ps[6][0:2, :], A("ones2"), A("tmpA"),
                                                   start=True, stop=True),
                      reads=[B("ones2"), B("tmpA")], writes=[psB[6]])
                r = sb["rows"]
                kb.op("vector", lambda e: e.tensor_scalar(r[0:2, :], ps[6][0:2, :], 1e-24, None, op0=ALU.max),
                      reads=[psB[6]], writes=[B("rows")])
                kb.op("scalar", lambda e: e.activation(r[0:2, :], r[0:2, :], AF.Ln),
                      reads=[B("rows")], writes=[B("rows")])
                kb.op("scalar", lambda e: e.activation(r[0:2, :], r[0:2, :], AF.Exp, scale=-0.5),
                      reads=[B("rows")], writes=[B("rows")])
                kb.op("tensor", lambda e: e.matmul(ps[6][:, :], A("selrn"), r[:],
                                                   start=True, stop=True),
                      reads=[B("selrn"), B("rows")], writes=[psB[6]])
                scn = qsn if es < 4 else ksn
                kb.op("vector", lambda e, phs=phs, scn=scn: e.scalar_tensor_tensor(
                    phs, phs, sb[scn][:, 0:1], ps[6][:, :], op0=ALU.mult, op1=ALU.mult),
                    reads=[B("PH"), B(scn), psB[6]], writes=[B("PH")])
                cs = wk["wkcsq"] if es < 4 else wk["wkcsk"]
                sn = wk["wksnq"] if es < 4 else wk["wksnk"]
                kb.op("tensor", lambda e, phs=phs: e.matmul(ps[7][:, :], A("perm"), phs,
                                                            start=True, stop=True),
                      reads=[B("perm"), B("PH")], writes=[psB[7]])
                kb.op("vector", lambda e, sn=sn: e.tensor_tensor(A("tmpA"), ps[7][:, :], sn, op=ALU.mult),
                      reads=[psB[7], B("PP1")], writes=[B("tmpA")])
                kb.op("vector", lambda e, phs=phs, cs=cs: e.tensor_tensor(A("tmpB"), phs, cs, op=ALU.mult),
                      reads=[B("PH"), B("PP1")], writes=[B("tmpB")])
                kb.op("vector", lambda e, phs=phs: e.tensor_tensor(phs, A("tmpA"), A("tmpB"), op=ALU.add),
                      reads=[B("tmpA"), B("tmpB")], writes=[B("PH")])
            kt0 = 4 if c == 0 else 0
            for hp in range(4):
                qes = PH[:, hp * 512:(hp + 1) * 512]
                for kt in range(kt0, 8):
                    if kt < 4:
                        kbase, kB = sb[kp_prv][:, kt * 512:], B(kp_prv)
                        ksA = kbase[0:64, hp * 0:0 + 128] if False else sb[kp_prv][0:64, hp * 512 + kt * 128: hp * 512 + kt * 128 + 128]
                        ksB = sb[kp_prv][64:128, hp * 512 + kt * 128: hp * 512 + kt * 128 + 128]
                        vsrc = vb_prv + (kt % 4) * 520
                    else:
                        ksA = PH[0:64, (4 + hp) * 512 + (kt - 4) * 128: (4 + hp) * 512 + (kt - 4) * 128 + 128]
                        ksB = PH[64:128, (4 + hp) * 512 + (kt - 4) * 128: (4 + hp) * 512 + (kt - 4) * 128 + 128]
                        kB = B("PH")
                        vsrc = vb_cur + (kt - 4) * 520
                    bkA, bkB = ps[kt % 2], ps[2 + kt % 2]
                    bkAB_, bkBB_ = psB[kt % 2], psB[2 + kt % 2]
                    kb.op("tensor", lambda e, ksA=ksA, qes=qes, bkA=bkA: e.matmul(
                        bkA[:, :], ksA, qes[0:64, :], start=True, stop=True, tile_position=(0, 0)),
                        reads=[kB, B("PH")], writes=[bkAB_])
                    kb.op("tensor", lambda e, ksB=ksB, qes=qes, bkB=bkB: e.matmul(
                        bkB[:, :], ksB, qes[64:128, :], start=True, stop=True, tile_position=(64, 0)),
                        reads=[kB, B("PH")], writes=[bkBB_])
                    if kt >= 4:
                        qt = kt - 4
                        wdt = (qt + 1) * 128
                        msl = sb["mask_cur"][:, 512 - wdt:] if False else None
                        for bk_, bkb_ in ((bkA, bkAB_), (bkB, bkBB_)):
                            kb.op("vector", lambda e, bk_=bk_, wdt=wdt: e.tensor_tensor(
                                bk_[:, 0:wdt], bk_[:, 0:wdt], sb["mask_cur"][:, 512 - wdt:], op=ALU.add),
                                reads=[bkb_, B("mask_cur")], writes=[bkb_])
                    else:
                        wdt = 512 - kt * 128
                        for bk_, bkb_ in ((bkA, bkAB_), (bkB, bkBB_)):
                            kb.op("vector", lambda e, bk_=bk_, kt=kt, wdt=wdt: e.tensor_tensor(
                                bk_[:, kt * 128:512], bk_[:, kt * 128:512], sb["mask_prev"][:, 0:wdt], op=ALU.add),
                                reads=[bkb_, B("mask_prev")], writes=[bkb_])
                    eA = PO[:, (kt % 2) * 512:(kt % 2) * 512 + 512]
                    eB = PO[:, 1024 + (kt % 2) * 512:1024 + (kt % 2) * 512 + 512]
                    kb.op("scalar", lambda e, bkA=bkA, eA=eA: e.activation(eA, bkA[:, :], AF.Exp, scale=8.0),
                          reads=[bkAB_], writes=[B("PO")])
                    kb.op("scalar", lambda e, bkB=bkB, eB=eB: e.activation(eB, bkB[:, :], AF.Exp, scale=8.0),
                          reads=[bkBB_], writes=[B("PO")])
                    vA = sb["PP3"][:, vsrc:vsrc + 520].rearrange("p (h d) -> p h d", d=65)[:, 2 * hp, :]
                    vB = sb["PP3"][:, vsrc:vsrc + 520].rearrange("p (h d) -> p h d", d=65)[:, 2 * hp + 1, :]
                    kb.op("tensor", lambda e, vA=vA, eA=eA, kt=kt: e.matmul(
                        ps[4][0:65, :], vA, eA, start=(kt == kt0), stop=(kt == 7)),
                        reads=[B("PP3"), B("PO")], writes=[psB[4]])
                    kb.op("tensor", lambda e, vB=vB, eB=eB, kt=kt: e.matmul(
                        ps[5][0:65, :], vB, eB, start=(kt == kt0), stop=(kt == 7)),
                        reads=[B("PP3"), B("PO")], writes=[psB[5]])
                r = sb["rows"]
                for (bank, bb_, rrow, seln, bcn, orow) in (
                        (ps[4], psB[4], r[64:65, :], "sel64", "bc0", 0),
                        (ps[5], psB[5], r[96:97, :], "sel96", "bc1", 64)):
                    kb.op("scalar", lambda e, bank=bank, rrow=rrow: e.activation(rrow, bank[64:65, :], AF.Ln),
                          reads=[bb_], writes=[B("rows")])
                    kb.op("scalar", lambda e, rrow=rrow: e.activation(rrow, rrow, AF.Exp, scale=-1.0),
                          reads=[B("rows")], writes=[B("rows")])
                    kb.op("tensor", lambda e, seln=seln: e.matmul(
                        ps[6][0:64, :], sb[seln][:, 0:64], r[:], start=True, stop=True),
                        reads=[B(seln), B("rows")], writes=[psB[6]])
                    kb.op("scalar", lambda e, bcn=bcn: e.copy(sb[bcn][0:64, :], ps[6][0:64, :]),
                          reads=[psB[6]], writes=[B(bcn)])
                    kb.op("vector", lambda e, bank=bank, bcn=bcn, hp=hp, orow=orow: e.tensor_tensor(
                        sb["PP2"][orow:orow + 64, 2048 + hp * 512:2048 + (hp + 1) * 512],
                        bank[0:64, :], sb[bcn][0:64, :], op=ALU.mult),
                        reads=[bb_, B(bcn)], writes=[B("PP2")])
            kb.op("vector", lambda e: e.tensor_copy(sb[kp_cur][:, 0:2048], PH[:, 2048:4096]),
                  reads=[B("PH")], writes=[B(kp_cur)])
            for ec in range(NDC):
                pk = ec % 2
                for cc in range(4):
                    kb.op("tensor", lambda e, ec=ec, cc=cc, pk=pk: e.matmul(
                        ps[pk][:, :], sb["PW3"][:, (ec * 4 + cc) * 128:(ec * 4 + cc) * 128 + 128],
                        sb["PP2"][:, 2048 + cc * 512:2048 + (cc + 1) * 512],
                        start=(cc == 0), stop=(cc == 3)),
                        reads=[B("PW3"), B("PP2")], writes=[psB[pk]])
                kb.op("vector", lambda e, ec=ec, pk=pk: e.tensor_tensor(
                    PO[:, ec * 512:(ec + 1) * 512], ps[pk][:, :], xsl(ec if False else ec, 0, 512) if False else PX[:, ec * 520 + 1:ec * 520 + 513],
                    op=ALU.add),
                    reads=[psB[pk], B("PX")], writes=[B("PO")])
            kb.dma("qg", scr[dst][:, t0:t0 + T].rearrange("(dc p) t -> p dc t", p=128),
                   PO[:, 0:NDC * T], reads=[B("PO")], writes=[hbuf[dst][c]])

        def s4_chunk(l, c, src, dst, final=False):
            t0 = c * T
            PX, PH, PO = sb["PX"], sb["PH"], sb["PO"]
            lo = max(t0 - 1, 0)
            ncol = t0 + T - lo
            coff = 1 if c == 0 else 0
            rd = [hbuf[src][c]] + ([hbuf[src][c - 1]] if c > 0 else [])
            kb.dma("qw", PX[:].rearrange("p (dc j) -> p dc j", dc=NDC)[:, :, coff:coff + ncol],
                   scr[src][:, lo:t0 + T].rearrange("(dc p) t -> p dc t", p=128),
                   reads=rd, writes=[B("PX")])
            if c == 0:
                kb.op("vector", lambda e: e.memset(
                    PX[:].rearrange("p (dc j) -> p dc j", dc=NDC)[:, :, 0:1], 0.0),
                    reads=[], writes=[B("PX")])

            def xsh(dc, j0=0, width=512):
                base = dc * 520 + (j0 + (1 if dc < 4 else 0))
                return PX[:, base:base + width]

            for dc in range(NDC):
                kb.op("tensor", lambda e, dc=dc: e.matmul(
                    ps[6][0:1, :], A("ones1"), xsh(dc), start=(dc == 0), stop=(dc == NDC - 1)),
                    reads=[B("ones1"), B("PX")], writes=[psB[6]])
            for dc in range(NDC):
                kb.op("scalar", lambda e, dc=dc: e.activation(A("tmpA"), xsh(dc), AF.Square),
                      reads=[B("PX")], writes=[B("tmpA")])
                kb.op("tensor", lambda e, dc=dc: e.matmul(
                    ps[7][0:1, :], A("ones1"), A("tmpA"), start=(dc == 0), stop=(dc == NDC - 1)),
                    reads=[B("ones1"), B("tmpA")], writes=[psB[7]])
            r = sb["rows"]
            rm, rm2, rv, ri = r[0:1, :], r[32:33, :], r[64:65, :], r[96:97, :]
            kb.op("vector", lambda e: e.tensor_scalar(rm, ps[6][0:1, :], 1.0 / D, None, op0=ALU.mult),
                  reads=[psB[6]], writes=[B("rows")])
            kb.op("vector", lambda e: e.tensor_tensor(rm2, rm, rm, op=ALU.mult),
                  reads=[B("rows")], writes=[B("rows")])
            kb.op("vector", lambda e: e.scalar_tensor_tensor(rv, ps[7][0:1, :], 1.0 / D, rm2,
                                                             op0=ALU.mult, op1=ALU.subtract),
                  reads=[psB[7], B("rows")], writes=[B("rows")])
            kb.op("vector", lambda e: e.tensor_scalar(rv, rv, 1e-5, None, op0=ALU.add),
                  reads=[B("rows")], writes=[B("rows")])
            kb.op("scalar", lambda e: e.activation(ri, rv, AF.Ln),
                  reads=[B("rows")], writes=[B("rows")])
            kb.op("scalar", lambda e: e.activation(ri, ri, AF.Exp, scale=-0.5),
                  reads=[B("rows")], writes=[B("rows")])
            kb.op("tensor", lambda e: e.matmul(ps[2][:, :], A("sel0"), r[:], start=True, stop=True),
                  reads=[B("sel0"), B("rows")], writes=[psB[2]])
            kb.op("tensor", lambda e: e.matmul(ps[3][:, :], A("sel96"), r[:], start=True, stop=True),
                  reads=[B("sel96"), B("rows")], writes=[psB[3]])
            for dc in range(NDC):
                kb.op("vector", lambda e, dc=dc: e.tensor_tensor(A("tmpA"), xsh(dc), ps[2][:, :], op=ALU.subtract),
                      reads=[B("PX"), psB[2]], writes=[B("tmpA")])
                kb.op("vector", lambda e: e.tensor_tensor(A("tmpB"), A("tmpA"), ps[3][:, :], op=ALU.mult),
                      reads=[B("tmpA"), psB[3]], writes=[B("tmpB")])
                kb.op("vector", lambda e, dc=dc: e.tensor_scalar(
                    PH[:, dc * 512:(dc + 1) * 512], A("tmpB"),
                    sb[f"lg{l}"][:, dc:dc + 1], sb[f"lb{l}"][:, dc:dc + 1],
                    op0=ALU.mult, op1=ALU.add),
                    reads=[B("tmpB"), B(f"lg{l}"), B(f"lb{l}")], writes=[B("PH")])

            def pslot(jc):
                if jc < 8:
                    return sb["PP1"][:, jc * 512:(jc + 1) * 512], B("PP1")
                if jc < 16:
                    return sb["PP2"][:, (jc - 8) * 512:(jc - 7) * 512], B("PP2")
                return sb["PP3"][:, (jc - 16) * 512:(jc - 15) * 512], B("PP3")

            for jc in range(NJ):
                wn = "PW0" if jc % 2 == 0 else "PW1"
                kb.dma("qw", sb[wn][:, 0:2048], dr[f"w1_{l}"][:, jc * 2048:(jc + 1) * 2048],
                       reads=[wbuf[f"w1_{l}"]], writes=[B(wn)])
                for dc in range(NDC):
                    kb.op("tensor", lambda e, wn=wn, dc=dc: e.matmul(
                        ps[0][:, :], sb[wn][:, dc * 256:dc * 256 + 128],
                        PH[:, dc * 512:(dc + 1) * 512], start=(dc == 0), stop=(dc == NDC - 1)),
                        reads=[B(wn), B("PH")], writes=[psB[0]])
                for dc in range(NDC):
                    kb.op("tensor", lambda e, wn=wn, dc=dc: e.matmul(
                        ps[1][:, :], sb[wn][:, dc * 256 + 128:dc * 256 + 256],
                        PH[:, dc * 512:(dc + 1) * 512], start=(dc == 0), stop=(dc == NDC - 1)),
                        reads=[B(wn), B("PH")], writes=[psB[1]])
                kb.op("scalar", lambda e, jc=jc: e.activation(
                    A("tmpA"), ps[1][:, :], AF.Gelu, bias=sb[f"b1g{l}"][:, jc:jc + 1]),
                    reads=[psB[1], B(f"b1g{l}")], writes=[B("tmpA")])
                pdst, pB = pslot(jc)
                kb.op("vector", lambda e, jc=jc, pdst=pdst: e.scalar_tensor_tensor(
                    pdst, ps[0][:, :], sb[f"b1a{l}"][:, jc:jc + 1], A("tmpA"),
                    op0=ALU.add, op1=ALU.mult),
                    reads=[psB[0], B(f"b1a{l}"), B("tmpA")], writes=[pB])
            if final:
                kb.dma("qw", sb["WDB0"][:, 0:2048],
                       scr["attnb"][:, t0:t0 + T].rearrange("(dc p) t -> p dc t", p=128)[:, 0:4, :],
                       reads=[hbuf["attnb"][c]], writes=[B("WDB0")])
                kb.dma("qw", sb["WDB1"][:, 0:2048],
                       scr["attnb"][:, t0:t0 + T].rearrange("(dc p) t -> p dc t", p=128)[:, 4:8, :],
                       reads=[hbuf["attnb"][c]], writes=[B("WDB1")])
                kb.dma("qw", PH[:, 0:4096],
                       dr["x"][:, t0:t0 + T].rearrange("(dc p) t -> p dc t", p=128),
                       reads=[hbuf["x"][c]], writes=[B("PH")])
            for jc in range(NJ):
                wn = "PW2" if jc % 2 == 0 else "PW3"
                kb.dma("qw", sb[wn][:, 0:1024], dr[f"w2_{l}"][:, jc * 1024:(jc + 1) * 1024],
                       reads=[wbuf[f"w2_{l}"]], writes=[B(wn)])
                psrc, pB = pslot(jc)
                for ec in range(NDC):
                    kb.op("tensor", lambda e, wn=wn, ec=ec, psrc=psrc, jc=jc: e.matmul(
                        ps[ec][:, :], sb[wn][:, ec * 128:(ec + 1) * 128], psrc,
                        start=(jc == 0), stop=(jc == NJ - 1)),
                        reads=[B(wn), pB], writes=[psB[ec]])
            for ec in range(NDC):
                if not final:
                    kb.op("vector", lambda e, ec=ec: e.scalar_tensor_tensor(
                        PO[:, ec * 512:(ec + 1) * 512], ps[ec][:, :],
                        sb[f"b2_{l}"][:, ec:ec + 1], PX[:, ec * 520 + 1:ec * 520 + 513],
                        op0=ALU.add, op1=ALU.add),
                        reads=[psB[ec], B(f"b2_{l}"), B("PX")], writes=[B("PO")])
                else:
                    kb.op("vector", lambda e, ec=ec: e.scalar_tensor_tensor(
                        A("tmpA"), ps[ec][:, :],
                        sb[f"b2_{l}"][:, ec:ec + 1], PX[:, ec * 520 + 1:ec * 520 + 513],
                        op0=ALU.add, op1=ALU.add),
                        reads=[psB[ec], B(f"b2_{l}"), B("PX")], writes=[B("tmpA")])
                    at = (sb["WDB0"] if ec < 4 else sb["WDB1"])[:, (ec % 4) * 512:(ec % 4 + 1) * 512]
                    atB = B("WDB0") if ec < 4 else B("WDB1")
                    kb.op("vector", lambda e, at=at: e.tensor_tensor(A("tmpB"), A("tmpA"), at, op=ALU.add),
                          reads=[B("tmpA"), atB], writes=[B("tmpB")])
                    kb.op("vector", lambda e, ec=ec: e.tensor_tensor(
                        PO[:, ec * 512:(ec + 1) * 512], A("tmpB"), PH[:, ec * 512:(ec + 1) * 512],
                        op=ALU.add),
                        reads=[B("tmpB"), B("PH")], writes=[B("PO")])
            tgt = out_h if final else scr[dst]
            tb = hbuf["out"][c] if final else hbuf[dst][c]
            kb.dma("qg", tgt[:, t0:t0 + T].rearrange("(dc p) t -> p dc t", p=128),
                   PO[:, 0:NDC * T], reads=[B("PO")], writes=[tb])

        if stages >= 1:
            for c in range(NCH):
                s1_chunk(c)
        if stages >= 2:
            s2_prep()
            for c in range(NCH):
                s2_chunk(c)
        if stages >= 3:
            s3_prep(0)
            for c in range(NCH):
                s3_chunk(0, c, "xn", "xa")
        if stages >= 4:
            for c in range(NCH):
                s4_chunk(0, c, "xa", "xb")
        if stages >= 5:
            s3_prep(1)
            for c in range(NCH):
                s3_chunk(1, c, "xb", "xn")
        if stages >= 6:
            for c in range(NCH):
                s4_chunk(1, c, "xn", None, final=True)

        final_bufs = []
        if debug_spills:
            for n in ("xn", "attnb", "xa", "xb"):
                final_bufs += hbuf[n]
        final_bufs += hbuf["out"]
        final_bufs = [b for b in final_bufs if b.writer is not None]
        if final_bufs:
            kb.op("sync", lambda e: e.nop(), reads=final_bufs, writes=())

        kb.finalize_and_emit(sems)
    return nc


def kernel(**inputs):
    g = host_prep(inputs)
    x = np.asarray(inputs["L0_attn_x"], dtype=np.float32)
    nc = build_program(debug_spills=CFG["debug_spills"], stages=CFG["stages"])
    in_maps = []
    for core in range(8):
        m = dict(g)
        m["x"] = np.ascontiguousarray(x[core].T)
        in_maps.append(m)
    res = run_bass_kernel_spmd(nc, in_maps, core_ids=list(range(8)))
    outs = [r["out"] for r in res.results]
    return np.ascontiguousarray(np.stack(outs, 0).transpose(0, 2, 1))


# version 02
# speedup vs baseline: 1.0946x; 1.0946x over previous
import math
import numpy as np
import concourse.bass as bass
import concourse.mybir as mybir
from concourse.bass_utils import run_bass_kernel_spmd

F32 = mybir.dt.float32
AF = mybir.ActivationFunctionType
ALU = mybir.AluOpType

SEQ = 4096
D = 1024
T = 512
NCH = SEQ // T
NDC = D // 128
INNER = 512
DS = 8
NH = 8
DH = 64
HID = 2730
NJ = 22
HIDP = NJ * 128
NEG = -1.0e30
TWO_PI = 2.0 * math.pi

CFG = {"stages": 6, "debug_spills": False, "serialize": False}



def _tile_lhsT_nmajor(m, kc, ncnt, nw=128):
    K, N = m.shape
    assert K == kc * 128
    pad = ncnt * nw - N
    if pad:
        m = np.concatenate([m, np.zeros((K, pad), m.dtype)], axis=1)
    return np.ascontiguousarray(
        m.reshape(kc, 128, ncnt, nw).transpose(1, 2, 0, 3).reshape(128, -1))


def _cols(v, nt):
    return np.ascontiguousarray(v.reshape(nt, 128).T)


def host_prep(inputs):
    f = np.float32
    g = {}
    g["w_in"] = _tile_lhsT_nmajor(np.ascontiguousarray(inputs["L0_attn_W_in"].astype(f).T), 8, 8)
    g["w_out"] = _tile_lhsT_nmajor(np.ascontiguousarray(inputs["L0_attn_W_out"].astype(f).T), 8, 8)
    g["w_xp"] = _tile_lhsT_nmajor(np.ascontiguousarray(inputs["L0_attn_W_xproj"].astype(f).T), 4, 1, nw=80)
    g["w_dt"] = np.ascontiguousarray(inputs["L0_attn_W_dt"].astype(f).T)
    g["b_dt"] = _cols(inputs["L0_attn_b_dt"].astype(f), 4)
    g["d_col"] = _cols(inputs["L0_attn_D"].astype(f), 4)
    al = inputs["L0_attn_A_log"].astype(f)
    g["a_cols"] = np.ascontiguousarray(al.reshape(4, 128, DS).transpose(1, 0, 2).reshape(128, 4 * DS))
    g["kx"] = np.ascontiguousarray(inputs["L0_attn_Kx"][:, 0, :].astype(f).reshape(4, 128, 3).transpose(1, 0, 2).reshape(128, 12))
    g["kz"] = np.ascontiguousarray(inputs["L0_attn_Kz"][:, 0, :].astype(f).reshape(4, 128, 3).transpose(1, 0, 2).reshape(128, 12))
    for l in range(2):
        g[f"wqkv{l}"] = _tile_lhsT_nmajor(inputs["L0_ffn_wqkv"][l].astype(f), 8, 12)
        g[f"wo{l}"] = _tile_lhsT_nmajor(inputs["L0_ffn_wo"][l].astype(f), 4, 8)
        w1 = inputs["L0_ffn_w1"][l].astype(f)
        blocks = []
        for jc in range(NJ):
            a = w1[:, jc * 128:min((jc + 1) * 128, HID)]
            gg = w1[:, HID + jc * 128: HID + min((jc + 1) * 128, HID)]
            if a.shape[1] < 128:
                a = np.concatenate([a, np.zeros((D, 128 - a.shape[1]), f)], 1)
                gg = np.concatenate([gg, np.zeros((D, 128 - gg.shape[1]), f)], 1)
            blk = np.concatenate([a, gg], 1)
            blocks.append(blk.reshape(8, 128, 256).transpose(1, 0, 2).reshape(128, 2048))
        g[f"w1_{l}"] = np.ascontiguousarray(np.concatenate(blocks, 1))
        w2 = inputs["L0_ffn_w2"][l].astype(f)
        w2p = np.concatenate([w2, np.zeros((HIDP - HID, D), f)], 0)
        w2r = w2p.reshape(NJ, 128, 8, 128).transpose(1, 0, 2, 3).reshape(128, -1)
        g[f"w2_{l}"] = np.ascontiguousarray(w2r)
        b1 = inputs["L0_ffn_b1"][l].astype(f)
        g[f"b1a{l}"] = _cols(np.concatenate([b1[:HID], np.zeros(HIDP - HID, f)]), NJ)
        g[f"b1g{l}"] = _cols(np.concatenate([b1[HID:], np.zeros(HIDP - HID, f)]), NJ)
        g[f"b2_{l}"] = _cols(inputs["L0_ffn_b2"][l].astype(f), 8)
        g[f"lg{l}"] = _cols(inputs["L0_ffn_ln_g"][l].astype(f), 8)
        g[f"lb{l}"] = _cols(inputs["L0_ffn_ln_b"][l].astype(f), 8)
        g[f"qs{l}"] = np.ascontiguousarray(np.tile(inputs["L0_ffn_q_scale"][l].astype(f), 2)[:, None])
        g[f"ks{l}"] = np.ascontiguousarray(np.tile(inputs["L0_ffn_k_scale"][l].astype(f), 2)[:, None])
    g["ones1"] = np.ones((128, 1), f)
    o2 = np.zeros((128, 2), f)
    o2[:64, 0] = 1.0
    o2[64:, 1] = 1.0
    g["ones2"] = o2
    perm = np.zeros((128, 128), f)
    for i in range(128):
        perm[i ^ 32, i] = 1.0
    g["perm"] = perm
    mc = np.zeros((128, 128), f)
    mp = np.zeros((128, 128), f)
    for k in range(128):
        for q in range(128):
            if k > q:
                mc[k, q] = NEG
            if k < q:
                mp[k, q] = NEG
    g["mask_cur"] = np.concatenate([np.full((128, 384), NEG, f), mc], 1)
    g["mask_prev"] = np.concatenate([mp, np.full((128, 384), NEG, f)], 1)
    g["trow"] = np.arange(SEQ, dtype=f)[None, :]
    g["hrow"] = np.tile(np.arange(0, DH, 2, dtype=f), 2)[None, :]
    sel16 = np.zeros((16, 16 * 128), f)
    for sv in range(16):
        sel16[sv, sv * 128:(sv + 1) * 128] = 1.0
    g["sel16"] = sel16
    for nm, row in (("sel0", 0), ("sel64", 64), ("sel96", 96)):
        m = np.zeros((128, 128), f)
        m[row, :] = 1.0
        g[nm] = m
    selrn = np.zeros((128, 128), f)
    selrn[0, 0:64] = 1.0
    selrn[1, 64:128] = 1.0
    g["selrn"] = selrn
    return g


IN_SHAPES = {
    "x": (D, SEQ),
    "w_in": (128, 8192), "w_out": (128, 8192), "w_xp": (128, 320),
    "w_dt": (64, 512), "b_dt": (128, 4), "d_col": (128, 4), "a_cols": (128, 32),
    "kx": (128, 12), "kz": (128, 12), "ones1": (128, 1), "ones2": (128, 2),
    "perm": (128, 128), "mask_cur": (128, 512), "mask_prev": (128, 512),
    "trow": (1, SEQ), "hrow": (1, 64),
    "sel16": (16, 2048), "sel0": (128, 128), "sel64": (128, 128),
    "sel96": (128, 128), "selrn": (128, 128),
}
for _l in range(2):
    IN_SHAPES.update({
        f"wqkv{_l}": (128, 12288), f"wo{_l}": (128, 4096),
        f"w1_{_l}": (128, NJ * 2048),
        f"b1a{_l}": (128, NJ), f"b1g{_l}": (128, NJ), f"b2_{_l}": (128, 8),
        f"lg{_l}": (128, 8), f"lb{_l}": (128, 8), f"qs{_l}": (128, 1), f"ks{_l}": (128, 1),
        f"w2_{_l}": (128, NJ * 1024),
    })



ENGS = ("sync", "tensor", "vector", "scalar", "gpsimd")
QUEUES = {"qw": "sync", "qg": "gpsimd"}


class _Op:
    __slots__ = ("stream", "emit", "flag", "val", "waits", "isdma", "dmaidx", "snap")

    def __init__(self, stream, emit, isdma=False):
        self.stream = stream
        self.emit = emit
        self.flag = False
        self.val = None
        self.waits = []
        self.isdma = isdma
        self.dmaidx = None
        self.snap = None


class Buf:
    __slots__ = ("name", "writer", "readers")

    def __init__(self, name):
        self.name = name
        self.writer = None
        self.readers = {}


class KB:
    def __init__(self, nc, serialize=False):
        self.nc = nc
        self.streams = {e: [] for e in ENGS}
        self.dmacount = {q: 0 for q in QUEUES}
        self.allops = []
        self.serialize = serialize
        self.lastop = {}

    def buf(self, name):
        return Buf(name)

    def _add(self, stream, emit, reads, writes, isdma=False):
        eng = QUEUES[stream] if stream in QUEUES else stream
        op = _Op(stream, emit, isdma=isdma)
        op.snap = dict(self.dmacount)
        if isdma:
            op.dmaidx = self.dmacount[stream]
            self.dmacount[stream] += 1
        deps = []
        for b in reads:
            if b.writer is not None:
                deps.append(b.writer)
        for b in writes:
            if b.writer is not None:
                deps.append(b.writer)
            deps.extend(b.readers.values())
        if self.serialize:
            deps.extend(self.lastop.values())
        for (cs, ps_) in CFG.get("ser_pairs", ()):
            if stream == cs and ps_ in self.lastop:
                deps.append(self.lastop[ps_])
        seen = set()
        for d in deps:
            if id(d) in seen:
                continue
            seen.add(id(d))
            if d.stream == stream:
                continue
            op.waits.append(d)
        self.streams[eng].append(op)
        self.allops.append(op)
        self.lastop[stream] = op
        for b in reads:
            b.readers[stream] = op
        for b in writes:
            b.writer = op
            b.readers = {}
        return op

    def op(self, eng, emit, reads=(), writes=()):
        return self._add(eng, emit, reads, writes, isdma=False)

    def dma(self, q, out_ap, in_ap, reads=(), writes=()):
        def emit(eng, _o=out_ap, _i=in_ap):
            return eng.dma_start(out=_o, in_=_i)
        return self._add(q, emit, reads, writes, isdma=True)

    def finalize_and_emit(self, sems):
        for op in self.allops:
            for d in op.waits:
                if not d.isdma:
                    d.flag = True
        for e in ENGS:
            v = 0
            for op in self.streams[e]:
                if op.isdma:
                    continue
                if op.flag:
                    v += 1
                    op.val = v

        def tgt(d, consumer):
            if d.isdma:
                return (d.stream, 16 * max(d.dmaidx + 1, consumer.snap.get(d.stream, 0)))
            return (d.stream, d.val)

        emitted = {e: [] for e in ENGS}
        for e in ENGS:
            hi = {}
            for op in self.streams[e]:
                for d in op.waits:
                    s, v = tgt(d, op)
                    assert v is not None
                    if hi.get(s, 0) >= v:
                        continue
                    hi[s] = v
                    emitted[e].append(("w", s, v))
                inc = (op.stream, 16) if op.isdma else ((op.stream, 1) if op.flag else None)
                emitted[e].append(("o", op.emit, inc))
        nc = self.nc
        with nc.Block() as block:
            def mk(e):
                def run(eng):
                    for item in emitted[e]:
                        if item[0] == "w":
                            eng.wait_ge(sems[item[1]], item[2])
                        else:
                            h = item[1](eng)
                            if item[2] is not None:
                                h.then_inc(sems[item[2][0]], item[2][1])
                return run
            block.sync(mk("sync"))
            block.tensor(mk("tensor"))
            block.vector(mk("vector"))
            block.scalar(mk("scalar"))
            block.gpsimd(mk("gpsimd"))



def build_program(debug_spills=False, stages=6):
    nc = bass.Bass("TRN2", target_bir_lowering=False)
    f = F32

    dr = {n: nc.dram_tensor(n, list(s), f, kind="ExternalInput").ap()
          for n, s in IN_SHAPES.items()}
    scr = {}
    for n in ("xn", "attnb", "xa", "xb"):
        kind = {"kind": "ExternalOutput"} if debug_spills else {}
        scr[n] = nc.dram_tensor(n, [D, SEQ], f, **kind).ap()
    rotm = nc.dram_tensor("rotm", [256, SEQ], f).ap()
    dbg = {}
    if debug_spills:
        for n in ("dbg_xz", "dbg_dy", "dbg_h", "dbg_bc"):
            dbg[n] = nc.dram_tensor(n, [1024, SEQ], f, kind="ExternalOutput").ap()
        for n in ("dbg_dec", "dbg_drv", "dbg_st"):
            dbg[n] = nc.dram_tensor(n, [128, 4104], f, kind="ExternalOutput").ap()
    out_h = nc.dram_tensor("out", [D, SEQ], f, kind="ExternalOutput").ap()

    kb = KB(nc, serialize=CFG.get("serialize", False))
    hbuf = {n: [kb.buf(f"{n}{c}") for c in range(NCH)] for n in
            ("x", "xn", "attnb", "xa", "xb", "out")}
    rotmB = [kb.buf(f"rotm{i}") for i in range(4)]
    wbuf = {n: kb.buf("W" + n) for n in IN_SHAPES}

    import contextlib
    sb = {}
    bb = {}

    with contextlib.ExitStack() as stack:
        def alloc(name, cols, parts=128, dtype=F32):
            h = stack.enter_context(nc.sbuf_tensor("sb_" + name, [parts, cols], dtype))
            sb[name] = h
            bb[name] = kb.buf(name)
            return h

        for i in range(4):
            alloc(f"PW{i}", 4104)
        alloc("PX", 4160)
        alloc("PH", 4160)
        alloc("PO", 4104)
        alloc("PP1", 4096)
        alloc("PP2", 4096)
        alloc("PP3", 4160)
        alloc("WDB0", 2048)
        alloc("WDB1", 2048)
        for n in ("tmpA", "tmpB", "bc0", "bc1"):
            alloc(n, T)
        for n in ("sel0", "sel64", "sel96", "selrn"):
            alloc(n, 128)
        alloc("rows", T, parts=128)
        alloc("proj", T, parts=80)
        alloc("bct", T, parts=16)
        alloc("carry", 32)
        alloc("tint", T, parts=64, dtype=mybir.dt.int32)
        for n in ("ones1", "ones2", "perm", "mask_cur", "mask_prev", "b_dt",
                  "d_col", "a_cols", "kx", "kz", "w_xp", "hrow"):
            alloc(n, IN_SHAPES[n][1], parts=IN_SHAPES[n][0])
        alloc("w_dt", 512, parts=64)
        for l in range(2):
            for n in (f"b1a{l}", f"b1g{l}", f"b2_{l}", f"lg{l}", f"lb{l}", f"qs{l}", f"ks{l}"):
                alloc(n, IN_SHAPES[n][1], parts=IN_SHAPES[n][0])
        alloc("invf", 64, parts=1)
        alloc("logb", 64, parts=1)

        ps, psB = [], []
        for i in range(8):
            h = stack.enter_context(nc.psum_tensor(f"pb{i}", [128, 512], f))
            ps.append(h)
            psB.append(kb.buf(f"pb{i}"))

        sems = {}
        for e in ENGS:
            sems[e] = stack.enter_context(nc.semaphore("s_" + e))
        for q in QUEUES:
            sems[q] = stack.enter_context(nc.semaphore("s_" + q))

        A = lambda n: sb[n][:]
        B = lambda n: bb[n]

        def ld_const(n):
            kb.dma("qw", A(n), dr[n], reads=[wbuf[n]], writes=[B(n)])

        for n in ("ones1", "ones2", "perm", "mask_cur", "mask_prev", "b_dt",
                  "d_col", "a_cols", "kx", "kz", "w_xp", "w_dt", "hrow",
                  "sel0", "sel64", "sel96", "selrn"):
            ld_const(n)
        trow = sb["PP1"][0:1, 0:SEQ]
        prow = sb["PP2"][0:1, 0:SEQ]
        kb.dma("qw", trow, dr["trow"], reads=[wbuf["trow"]], writes=[B("PP1")])
        for l in range(2):
            for n in (f"b1a{l}", f"b1g{l}", f"b2_{l}", f"lg{l}", f"lb{l}", f"qs{l}", f"ks{l}"):
                ld_const(n)

        kb.op("vector", lambda e: e.tensor_scalar(A("a_cols"), A("a_cols"), 10.0, None, op0=ALU.min),
              reads=[B("a_cols")], writes=[B("a_cols")])
        kb.op("scalar", lambda e: e.activation(A("a_cols"), A("a_cols"), AF.Exp),
              reads=[B("a_cols")], writes=[B("a_cols")])
        kb.op("scalar", lambda e: e.activation(A("a_cols"), A("a_cols"), AF.Ln, bias=1.0),
              reads=[B("a_cols")], writes=[B("a_cols")])
        kb.op("vector", lambda e: e.tensor_scalar(A("a_cols"), A("a_cols"), 1e-4, None, op0=ALU.add),
              reads=[B("a_cols")], writes=[B("a_cols")])

        def s0_tables():
            inv_c = -math.log(10000.0) / DH
            kb.op("scalar", lambda e: e.activation(A("invf"), A("hrow"), AF.Exp, scale=inv_c),
                  reads=[B("hrow")], writes=[B("invf")])
            kb.op("vector", lambda e: e.tensor_scalar(A("logb"), A("hrow"), 1.0 / (1.4 * DH), 0.4 / 1.4, op0=ALU.mult, op1=ALU.add),
                  reads=[B("hrow")], writes=[B("logb")])
            kb.op("scalar", lambda e: e.activation(A("logb"), A("logb"), AF.Ln),
                  reads=[B("logb")], writes=[B("logb")])
            kb.op("vector", lambda e: e.tensor_scalar(prow, trow, 1.0 / 256.0, -8.0, op0=ALU.mult, op1=ALU.add),
                  reads=[B("PP1")], writes=[B("PP2")])
            u = sb["PX"][0:64, 0:512]
            fr = sb["PX"][0:64, 520:1032]
            sc3 = sb["PX"][0:64, 1040:1552]
            sn = sb["PH"][0:64, 0:512]
            cs = sb["PH"][0:64, 520:1032]
            scp = sb["PH"][0:64, 1040:1552]
            scm = sb["PH"][0:64, 1560:2072]
            ti = sb["tint"][0:64, :]

            def wrap(dst, src, rb, wb):
                kb.op("vector", lambda e: e.tensor_copy(ti, src), reads=[rb], writes=[B("tint")])
                kb.op("vector", lambda e: e.tensor_copy(sc3, ti), reads=[B("tint")], writes=[B("PX")])
                kb.op("vector", lambda e: e.tensor_tensor(dst, src, sc3, op=ALU.subtract),
                      reads=[rb, B("PX")], writes=[wb])
                kb.op("vector", lambda e: e.tensor_scalar(sc3, dst, 0.5, None, op0=ALU.is_ge),
                      reads=[wb], writes=[B("PX")])
                kb.op("vector", lambda e: e.tensor_tensor(dst, dst, sc3, op=ALU.subtract),
                      reads=[wb, B("PX")], writes=[wb])
                kb.op("vector", lambda e: e.tensor_scalar(sc3, dst, -0.5, None, op0=ALU.is_lt),
                      reads=[wb], writes=[B("PX")])
                kb.op("vector", lambda e: e.tensor_tensor(dst, dst, sc3, op=ALU.add),
                      reads=[wb, B("PX")], writes=[wb])

            for tc in range(8):
                sl = slice(tc * 512, (tc + 1) * 512)
                kb.op("tensor", lambda e, sl=sl: e.matmul(ps[0][0:64, :], A("invf"), trow[:, sl], start=True, stop=True),
                      reads=[B("invf"), B("PP1")], writes=[psB[0]])
                kb.op("vector", lambda e: e.tensor_scalar(u, ps[0][0:64, :], 1.0 / TWO_PI, None, op0=ALU.mult),
                      reads=[psB[0]], writes=[B("PX")])
                kb.op("tensor", lambda e, sl=sl: e.matmul(ps[1][0:64, :], A("logb"), prow[:, sl], start=True, stop=True),
                      reads=[B("logb"), B("PP2")], writes=[psB[1]])
                kb.op("scalar", lambda e: e.activation(scp, ps[1][0:64, :], AF.Exp),
                      reads=[psB[1]], writes=[B("PH")])
                kb.op("scalar", lambda e: e.activation(scm, ps[1][0:64, :], AF.Exp, scale=-1.0),
                      reads=[psB[1]], writes=[B("PH")])
                wrap(fr, u, B("PX"), B("PX"))
                kb.op("scalar", lambda e: e.activation(sn, fr, AF.Sin, scale=TWO_PI),
                      reads=[B("PX")], writes=[B("PH")])
                kb.op("vector", lambda e: e.tensor_scalar(u, fr, 0.25, None, op0=ALU.add),
                      reads=[B("PX")], writes=[B("PX")])
                wrap(fr, u, B("PX"), B("PX"))
                kb.op("scalar", lambda e: e.activation(cs, fr, AF.Sin, scale=TWO_PI),
                      reads=[B("PX")], writes=[B("PH")])
                for (ri, trg, sgn, scl) in ((0, cs, 1.0, scp), (64, sn, 1.0, scp),
                                            (128, cs, 1.0, scm), (192, sn, 1.0, scm)):
                    dst = sb["PX"][0:64, 1560:2072]
                    kb.op("vector", lambda e, trg=trg, scl=scl: e.tensor_tensor(dst, trg, scl, op=ALU.mult),
                          reads=[B("PH")], writes=[B("PX")])
                    if ri in (64, 192):
                        kb.op("vector", lambda e: e.tensor_scalar(sb["PX"][0:32, 1560:2072], sb["PX"][0:32, 1560:2072], -1.0, None, op0=ALU.mult),
                              reads=[B("PX")], writes=[B("PX")])
                    kb.dma("qg", rotm[ri:ri + 64, sl], dst,
                           reads=[B("PX")], writes=[rotmB[ri // 64]])

        s0_tables()

        def s1_chunk(c):
            t0 = c * T
            kb.dma("qw", sb["PX"][:, 0:NDC * T],
                   dr["x"][:, t0:t0 + T].rearrange("(dc p) t -> p dc t", p=128),
                   reads=[hbuf["x"][c]], writes=[B("PX")])
            for dc in range(NDC):
                xt = sb["PX"][:, dc * T:(dc + 1) * T]
                kb.op("tensor", lambda e, xt=xt, dc=dc: e.matmul(
                    ps[0][0:1, :], A("ones1"), xt, start=(dc == 0), stop=(dc == NDC - 1)),
                    reads=[B("PX"), B("ones1")], writes=[psB[0]])
            for dc in range(NDC):
                xt = sb["PX"][:, dc * T:(dc + 1) * T]
                kb.op("scalar", lambda e, xt=xt: e.activation(A("tmpA"), xt, AF.Square),
                      reads=[B("PX")], writes=[B("tmpA")])
                kb.op("tensor", lambda e, dc=dc: e.matmul(
                    ps[1][0:1, :], A("ones1"), A("tmpA"), start=(dc == 0), stop=(dc == NDC - 1)),
                    reads=[B("tmpA"), B("ones1")], writes=[psB[1]])
            r = sb["rows"]
            rm, rm2, rv, ri = r[0:1, :], r[32:33, :], r[64:65, :], r[96:97, :]
            kb.op("vector", lambda e: e.tensor_scalar(rm, ps[0][0:1, :], 1.0 / D, None, op0=ALU.mult),
                  reads=[psB[0]], writes=[B("rows")])
            kb.op("vector", lambda e: e.tensor_tensor(rm2, rm, rm, op=ALU.mult),
                  reads=[B("rows")], writes=[B("rows")])
            kb.op("vector", lambda e: e.scalar_tensor_tensor(rv, ps[1][0:1, :], 1.0 / D, rm2,
                                                             op0=ALU.mult, op1=ALU.subtract),
                  reads=[psB[1], B("rows")], writes=[B("rows")])
            kb.op("vector", lambda e: e.tensor_scalar(rv, rv, 1e-5, None, op0=ALU.add),
                  reads=[B("rows")], writes=[B("rows")])
            kb.op("scalar", lambda e: e.activation(ri, rv, AF.Ln),
                  reads=[B("rows")], writes=[B("rows")])
            kb.op("scalar", lambda e: e.activation(ri, ri, AF.Exp, scale=-0.5),
                  reads=[B("rows")], writes=[B("rows")])
            kb.op("tensor", lambda e: e.matmul(ps[2][:, :], A("sel0"), r[:], start=True, stop=True),
                  reads=[B("sel0"), B("rows")], writes=[psB[2]])
            kb.op("tensor", lambda e: e.matmul(ps[3][:, :], A("sel96"), r[:], start=True, stop=True),
                  reads=[B("sel96"), B("rows")], writes=[psB[3]])
            for dc in range(NDC):
                xt = sb["PX"][:, dc * T:(dc + 1) * T]
                ot = sb["PO"][:, dc * T:(dc + 1) * T]
                kb.op("vector", lambda e, xt=xt, ot=ot: e.tensor_tensor(ot, xt, ps[2][:, :], op=ALU.subtract),
                      reads=[B("PX"), psB[2]], writes=[B("PO")])
                kb.op("vector", lambda e, ot=ot: e.tensor_tensor(ot, ot, ps[3][:, :], op=ALU.mult),
                      reads=[B("PO"), psB[3]], writes=[B("PO")])
            kb.dma("qg", scr["xn"][:, t0:t0 + T].rearrange("(dc p) t -> p dc t", p=128),
                   sb["PO"][:, 0:NDC * T], reads=[B("PO")], writes=[hbuf["xn"][c]])

        def s2_prep():
            kb.op("vector", lambda e: e.memset(A("carry"), 0.0), reads=[], writes=[B("carry")])
            kb.op("vector", lambda e: e.memset(sb["PW1"][:, 0:8], 0.0), reads=[], writes=[B("PW1")])
            kb.dma("qw", sb["PW3"][0:16, 0:2048], dr["sel16"],
                   reads=[wbuf["sel16"]], writes=[B("PW3")])

        def s2_chunk(c):
            t0 = c * T
            PX, PH = sb["PX"], sb["PH"]
            lo = max(t0 - 1, 0)
            hi = min(t0 + T + 1, SEQ)
            ncol = hi - lo
            coff = 1 if c == 0 else 0
            rd = [hbuf["xn"][c]]
            if c > 0:
                rd.append(hbuf["xn"][c - 1])
            if c < NCH - 1:
                rd.append(hbuf["xn"][c + 1])
            kb.dma("qw", PX[:].rearrange("p (dc j) -> p dc j", dc=NDC)[:, :, coff:coff + ncol],
                   scr["xn"][:, lo:hi].rearrange("(dc p) t -> p dc t", p=128),
                   reads=rd, writes=[B("PX")])
            for ec in range(NDC):
                wb = "WDB0" if ec % 2 == 0 else "WDB1"
                half = (ec // 2) % 2
                wsl = sb[wb][:, half * 1024:(half + 1) * 1024]
                kb.dma("qw", wsl, dr["w_in"][:, ec * 1024:(ec + 1) * 1024],
                       reads=[wbuf["w_in"]], writes=[B(wb)])
                pk = ec % 2
                j0 = coff
                j1 = 514 if c < NCH - 1 else 513
                m0 = j0
                m1 = min(j0 + 512, j1)
                for dc in range(NDC):
                    kb.op("tensor", lambda e, ec=ec, dc=dc, wsl=wsl, m0=m0, m1=m1, pk=pk: e.matmul(
                        ps[pk][:, 0:m1 - m0], wsl[:, dc * 128:(dc + 1) * 128],
                        PX[:, dc * 520 + m0:dc * 520 + m1], start=(dc == 0), stop=(dc == NDC - 1)),
                        reads=[B(wb), B("PX")], writes=[psB[pk]])
                kb.op("scalar", lambda e, ec=ec, pk=pk, m0=m0, m1=m1: e.copy(
                    PH[:, ec * 520 + m0:ec * 520 + m1], ps[pk][:, 0:m1 - m0]),
                    reads=[psB[pk]], writes=[B("PH")])
                if m1 < j1:
                    for dc in range(NDC):
                        kb.op("tensor", lambda e, ec=ec, dc=dc, wsl=wsl, m1=m1, j1=j1, pk=pk: e.matmul(
                            ps[2 + pk][:, 0:j1 - m1], wsl[:, dc * 128:(dc + 1) * 128],
                            PX[:, dc * 520 + m1:dc * 520 + j1], start=(dc == 0), stop=(dc == NDC - 1)),
                            reads=[B(wb), B("PX")], writes=[psB[2 + pk]])
                    kb.op("scalar", lambda e, ec=ec, pk=pk, m1=m1, j1=j1: e.copy(
                        PH[:, ec * 520 + m1:ec * 520 + j1], ps[2 + pk][:, 0:j1 - m1]),
                        reads=[psB[2 + pk]], writes=[B("PH")])
                if c == 0:
                    kb.op("vector", lambda e, ec=ec: e.memset(PH[:, ec * 520:ec * 520 + 1], 0.0),
                          reads=[], writes=[B("PH")])
                if c == NCH - 1:
                    kb.op("vector", lambda e, ec=ec: e.memset(PH[:, ec * 520 + 513:ec * 520 + 514], 0.0),
                          reads=[], writes=[B("PH")])
            for ec in range(NDC):
                kern = "kx" if ec < 4 else "kz"
                cb = ec % 4
                h0 = PH[:, ec * 520 + 0:ec * 520 + 512]
                h1 = PH[:, ec * 520 + 1:ec * 520 + 513]
                h2 = PH[:, ec * 520 + 2:ec * 520 + 514]
                kb.op("vector", lambda e, h1=h1, kern=kern, cb=cb: e.tensor_scalar(
                    A("tmpA"), h1, sb[kern][:, cb * 3 + 1:cb * 3 + 2], None, op0=ALU.mult),
                    reads=[B("PH"), B(kern)], writes=[B("tmpA")])
                kb.op("vector", lambda e, h0=h0, kern=kern, cb=cb: e.scalar_tensor_tensor(
                    A("tmpB"), h0, sb[kern][:, cb * 3 + 0:cb * 3 + 1], A("tmpA"), op0=ALU.mult, op1=ALU.add),
                    reads=[B("PH"), B(kern), B("tmpA")], writes=[B("tmpB")])
                kb.op("vector", lambda e, h2=h2, kern=kern, cb=cb: e.scalar_tensor_tensor(
                    A("tmpA"), h2, sb[kern][:, cb * 3 + 2:cb * 3 + 3], A("tmpB"), op0=ALU.mult, op1=ALU.add),
                    reads=[B("PH"), B(kern), B("tmpB")], writes=[B("tmpA")])
                kb.op("scalar", lambda e, ec=ec: e.activation(
                    sb["PP1"][:, ec * 512:(ec + 1) * 512], A("tmpA"), AF.Silu),
                    reads=[B("tmpA")], writes=[B("PP1")])
            for cb in range(4):
                kb.op("tensor", lambda e, cb=cb: e.matmul(
                    ps[4][0:80, :], sb["w_xp"][:, cb * 80:(cb + 1) * 80],
                    sb["PP1"][:, cb * 512:(cb + 1) * 512], start=(cb == 0), stop=(cb == 3)),
                    reads=[B("w_xp"), B("PP1")], writes=[psB[4]])
            kb.op("scalar", lambda e: e.copy(sb["proj"][0:80, :], ps[4][0:80, :]),
                  reads=[psB[4]], writes=[B("proj")])
            kb.op("scalar", lambda e: e.activation(A("bct"), sb["proj"][64:80, :], AF.Tanh),
                  reads=[B("proj")], writes=[B("bct")])
            for cb in range(4):
                kb.op("tensor", lambda e, cb=cb: e.matmul(
                    ps[5][:, :], sb["w_dt"][:, cb * 128:(cb + 1) * 128], sb["proj"][0:64, :],
                    start=True, stop=True),
                    reads=[B("w_dt"), B("proj")], writes=[psB[5]])
                kb.op("vector", lambda e, cb=cb: e.tensor_scalar(
                    A("tmpA"), ps[5][:, :], sb["b_dt"][:, cb:cb + 1], 10.0, op0=ALU.add, op1=ALU.min),
                    reads=[psB[5], B("b_dt")], writes=[B("tmpA")])
                kb.op("scalar", lambda e: e.activation(A("tmpB"), A("tmpA"), AF.Exp),
                      reads=[B("tmpA")], writes=[B("tmpB")])
                kb.op("scalar", lambda e: e.activation(A("tmpA"), A("tmpB"), AF.Ln, bias=1.0),
                      reads=[B("tmpB")], writes=[B("tmpA")])
                kb.op("vector", lambda e, cb=cb: e.tensor_scalar(
                    sb["PP2"][:, cb * 512:(cb + 1) * 512], A("tmpA"), 1e-4, 1.0, op0=ALU.max, op1=ALU.min),
                    reads=[B("tmpA")], writes=[B("PP2")])
            for s in range(DS):
                pk = 6 + s % 2
                kb.op("tensor", lambda e, s=s, pk=pk: e.matmul(
                    ps[pk][:, :], sb["PW3"][0:16, s * 128:(s + 1) * 128], A("bct"),
                    start=True, stop=True),
                    reads=[B("PW3"), B("bct")], writes=[psB[pk]])
                kb.op("scalar", lambda e, s=s, pk=pk: e.copy(
                    sb["PP3"][:, s * 512:(s + 1) * 512], ps[pk][:, :]),
                    reads=[psB[pk]], writes=[B("PP3")])
                kb.op("tensor", lambda e, s=s, pk=pk: e.matmul(
                    ps[pk][:, :], sb["PW3"][0:16, (8 + s) * 128:(9 + s) * 128], A("bct"),
                    start=True, stop=True),
                    reads=[B("PW3"), B("bct")], writes=[psB[pk]])
                kb.op("scalar", lambda e, s=s, pk=pk: e.copy(
                    sb["PH"][:, s * 512:(s + 1) * 512], ps[pk][:, :]),
                    reads=[psB[pk]], writes=[B("PH")])
            for cb in range(4):
                SA, SB_, SC = sb["PW0"], sb["PW1"], sb["PW2"]
                for s in range(DS):
                    dst = SA[:, 8:4104].rearrange("p (t s) -> p t s", s=8)[:, :, s]
                    kb.op("vector", lambda e, dst=dst, cb=cb, s=s: e.tensor_scalar(
                        dst, sb["PP2"][:, cb * 512:(cb + 1) * 512],
                        sb["a_cols"][:, cb * 8 + s:cb * 8 + s + 1], None, op0=ALU.mult),
                        reads=[B("PP2"), B("a_cols")], writes=[B("PW0")])
                kb.op("scalar", lambda e: e.activation(SB_[:, 8:4104], SA[:, 8:4104], AF.Exp, scale=-1.0),
                      reads=[B("PW0")], writes=[B("PW1")])
                kb.op("vector", lambda e: e.tensor_scalar(SB_[:, 8:4104], SB_[:, 8:4104], 1e-4, 1.0, op0=ALU.max, op1=ALU.min),
                      reads=[B("PW1")], writes=[B("PW1")])
                xbrep = sb["PP1"][:, cb * 512:(cb + 1) * 512].unsqueeze(2).broadcast_to([128, 512, 8])
                bview = sb["PP3"][:, 0:4096].rearrange("p (s t) -> p t s", s=8)
                kb.op("vector", lambda e, xbrep=xbrep, bview=bview: e.tensor_tensor(
                    SA[:, 8:4104].rearrange("p (t s) -> p t s", s=8),
                    bview, xbrep, op=ALU.mult),
                    reads=[B("PP3"), B("PP1")], writes=[B("PW0")])
                kb.op("scalar", lambda e, cb=cb: e.copy(SC[:, 0:8], sb["carry"][:, cb * 8:cb * 8 + 8]),
                      reads=[B("carry")], writes=[B("PW2")])
                kb.op("vector", lambda e: e.scalar_tensor_tensor(
                    SC[:, 8:4104], SB_[:, 8:4104], 1.0, SA[:, 8:4104], op0=ALU.subtract, op1=ALU.mult),
                    reads=[B("PW1"), B("PW0")], writes=[B("PW2")])
                for s in range(DS):
                    d0 = SB_[:, 0:4104].rearrange("p (j s) -> p j s", s=8)[:, :, s]
                    d1 = SC[:, 0:4104].rearrange("p (j s) -> p j s", s=8)[:, :, s]
                    oo = SA[:, 0:4104].rearrange("p (j s) -> p j s", s=8)[:, :, s]
                    kb.op("vector", lambda e, d0=d0, d1=d1, oo=oo: e.tensor_tensor_scan(
                        oo, d0, d1, 0.0, op0=ALU.mult, op1=ALU.add),
                        reads=[B("PW1"), B("PW2")], writes=[B("PW0")])
                if debug_spills and c == 0 and cb == 0:
                    kb.dma("qg", dbg["dbg_dec"], SB_[:, 0:4104], reads=[B("PW1")], writes=[hbuf["out"][0]])
                    kb.dma("qg", dbg["dbg_drv"], SC[:, 0:4104], reads=[B("PW2")], writes=[hbuf["out"][0]])
                    kb.dma("qg", dbg["dbg_st"], SA[:, 0:4104], reads=[B("PW0")], writes=[hbuf["out"][0]])
                cview = sb["PH"][:, 0:4096].rearrange("p (s t) -> p t s", s=8)
                kb.op("vector", lambda e, cview=cview: e.tensor_tensor(
                    SC[:, 8:4104].rearrange("p (t s) -> p t s", s=8),
                    SA[:, 8:4104].rearrange("p (t s) -> p t s", s=8),
                    cview, op=ALU.mult),
                    reads=[B("PW0"), B("PH")], writes=[B("PW2")])
                kb.op("vector", lambda e, cb=cb: e.tensor_reduce(
                    A("tmpA"),
                    SC[:, 8:4104].rearrange("p (t s) -> p t s", s=8),
                    mybir.AxisListType.X, ALU.add, negate=True),
                    reads=[B("PW2")], writes=[B("tmpA")])
                kb.op("scalar", lambda e, cb=cb: e.copy(
                    sb["carry"][:, cb * 8:cb * 8 + 8], SA[:, 8 + 511 * 8: 8 + 512 * 8]),
                    reads=[B("PW0")], writes=[B("carry")])
                kb.op("vector", lambda e, cb=cb: e.scalar_tensor_tensor(
                    sb["PP2"][:, 2048 + cb * 512:2048 + (cb + 1) * 512],
                    sb["PP1"][:, cb * 512:(cb + 1) * 512], sb["d_col"][:, cb:cb + 1],
                    A("tmpA"), op0=ALU.mult, op1=ALU.add),
                    reads=[B("PP1"), B("d_col"), B("tmpA")], writes=[B("PP2")])
            for ec in range(NDC):
                wb = "WDB0" if ec % 2 == 0 else "WDB1"
                half = (ec // 2) % 2
                wsl = sb[wb][:, half * 1024:(half + 1) * 1024]
                kb.dma("qw", wsl, dr["w_out"][:, ec * 1024:(ec + 1) * 1024],
                       reads=[wbuf["w_out"]], writes=[B(wb)])
                pk = ec % 2
                for dc in range(NDC):
                    if dc < 4:
                        rhs = sb["PP2"][:, 2048 + dc * 512:2048 + (dc + 1) * 512]
                        rb = B("PP2")
                    else:
                        rhs = sb["PP1"][:, 2048 + (dc - 4) * 512:2048 + (dc - 3) * 512]
                        rb = B("PP1")
                    kb.op("tensor", lambda e, wsl=wsl, dc=dc, rhs=rhs, pk=pk: e.matmul(
                        ps[pk][:, :], wsl[:, dc * 128:(dc + 1) * 128], rhs,
                        start=(dc == 0), stop=(dc == NDC - 1)),
                        reads=[B(wb), rb], writes=[psB[pk]])
                kb.op("scalar", lambda e, ec=ec, pk=pk: e.copy(
                    sb["PO"][:, ec * 512:(ec + 1) * 512], ps[pk][:, :]),
                    reads=[psB[pk]], writes=[B("PO")])
            kb.dma("qg", scr["attnb"][:, t0:t0 + T].rearrange("(dc p) t -> p dc t", p=128),
                   sb["PO"][:, 0:NDC * T], reads=[B("PO")], writes=[hbuf["attnb"][c]])
            if debug_spills:
                kb.dma("qg", dbg["dbg_xz"][:, t0:t0 + T].rearrange("(dc p) t -> p dc t", p=128),
                       sb["PP1"][:, 0:4096], reads=[B("PP1")], writes=[hbuf["out"][c]])
                kb.dma("qg", dbg["dbg_dy"][:, t0:t0 + T].rearrange("(dc p) t -> p dc t", p=128),
                       sb["PP2"][:, 0:4096], reads=[B("PP2")], writes=[hbuf["out"][c]])
                kb.dma("qg", dbg["dbg_h"][:, t0:t0 + T].rearrange("(dc p) t -> p dc t", p=128),
                       sb["PH"][:].rearrange("p (dc j) -> p dc j", dc=NDC)[:, :, 1:513],
                       reads=[B("PH")], writes=[hbuf["out"][c]])
                kb.dma("qg", dbg["dbg_bc"][:, t0:t0 + T].rearrange("(dc p) t -> p dc t", p=128),
                       sb["PP3"][:, 0:4096], reads=[B("PP3")], writes=[hbuf["out"][c]])

        def s3_prep(l):
            for q in range(3):
                kb.dma("qw", sb[f"PW{q}"][:, 0:4096], dr[f"wqkv{l}"][:, q * 4096:(q + 1) * 4096],
                       reads=[wbuf[f"wqkv{l}"]], writes=[B(f"PW{q}")])
            kb.dma("qw", sb["PW3"][:, 0:4096], dr[f"wo{l}"],
                   reads=[wbuf[f"wo{l}"]], writes=[B("PW3")])

        def _emit_pv(item, kt0):
            kt, vA, vB, eA, eB = item
            kb.op("tensor", lambda e, vA=vA, eA=eA, kt=kt: e.matmul(
                ps[4][0:65, :], vA, eA, start=(kt == kt0), stop=(kt == 7)),
                reads=[B("PP3"), B("PO")], writes=[psB[4]])
            kb.op("tensor", lambda e, vB=vB, eB=eB, kt=kt: e.matmul(
                ps[5][0:65, :], vB, eB, start=(kt == kt0), stop=(kt == 7)),
                reads=[B("PP3"), B("PO")], writes=[psB[5]])

        def wqkv_tile(es, dc):
            q = es // 4
            off = (es % 4) * 1024 + dc * 128
            return sb[f"PW{q}"][:, off:off + 128], B(f"PW{q}")

        def s3_chunk(l, c, src, dst):
            t0 = c * T
            PX, PH, PO = sb["PX"], sb["PH"], sb["PO"]
            qsn, ksn = f"qs{l}", f"ks{l}"
            vb_cur = (c % 2) * 2080
            vb_prv = ((c + 1) % 2) * 2080
            kp_cur = "PP1" if c % 2 == 0 else "PP2"
            kp_prv = "PP2" if c % 2 == 0 else "PP1"
            lo = max(t0 - 1, 0)
            ncol = t0 + T - lo
            coff = 1 if c == 0 else 0
            rd = [hbuf[src][c]] + ([hbuf[src][c - 1]] if c > 0 else [])
            kb.dma("qw", PX[:].rearrange("p (dc j) -> p dc j", dc=NDC)[:, :, coff:coff + ncol],
                   scr[src][:, lo:t0 + T].rearrange("(dc p) t -> p dc t", p=128),
                   reads=rd, writes=[B("PX")])
            if c == 0:
                kb.op("vector", lambda e: e.memset(
                    PX[:].rearrange("p (dc j) -> p dc j", dc=NDC)[:, :, 0:1], 0.0),
                    reads=[], writes=[B("PX")])
            wk = {"wkcsq": sb["PP1"][:, 2048:2560], "wksnq": sb["PP1"][:, 2560:3072],
                  "wkcsk": sb["PP1"][:, 3072:3584], "wksnk": sb["PP1"][:, 3584:4096]}
            for (nm, ri) in (("wkcsq", 0), ("wksnq", 64), ("wkcsk", 128), ("wksnk", 192)):
                kb.dma("qw", wk[nm][0:64, :], rotm[ri:ri + 64, t0:t0 + T],
                       reads=[rotmB[ri // 64]], writes=[B("PP1")])
                kb.dma("qw", wk[nm][64:128, :], rotm[ri:ri + 64, t0:t0 + T],
                       reads=[rotmB[ri // 64]], writes=[B("PP1")])

            def xsl(dc, j0, width):
                base = dc * 520 + (j0 + (1 if dc < 4 else 0))
                return PX[:, base:base + width]

            for es in range(8):
                pk = es % 2
                for dc in range(NDC):
                    w, wB = wqkv_tile(es, dc)
                    kb.op("tensor", lambda e, w=w, dc=dc, pk=pk: e.matmul(
                        ps[pk][:, :], w, xsl(dc, 0, 512), start=(dc == 0), stop=(dc == NDC - 1)),
                        reads=[wB, B("PX")], writes=[psB[pk]])
                kb.op("scalar", lambda e, es=es, pk=pk: e.copy(PH[:, es * 512:(es + 1) * 512], ps[pk][:, :]),
                      reads=[psB[pk]], writes=[B("PH")])
            for tt in range(4):
                pk = 4 + tt % 2
                first = True
                for dc in range(NDC):
                    for esv in range(4):
                        w, wB = wqkv_tile(8 + esv, dc)
                        kb.op("tensor", lambda e, w=w, dc=dc, tt=tt, esv=esv, pk=pk, first=first: e.matmul(
                            ps[pk][:, esv * 128:(esv + 1) * 128],
                            xsl(dc, tt * 128, 128), w,
                            start=first, stop=(dc == NDC - 1 and esv == 3)),
                            reads=[wB, B("PX")], writes=[psB[pk]])
                        first = False
                dstv = sb["PP3"][:, vb_cur + tt * 520: vb_cur + tt * 520 + 520]
                kb.op("scalar", lambda e, dstv=dstv, pk=pk: e.copy(
                    dstv.rearrange("p (h d) -> p h d", d=65)[:, :, 0:64], ps[pk][:, :]),
                    reads=[psB[pk]], writes=[B("PP3")])
            kb.op("vector", lambda e: e.memset(
                sb["PP3"][:, vb_cur:vb_cur + 2080].rearrange("p (tt h d) -> p tt h d", h=8, d=65)[:, :, :, 64:65], 1.0),
                reads=[], writes=[B("PP3")])
            for es in range(8):
                phs = PH[:, es * 512:(es + 1) * 512]
                kb.op("scalar", lambda e, phs=phs: e.activation(A("tmpA"), phs, AF.Square),
                      reads=[B("PH")], writes=[B("tmpA")])
                kb.op("tensor", lambda e: e.matmul(ps[6][0:2, :], A("ones2"), A("tmpA"),
                                                   start=True, stop=True),
                      reads=[B("ones2"), B("tmpA")], writes=[psB[6]])
                r = sb["rows"]
                kb.op("vector", lambda e: e.tensor_scalar(r[0:2, :], ps[6][0:2, :], 1e-24, None, op0=ALU.max),
                      reads=[psB[6]], writes=[B("rows")])
                kb.op("scalar", lambda e: e.activation(r[0:2, :], r[0:2, :], AF.Ln),
                      reads=[B("rows")], writes=[B("rows")])
                kb.op("scalar", lambda e: e.activation(r[0:2, :], r[0:2, :], AF.Exp, scale=-0.5),
                      reads=[B("rows")], writes=[B("rows")])
                kb.op("tensor", lambda e: e.matmul(ps[6][:, :], A("selrn"), r[:],
                                                   start=True, stop=True),
                      reads=[B("selrn"), B("rows")], writes=[psB[6]])
                scn = qsn if es < 4 else ksn
                kb.op("vector", lambda e, phs=phs, scn=scn: e.scalar_tensor_tensor(
                    phs, phs, sb[scn][:, 0:1], ps[6][:, :], op0=ALU.mult, op1=ALU.mult),
                    reads=[B("PH"), B(scn), psB[6]], writes=[B("PH")])
                cs = wk["wkcsq"] if es < 4 else wk["wkcsk"]
                sn = wk["wksnq"] if es < 4 else wk["wksnk"]
                kb.op("tensor", lambda e, phs=phs: e.matmul(ps[7][:, :], A("perm"), phs,
                                                            start=True, stop=True),
                      reads=[B("perm"), B("PH")], writes=[psB[7]])
                kb.op("vector", lambda e, sn=sn: e.tensor_tensor(A("tmpA"), ps[7][:, :], sn, op=ALU.mult),
                      reads=[psB[7], B("PP1")], writes=[B("tmpA")])
                kb.op("vector", lambda e, phs=phs, cs=cs: e.tensor_tensor(A("tmpB"), phs, cs, op=ALU.mult),
                      reads=[B("PH"), B("PP1")], writes=[B("tmpB")])
                kb.op("vector", lambda e, phs=phs: e.tensor_tensor(phs, A("tmpA"), A("tmpB"), op=ALU.add),
                      reads=[B("tmpA"), B("tmpB")], writes=[B("PH")])
            kt0 = 4 if c == 0 else 0
            for hp in range(4):
                qes = PH[:, hp * 512:(hp + 1) * 512]
                pvq = []
                for kt in range(kt0, 8):
                    if kt < 4:
                        kbase, kB = sb[kp_prv][:, kt * 512:], B(kp_prv)
                        ksA = kbase[0:64, hp * 0:0 + 128] if False else sb[kp_prv][0:64, hp * 512 + kt * 128: hp * 512 + kt * 128 + 128]
                        ksB = sb[kp_prv][64:128, hp * 512 + kt * 128: hp * 512 + kt * 128 + 128]
                        vsrc = vb_prv + (kt % 4) * 520
                    else:
                        ksA = PH[0:64, (4 + hp) * 512 + (kt - 4) * 128: (4 + hp) * 512 + (kt - 4) * 128 + 128]
                        ksB = PH[64:128, (4 + hp) * 512 + (kt - 4) * 128: (4 + hp) * 512 + (kt - 4) * 128 + 128]
                        kB = B("PH")
                        vsrc = vb_cur + (kt - 4) * 520
                    bkA, bkB = ps[kt % 2], ps[2 + kt % 2]
                    bkAB_, bkBB_ = psB[kt % 2], psB[2 + kt % 2]
                    kb.op("tensor", lambda e, ksA=ksA, qes=qes, bkA=bkA: e.matmul(
                        bkA[:, :], ksA, qes[0:64, :], start=True, stop=True, tile_position=(0, 0)),
                        reads=[kB, B("PH")], writes=[bkAB_])
                    kb.op("tensor", lambda e, ksB=ksB, qes=qes, bkB=bkB: e.matmul(
                        bkB[:, :], ksB, qes[64:128, :], start=True, stop=True, tile_position=(64, 0)),
                        reads=[kB, B("PH")], writes=[bkBB_])
                    if kt >= 4:
                        qt = kt - 4
                        wdt = (qt + 1) * 128
                        msl = sb["mask_cur"][:, 512 - wdt:] if False else None
                        for bk_, bkb_ in ((bkA, bkAB_), (bkB, bkBB_)):
                            kb.op("vector", lambda e, bk_=bk_, wdt=wdt: e.tensor_tensor(
                                bk_[:, 0:wdt], bk_[:, 0:wdt], sb["mask_cur"][:, 512 - wdt:], op=ALU.add),
                                reads=[bkb_, B("mask_cur")], writes=[bkb_])
                    else:
                        wdt = 512 - kt * 128
                        for bk_, bkb_ in ((bkA, bkAB_), (bkB, bkBB_)):
                            kb.op("vector", lambda e, bk_=bk_, kt=kt, wdt=wdt: e.tensor_tensor(
                                bk_[:, kt * 128:512], bk_[:, kt * 128:512], sb["mask_prev"][:, 0:wdt], op=ALU.add),
                                reads=[bkb_, B("mask_prev")], writes=[bkb_])
                    eA = PO[:, (kt % 2) * 512:(kt % 2) * 512 + 512]
                    eB = PO[:, 1024 + (kt % 2) * 512:1024 + (kt % 2) * 512 + 512]
                    kb.op("scalar", lambda e, bkA=bkA, eA=eA: e.activation(eA, bkA[:, :], AF.Exp, scale=8.0),
                          reads=[bkAB_], writes=[B("PO")])
                    kb.op("scalar", lambda e, bkB=bkB, eB=eB: e.activation(eB, bkB[:, :], AF.Exp, scale=8.0),
                          reads=[bkBB_], writes=[B("PO")])
                    vA = sb["PP3"][:, vsrc:vsrc + 520].rearrange("p (h d) -> p h d", d=65)[:, 2 * hp, :]
                    vB = sb["PP3"][:, vsrc:vsrc + 520].rearrange("p (h d) -> p h d", d=65)[:, 2 * hp + 1, :]
                    pvq.append((kt, vA, vB, eA, eB))
                    if len(pvq) >= 2:
                        _emit_pv(pvq.pop(0), kt0)
                while pvq:
                    _emit_pv(pvq.pop(0), kt0)
                r = sb["rows"]
                for (bank, bb_, rrow, seln, bcn, orow) in (
                        (ps[4], psB[4], r[64:65, :], "sel64", "bc0", 0),
                        (ps[5], psB[5], r[96:97, :], "sel96", "bc1", 64)):
                    kb.op("scalar", lambda e, bank=bank, rrow=rrow: e.activation(rrow, bank[64:65, :], AF.Ln),
                          reads=[bb_], writes=[B("rows")])
                    kb.op("scalar", lambda e, rrow=rrow: e.activation(rrow, rrow, AF.Exp, scale=-1.0),
                          reads=[B("rows")], writes=[B("rows")])
                    kb.op("tensor", lambda e, seln=seln: e.matmul(
                        ps[6][0:64, :], sb[seln][:, 0:64], r[:], start=True, stop=True),
                        reads=[B(seln), B("rows")], writes=[psB[6]])
                    kb.op("scalar", lambda e, bcn=bcn: e.copy(sb[bcn][0:64, :], ps[6][0:64, :]),
                          reads=[psB[6]], writes=[B(bcn)])
                    kb.op("vector", lambda e, bank=bank, bcn=bcn, hp=hp, orow=orow: e.tensor_tensor(
                        sb["PP2"][orow:orow + 64, 2048 + hp * 512:2048 + (hp + 1) * 512],
                        bank[0:64, :], sb[bcn][0:64, :], op=ALU.mult),
                        reads=[bb_, B(bcn)], writes=[B("PP2")])
            kb.op("vector", lambda e: e.tensor_copy(sb[kp_cur][:, 0:2048], PH[:, 2048:4096]),
                  reads=[B("PH")], writes=[B(kp_cur)])
            for ec in range(NDC):
                pk = ec % 2
                for cc in range(4):
                    kb.op("tensor", lambda e, ec=ec, cc=cc, pk=pk: e.matmul(
                        ps[pk][:, :], sb["PW3"][:, (ec * 4 + cc) * 128:(ec * 4 + cc) * 128 + 128],
                        sb["PP2"][:, 2048 + cc * 512:2048 + (cc + 1) * 512],
                        start=(cc == 0), stop=(cc == 3)),
                        reads=[B("PW3"), B("PP2")], writes=[psB[pk]])
                kb.op("vector", lambda e, ec=ec, pk=pk: e.tensor_tensor(
                    PO[:, ec * 512:(ec + 1) * 512], ps[pk][:, :], xsl(ec if False else ec, 0, 512) if False else PX[:, ec * 520 + 1:ec * 520 + 513],
                    op=ALU.add),
                    reads=[psB[pk], B("PX")], writes=[B("PO")])
            kb.dma("qg", scr[dst][:, t0:t0 + T].rearrange("(dc p) t -> p dc t", p=128),
                   PO[:, 0:NDC * T], reads=[B("PO")], writes=[hbuf[dst][c]])

        def s4_chunk(l, c, src, dst, final=False):
            t0 = c * T
            PX, PH, PO = sb["PX"], sb["PH"], sb["PO"]
            lo = max(t0 - 1, 0)
            ncol = t0 + T - lo
            coff = 1 if c == 0 else 0
            rd = [hbuf[src][c]] + ([hbuf[src][c - 1]] if c > 0 else [])
            kb.dma("qw", PX[:].rearrange("p (dc j) -> p dc j", dc=NDC)[:, :, coff:coff + ncol],
                   scr[src][:, lo:t0 + T].rearrange("(dc p) t -> p dc t", p=128),
                   reads=rd, writes=[B("PX")])
            if c == 0:
                kb.op("vector", lambda e: e.memset(
                    PX[:].rearrange("p (dc j) -> p dc j", dc=NDC)[:, :, 0:1], 0.0),
                    reads=[], writes=[B("PX")])

            def xsh(dc, j0=0, width=512):
                base = dc * 520 + (j0 + (1 if dc < 4 else 0))
                return PX[:, base:base + width]

            for dc in range(NDC):
                kb.op("tensor", lambda e, dc=dc: e.matmul(
                    ps[6][0:1, :], A("ones1"), xsh(dc), start=(dc == 0), stop=(dc == NDC - 1)),
                    reads=[B("ones1"), B("PX")], writes=[psB[6]])
            for dc in range(NDC):
                kb.op("scalar", lambda e, dc=dc: e.activation(A("tmpA"), xsh(dc), AF.Square),
                      reads=[B("PX")], writes=[B("tmpA")])
                kb.op("tensor", lambda e, dc=dc: e.matmul(
                    ps[7][0:1, :], A("ones1"), A("tmpA"), start=(dc == 0), stop=(dc == NDC - 1)),
                    reads=[B("ones1"), B("tmpA")], writes=[psB[7]])
            r = sb["rows"]
            rm, rm2, rv, ri = r[0:1, :], r[32:33, :], r[64:65, :], r[96:97, :]
            kb.op("vector", lambda e: e.tensor_scalar(rm, ps[6][0:1, :], 1.0 / D, None, op0=ALU.mult),
                  reads=[psB[6]], writes=[B("rows")])
            kb.op("vector", lambda e: e.tensor_tensor(rm2, rm, rm, op=ALU.mult),
                  reads=[B("rows")], writes=[B("rows")])
            kb.op("vector", lambda e: e.scalar_tensor_tensor(rv, ps[7][0:1, :], 1.0 / D, rm2,
                                                             op0=ALU.mult, op1=ALU.subtract),
                  reads=[psB[7], B("rows")], writes=[B("rows")])
            kb.op("vector", lambda e: e.tensor_scalar(rv, rv, 1e-5, None, op0=ALU.add),
                  reads=[B("rows")], writes=[B("rows")])
            kb.op("scalar", lambda e: e.activation(ri, rv, AF.Ln),
                  reads=[B("rows")], writes=[B("rows")])
            kb.op("scalar", lambda e: e.activation(ri, ri, AF.Exp, scale=-0.5),
                  reads=[B("rows")], writes=[B("rows")])
            kb.op("tensor", lambda e: e.matmul(ps[2][:, :], A("sel0"), r[:], start=True, stop=True),
                  reads=[B("sel0"), B("rows")], writes=[psB[2]])
            kb.op("tensor", lambda e: e.matmul(ps[3][:, :], A("sel96"), r[:], start=True, stop=True),
                  reads=[B("sel96"), B("rows")], writes=[psB[3]])
            for dc in range(NDC):
                kb.op("vector", lambda e, dc=dc: e.tensor_tensor(A("tmpA"), xsh(dc), ps[2][:, :], op=ALU.subtract),
                      reads=[B("PX"), psB[2]], writes=[B("tmpA")])
                kb.op("vector", lambda e: e.tensor_tensor(A("tmpB"), A("tmpA"), ps[3][:, :], op=ALU.mult),
                      reads=[B("tmpA"), psB[3]], writes=[B("tmpB")])
                kb.op("vector", lambda e, dc=dc: e.tensor_scalar(
                    PH[:, dc * 512:(dc + 1) * 512], A("tmpB"),
                    sb[f"lg{l}"][:, dc:dc + 1], sb[f"lb{l}"][:, dc:dc + 1],
                    op0=ALU.mult, op1=ALU.add),
                    reads=[B("tmpB"), B(f"lg{l}"), B(f"lb{l}")], writes=[B("PH")])

            def pslot(jc):
                if jc < 8:
                    return sb["PP1"][:, jc * 512:(jc + 1) * 512], B("PP1")
                if jc < 16:
                    return sb["PP2"][:, (jc - 8) * 512:(jc - 7) * 512], B("PP2")
                return sb["PP3"][:, (jc - 16) * 512:(jc - 15) * 512], B("PP3")

            for jc in range(NJ):
                wn = "PW0" if jc % 2 == 0 else "PW1"
                pa = 0 + (jc % 2) * 4
                pg = 1 + (jc % 2) * 4
                gl = "tmpA" if jc % 2 == 0 else "tmpB"
                kb.dma("qw", sb[wn][:, 0:2048], dr[f"w1_{l}"][:, jc * 2048:(jc + 1) * 2048],
                       reads=[wbuf[f"w1_{l}"]], writes=[B(wn)])
                for dc in range(NDC):
                    kb.op("tensor", lambda e, wn=wn, dc=dc, pa=pa: e.matmul(
                        ps[pa][:, :], sb[wn][:, dc * 256:dc * 256 + 128],
                        PH[:, dc * 512:(dc + 1) * 512], start=(dc == 0), stop=(dc == NDC - 1)),
                        reads=[B(wn), B("PH")], writes=[psB[pa]])
                for dc in range(NDC):
                    kb.op("tensor", lambda e, wn=wn, dc=dc, pg=pg: e.matmul(
                        ps[pg][:, :], sb[wn][:, dc * 256 + 128:dc * 256 + 256],
                        PH[:, dc * 512:(dc + 1) * 512], start=(dc == 0), stop=(dc == NDC - 1)),
                        reads=[B(wn), B("PH")], writes=[psB[pg]])
                kb.op("scalar", lambda e, jc=jc, pg=pg, gl=gl: e.activation(
                    A(gl), ps[pg][:, :], AF.Gelu, bias=sb[f"b1g{l}"][:, jc:jc + 1]),
                    reads=[psB[pg], B(f"b1g{l}")], writes=[B(gl)])
                pdst, pB = pslot(jc)
                kb.op("vector", lambda e, jc=jc, pdst=pdst, pa=pa, gl=gl: e.scalar_tensor_tensor(
                    pdst, ps[pa][:, :], sb[f"b1a{l}"][:, jc:jc + 1], A(gl),
                    op0=ALU.add, op1=ALU.mult),
                    reads=[psB[pa], B(f"b1a{l}"), B(gl)], writes=[pB])
            if final:
                kb.dma("qw", sb["WDB0"][:, 0:2048],
                       scr["attnb"][:, t0:t0 + T].rearrange("(dc p) t -> p dc t", p=128)[:, 0:4, :],
                       reads=[hbuf["attnb"][c]], writes=[B("WDB0")])
                kb.dma("qw", sb["WDB1"][:, 0:2048],
                       scr["attnb"][:, t0:t0 + T].rearrange("(dc p) t -> p dc t", p=128)[:, 4:8, :],
                       reads=[hbuf["attnb"][c]], writes=[B("WDB1")])
                kb.dma("qw", PH[:, 0:4096],
                       dr["x"][:, t0:t0 + T].rearrange("(dc p) t -> p dc t", p=128),
                       reads=[hbuf["x"][c]], writes=[B("PH")])
            for jc in range(NJ):
                wn = "PW2" if jc % 2 == 0 else "PW3"
                kb.dma("qw", sb[wn][:, 0:1024], dr[f"w2_{l}"][:, jc * 1024:(jc + 1) * 1024],
                       reads=[wbuf[f"w2_{l}"]], writes=[B(wn)])
                psrc, pB = pslot(jc)
                for ec in range(NDC):
                    kb.op("tensor", lambda e, wn=wn, ec=ec, psrc=psrc, jc=jc: e.matmul(
                        ps[ec][:, :], sb[wn][:, ec * 128:(ec + 1) * 128], psrc,
                        start=(jc == 0), stop=(jc == NJ - 1)),
                        reads=[B(wn), pB], writes=[psB[ec]])
            for ec in range(NDC):
                if not final:
                    kb.op("vector", lambda e, ec=ec: e.scalar_tensor_tensor(
                        PO[:, ec * 512:(ec + 1) * 512], ps[ec][:, :],
                        sb[f"b2_{l}"][:, ec:ec + 1], PX[:, ec * 520 + 1:ec * 520 + 513],
                        op0=ALU.add, op1=ALU.add),
                        reads=[psB[ec], B(f"b2_{l}"), B("PX")], writes=[B("PO")])
                else:
                    kb.op("vector", lambda e, ec=ec: e.scalar_tensor_tensor(
                        A("tmpA"), ps[ec][:, :],
                        sb[f"b2_{l}"][:, ec:ec + 1], PX[:, ec * 520 + 1:ec * 520 + 513],
                        op0=ALU.add, op1=ALU.add),
                        reads=[psB[ec], B(f"b2_{l}"), B("PX")], writes=[B("tmpA")])
                    at = (sb["WDB0"] if ec < 4 else sb["WDB1"])[:, (ec % 4) * 512:(ec % 4 + 1) * 512]
                    atB = B("WDB0") if ec < 4 else B("WDB1")
                    kb.op("vector", lambda e, at=at: e.tensor_tensor(A("tmpB"), A("tmpA"), at, op=ALU.add),
                          reads=[B("tmpA"), atB], writes=[B("tmpB")])
                    kb.op("vector", lambda e, ec=ec: e.tensor_tensor(
                        PO[:, ec * 512:(ec + 1) * 512], A("tmpB"), PH[:, ec * 512:(ec + 1) * 512],
                        op=ALU.add),
                        reads=[B("tmpB"), B("PH")], writes=[B("PO")])
            tgt = out_h if final else scr[dst]
            tb = hbuf["out"][c] if final else hbuf[dst][c]
            kb.dma("qg", tgt[:, t0:t0 + T].rearrange("(dc p) t -> p dc t", p=128),
                   PO[:, 0:NDC * T], reads=[B("PO")], writes=[tb])

        if stages >= 1:
            for c in range(NCH):
                s1_chunk(c)
        if stages >= 2:
            s2_prep()
            for c in range(NCH):
                s2_chunk(c)
        if stages >= 3:
            s3_prep(0)
            for c in range(NCH):
                s3_chunk(0, c, "xn", "xa")
        if stages >= 4:
            for c in range(NCH):
                s4_chunk(0, c, "xa", "xb")
        if stages >= 5:
            s3_prep(1)
            for c in range(NCH):
                s3_chunk(1, c, "xb", "xn")
        if stages >= 6:
            for c in range(NCH):
                s4_chunk(1, c, "xn", None, final=True)

        final_bufs = []
        if debug_spills:
            for n in ("xn", "attnb", "xa", "xb"):
                final_bufs += hbuf[n]
        final_bufs += hbuf["out"]
        final_bufs = [b for b in final_bufs if b.writer is not None]
        if final_bufs:
            kb.op("sync", lambda e: e.nop(), reads=final_bufs, writes=())

        kb.finalize_and_emit(sems)
    return nc


def kernel(**inputs):
    g = host_prep(inputs)
    x = np.asarray(inputs["L0_attn_x"], dtype=np.float32)
    nc = build_program(debug_spills=CFG["debug_spills"], stages=CFG["stages"])
    in_maps = []
    for core in range(8):
        m = dict(g)
        m["x"] = np.ascontiguousarray(x[core].T)
        in_maps.append(m)
    res = run_bass_kernel_spmd(nc, in_maps, core_ids=list(range(8)))
    outs = [r["out"] for r in res.results]
    return np.ascontiguousarray(np.stack(outs, 0).transpose(0, 2, 1))


# version 03
# speedup vs baseline: 1.1097x; 1.0138x over previous
import math
import numpy as np
import concourse.bass as bass
import concourse.mybir as mybir
from concourse.bass_utils import run_bass_kernel_spmd

F32 = mybir.dt.float32
AF = mybir.ActivationFunctionType
ALU = mybir.AluOpType

SEQ = 4096
D = 1024
T = 512
NCH = SEQ // T
NDC = D // 128
INNER = 512
DS = 8
NH = 8
DH = 64
HID = 2730
NJ = 22
HIDP = NJ * 128
NEG = -1.0e30
TWO_PI = 2.0 * math.pi

CFG = {"stages": 6, "debug_spills": False, "serialize": False}



def _tile_lhsT_nmajor(m, kc, ncnt, nw=128):
    K, N = m.shape
    assert K == kc * 128
    pad = ncnt * nw - N
    if pad:
        m = np.concatenate([m, np.zeros((K, pad), m.dtype)], axis=1)
    return np.ascontiguousarray(
        m.reshape(kc, 128, ncnt, nw).transpose(1, 2, 0, 3).reshape(128, -1))


def _cols(v, nt):
    return np.ascontiguousarray(v.reshape(nt, 128).T)


def host_prep(inputs):
    f = np.float32
    g = {}
    g["w_in"] = _tile_lhsT_nmajor(np.ascontiguousarray(inputs["L0_attn_W_in"].astype(f).T), 8, 8)
    g["w_out"] = _tile_lhsT_nmajor(np.ascontiguousarray(inputs["L0_attn_W_out"].astype(f).T), 8, 8)
    g["w_xp"] = _tile_lhsT_nmajor(np.ascontiguousarray(inputs["L0_attn_W_xproj"].astype(f).T), 4, 1, nw=80)
    g["w_dt"] = np.ascontiguousarray(inputs["L0_attn_W_dt"].astype(f).T)
    g["b_dt"] = _cols(inputs["L0_attn_b_dt"].astype(f), 4)
    g["d_col"] = _cols(inputs["L0_attn_D"].astype(f), 4)
    al = inputs["L0_attn_A_log"].astype(f)
    g["a_cols"] = np.ascontiguousarray(al.reshape(4, 128, DS).transpose(1, 0, 2).reshape(128, 4 * DS))
    g["kx"] = np.ascontiguousarray(inputs["L0_attn_Kx"][:, 0, :].astype(f).reshape(4, 128, 3).transpose(1, 0, 2).reshape(128, 12))
    g["kz"] = np.ascontiguousarray(inputs["L0_attn_Kz"][:, 0, :].astype(f).reshape(4, 128, 3).transpose(1, 0, 2).reshape(128, 12))
    for l in range(2):
        g[f"wqkv{l}"] = _tile_lhsT_nmajor(inputs["L0_ffn_wqkv"][l].astype(f), 8, 12)
        g[f"wo{l}"] = _tile_lhsT_nmajor(inputs["L0_ffn_wo"][l].astype(f), 4, 8)
        w1 = inputs["L0_ffn_w1"][l].astype(f)
        blocks = []
        for jc in range(NJ):
            a = w1[:, jc * 128:min((jc + 1) * 128, HID)]
            gg = w1[:, HID + jc * 128: HID + min((jc + 1) * 128, HID)]
            if a.shape[1] < 128:
                a = np.concatenate([a, np.zeros((D, 128 - a.shape[1]), f)], 1)
                gg = np.concatenate([gg, np.zeros((D, 128 - gg.shape[1]), f)], 1)
            blk = np.concatenate([a, gg], 1)
            blocks.append(blk.reshape(8, 128, 256).transpose(1, 0, 2).reshape(128, 2048))
        g[f"w1_{l}"] = np.ascontiguousarray(np.concatenate(blocks, 1))
        w2 = inputs["L0_ffn_w2"][l].astype(f)
        w2p = np.concatenate([w2, np.zeros((HIDP - HID, D), f)], 0)
        w2r = w2p.reshape(NJ, 128, 8, 128).transpose(1, 0, 2, 3).reshape(128, -1)
        g[f"w2_{l}"] = np.ascontiguousarray(w2r)
        b1 = inputs["L0_ffn_b1"][l].astype(f)
        g[f"b1a{l}"] = _cols(np.concatenate([b1[:HID], np.zeros(HIDP - HID, f)]), NJ)
        g[f"b1g{l}"] = _cols(np.concatenate([b1[HID:], np.zeros(HIDP - HID, f)]), NJ)
        g[f"b2_{l}"] = _cols(inputs["L0_ffn_b2"][l].astype(f), 8)
        g[f"lg{l}"] = _cols(inputs["L0_ffn_ln_g"][l].astype(f), 8)
        g[f"lb{l}"] = _cols(inputs["L0_ffn_ln_b"][l].astype(f), 8)
        g[f"qs{l}"] = np.ascontiguousarray(np.tile(inputs["L0_ffn_q_scale"][l].astype(f), 2)[:, None])
        g[f"ks{l}"] = np.ascontiguousarray(np.tile(inputs["L0_ffn_k_scale"][l].astype(f), 2)[:, None])
    g["ones1"] = np.ones((128, 1), f)
    o2 = np.zeros((128, 2), f)
    o2[:64, 0] = 1.0
    o2[64:, 1] = 1.0
    g["ones2"] = o2
    perm = np.zeros((128, 128), f)
    for i in range(128):
        perm[i ^ 32, i] = 1.0
    g["perm"] = perm
    mc = np.zeros((128, 128), f)
    mp = np.zeros((128, 128), f)
    for k in range(128):
        for q in range(128):
            if k > q:
                mc[k, q] = NEG
            if k < q:
                mp[k, q] = NEG
    g["mask_cur"] = np.concatenate([np.full((128, 384), NEG, f), mc], 1)
    g["mask_prev"] = np.concatenate([mp, np.full((128, 384), NEG, f)], 1)
    g["trow"] = np.arange(SEQ, dtype=f)[None, :]
    g["hrow"] = np.tile(np.arange(0, DH, 2, dtype=f), 2)[None, :]
    sel16 = np.zeros((16, 16 * 128), f)
    for sv in range(16):
        sel16[sv, sv * 128:(sv + 1) * 128] = 1.0
    g["sel16"] = sel16
    for nm, row in (("sel0", 0), ("sel64", 64), ("sel96", 96)):
        m = np.zeros((128, 128), f)
        m[row, :] = 1.0
        g[nm] = m
    selrn = np.zeros((128, 128), f)
    selrn[0, 0:64] = 1.0
    selrn[1, 64:128] = 1.0
    g["selrn"] = selrn
    return g


IN_SHAPES = {
    "x": (D, SEQ),
    "w_in": (128, 8192), "w_out": (128, 8192), "w_xp": (128, 320),
    "w_dt": (64, 512), "b_dt": (128, 4), "d_col": (128, 4), "a_cols": (128, 32),
    "kx": (128, 12), "kz": (128, 12), "ones1": (128, 1), "ones2": (128, 2),
    "perm": (128, 128), "mask_cur": (128, 512), "mask_prev": (128, 512),
    "trow": (1, SEQ), "hrow": (1, 64),
    "sel16": (16, 2048), "sel0": (128, 128), "sel64": (128, 128),
    "sel96": (128, 128), "selrn": (128, 128),
}
for _l in range(2):
    IN_SHAPES.update({
        f"wqkv{_l}": (128, 12288), f"wo{_l}": (128, 4096),
        f"w1_{_l}": (128, NJ * 2048),
        f"b1a{_l}": (128, NJ), f"b1g{_l}": (128, NJ), f"b2_{_l}": (128, 8),
        f"lg{_l}": (128, 8), f"lb{_l}": (128, 8), f"qs{_l}": (128, 1), f"ks{_l}": (128, 1),
        f"w2_{_l}": (128, NJ * 1024),
    })



ENGS = ("sync", "tensor", "vector", "scalar", "gpsimd")
QUEUES = {"qw": "sync", "qg": "gpsimd"}


class _Op:
    __slots__ = ("stream", "emit", "flag", "val", "waits", "isdma", "dmaidx", "snap")

    def __init__(self, stream, emit, isdma=False):
        self.stream = stream
        self.emit = emit
        self.flag = False
        self.val = None
        self.waits = []
        self.isdma = isdma
        self.dmaidx = None
        self.snap = None


class Buf:
    __slots__ = ("name", "writer", "readers")

    def __init__(self, name):
        self.name = name
        self.writer = None
        self.readers = {}


class KB:
    def __init__(self, nc, serialize=False):
        self.nc = nc
        self.streams = {e: [] for e in ENGS}
        self.dmacount = {q: 0 for q in QUEUES}
        self.allops = []
        self.serialize = serialize
        self.lastop = {}

    def buf(self, name):
        return Buf(name)

    def _add(self, stream, emit, reads, writes, isdma=False):
        eng = QUEUES[stream] if stream in QUEUES else stream
        op = _Op(stream, emit, isdma=isdma)
        op.snap = dict(self.dmacount)
        if isdma:
            op.dmaidx = self.dmacount[stream]
            self.dmacount[stream] += 1
        deps = []
        for b in reads:
            if b.writer is not None:
                deps.append(b.writer)
        for b in writes:
            if b.writer is not None:
                deps.append(b.writer)
            deps.extend(b.readers.values())
        if self.serialize:
            deps.extend(self.lastop.values())
        for (cs, ps_) in CFG.get("ser_pairs", ()):
            if stream == cs and ps_ in self.lastop:
                deps.append(self.lastop[ps_])
        seen = set()
        for d in deps:
            if id(d) in seen:
                continue
            seen.add(id(d))
            if d.stream == stream:
                continue
            op.waits.append(d)
        self.streams[eng].append(op)
        self.allops.append(op)
        self.lastop[stream] = op
        for b in reads:
            b.readers[stream] = op
        for b in writes:
            b.writer = op
            b.readers = {}
        return op

    def op(self, eng, emit, reads=(), writes=()):
        return self._add(eng, emit, reads, writes, isdma=False)

    def dma(self, q, out_ap, in_ap, reads=(), writes=()):
        def emit(eng, _o=out_ap, _i=in_ap):
            return eng.dma_start(out=_o, in_=_i)
        return self._add(q, emit, reads, writes, isdma=True)

    def finalize_and_emit(self, sems):
        for op in self.allops:
            for d in op.waits:
                if not d.isdma:
                    d.flag = True
        for e in ENGS:
            v = 0
            for op in self.streams[e]:
                if op.isdma:
                    continue
                if op.flag:
                    v += 1
                    op.val = v

        def tgt(d, consumer):
            if d.isdma:
                return (d.stream, 16 * max(d.dmaidx + 1, consumer.snap.get(d.stream, 0)))
            return (d.stream, d.val)

        emitted = {e: [] for e in ENGS}
        for e in ENGS:
            hi = {}
            for op in self.streams[e]:
                for d in op.waits:
                    s, v = tgt(d, op)
                    assert v is not None
                    if hi.get(s, 0) >= v:
                        continue
                    hi[s] = v
                    emitted[e].append(("w", s, v))
                inc = (op.stream, 16) if op.isdma else ((op.stream, 1) if op.flag else None)
                emitted[e].append(("o", op.emit, inc))
        nc = self.nc
        with nc.Block() as block:
            def mk(e):
                def run(eng):
                    for item in emitted[e]:
                        if item[0] == "w":
                            eng.wait_ge(sems[item[1]], item[2])
                        else:
                            h = item[1](eng)
                            if item[2] is not None:
                                h.then_inc(sems[item[2][0]], item[2][1])
                return run
            block.sync(mk("sync"))
            block.tensor(mk("tensor"))
            block.vector(mk("vector"))
            block.scalar(mk("scalar"))
            block.gpsimd(mk("gpsimd"))



def build_program(debug_spills=False, stages=6):
    nc = bass.Bass("TRN2", target_bir_lowering=False)
    f = F32

    dr = {n: nc.dram_tensor(n, list(s), f, kind="ExternalInput").ap()
          for n, s in IN_SHAPES.items()}
    scr = {}
    for n in ("xn", "attnb", "xa", "xb"):
        kind = {"kind": "ExternalOutput"} if debug_spills else {}
        scr[n] = nc.dram_tensor(n, [D, SEQ], f, **kind).ap()
    rotm = nc.dram_tensor("rotm", [256, SEQ], f).ap()
    w1s = {l: nc.dram_tensor(f"w1s{l}", [128, NJ * 4096], mybir.dt.bfloat16).ap()
           for l in range(2)}
    dbg = {}
    if debug_spills:
        for n in ("dbg_xz", "dbg_dy", "dbg_h", "dbg_bc"):
            dbg[n] = nc.dram_tensor(n, [1024, SEQ], f, kind="ExternalOutput").ap()
        for n in ("dbg_dec", "dbg_drv", "dbg_st"):
            dbg[n] = nc.dram_tensor(n, [128, 4104], f, kind="ExternalOutput").ap()
    out_h = nc.dram_tensor("out", [D, SEQ], f, kind="ExternalOutput").ap()

    kb = KB(nc, serialize=CFG.get("serialize", False))
    hbuf = {n: [kb.buf(f"{n}{c}") for c in range(NCH)] for n in
            ("x", "xn", "attnb", "xa", "xb", "out")}
    w1sB = {l: kb.buf(f"w1s{l}") for l in range(2)}
    rotmB = [kb.buf(f"rotm{i}") for i in range(4)]
    wbuf = {n: kb.buf("W" + n) for n in IN_SHAPES}

    import contextlib
    sb = {}
    bb = {}

    with contextlib.ExitStack() as stack:
        def alloc(name, cols, parts=128, dtype=F32):
            h = stack.enter_context(nc.sbuf_tensor("sb_" + name, [parts, cols], dtype))
            sb[name] = h
            bb[name] = kb.buf(name)
            return h

        for i in range(4):
            alloc(f"PW{i}", 4104)
        alloc("PX", 4160)
        alloc("PH", 4160)
        alloc("PO", 4104)
        alloc("PP1", 4096)
        alloc("PP2", 4096)
        alloc("PP3", 4160)
        alloc("WDB0", 2048)
        alloc("WDB1", 2048)
        for n in ("tmpA", "tmpB", "bc0", "bc1"):
            alloc(n, T)
        for n in ("sel0", "sel64", "sel96", "selrn"):
            alloc(n, 128)
        alloc("rows", T, parts=128)
        alloc("proj", T, parts=80)
        alloc("bct", T, parts=16)
        alloc("carry", 32)
        alloc("tint", T, parts=64, dtype=mybir.dt.int32)
        for n in ("ones1", "ones2", "perm", "mask_cur", "mask_prev", "b_dt",
                  "d_col", "a_cols", "kx", "kz", "w_xp", "hrow"):
            alloc(n, IN_SHAPES[n][1], parts=IN_SHAPES[n][0])
        alloc("w_dt", 512, parts=64)
        for l in range(2):
            for n in (f"b1a{l}", f"b1g{l}", f"b2_{l}", f"lg{l}", f"lb{l}", f"qs{l}", f"ks{l}"):
                alloc(n, IN_SHAPES[n][1], parts=IN_SHAPES[n][0])
        alloc("invf", 64, parts=1)
        alloc("logb", 64, parts=1)

        ps, psB = [], []
        for i in range(8):
            h = stack.enter_context(nc.psum_tensor(f"pb{i}", [128, 512], f))
            ps.append(h)
            psB.append(kb.buf(f"pb{i}"))

        sems = {}
        for e in ENGS:
            sems[e] = stack.enter_context(nc.semaphore("s_" + e))
        for q in QUEUES:
            sems[q] = stack.enter_context(nc.semaphore("s_" + q))

        A = lambda n: sb[n][:]
        B = lambda n: bb[n]

        def ld_const(n):
            kb.dma("qw", A(n), dr[n], reads=[wbuf[n]], writes=[B(n)])

        for n in ("ones1", "ones2", "perm", "mask_cur", "mask_prev", "b_dt",
                  "d_col", "a_cols", "kx", "kz", "w_xp", "w_dt", "hrow",
                  "sel0", "sel64", "sel96", "selrn"):
            ld_const(n)
        trow = sb["PP1"][0:1, 0:SEQ]
        prow = sb["PP2"][0:1, 0:SEQ]
        kb.dma("qw", trow, dr["trow"], reads=[wbuf["trow"]], writes=[B("PP1")])
        for l in range(2):
            for n in (f"b1a{l}", f"b1g{l}", f"b2_{l}", f"lg{l}", f"lb{l}", f"qs{l}", f"ks{l}"):
                ld_const(n)

        kb.op("vector", lambda e: e.tensor_scalar(A("a_cols"), A("a_cols"), 10.0, None, op0=ALU.min),
              reads=[B("a_cols")], writes=[B("a_cols")])
        kb.op("scalar", lambda e: e.activation(A("a_cols"), A("a_cols"), AF.Exp),
              reads=[B("a_cols")], writes=[B("a_cols")])
        kb.op("scalar", lambda e: e.activation(A("a_cols"), A("a_cols"), AF.Ln, bias=1.0),
              reads=[B("a_cols")], writes=[B("a_cols")])
        kb.op("vector", lambda e: e.tensor_scalar(A("a_cols"), A("a_cols"), 1e-4, None, op0=ALU.add),
              reads=[B("a_cols")], writes=[B("a_cols")])

        def s0_tables():
            inv_c = -math.log(10000.0) / DH
            kb.op("scalar", lambda e: e.activation(A("invf"), A("hrow"), AF.Exp, scale=inv_c),
                  reads=[B("hrow")], writes=[B("invf")])
            kb.op("vector", lambda e: e.tensor_scalar(A("logb"), A("hrow"), 1.0 / (1.4 * DH), 0.4 / 1.4, op0=ALU.mult, op1=ALU.add),
                  reads=[B("hrow")], writes=[B("logb")])
            kb.op("scalar", lambda e: e.activation(A("logb"), A("logb"), AF.Ln),
                  reads=[B("logb")], writes=[B("logb")])
            kb.op("vector", lambda e: e.tensor_scalar(prow, trow, 1.0 / 256.0, -8.0, op0=ALU.mult, op1=ALU.add),
                  reads=[B("PP1")], writes=[B("PP2")])
            u = sb["PX"][0:64, 0:512]
            fr = sb["PX"][0:64, 520:1032]
            sc3 = sb["PX"][0:64, 1040:1552]
            sn = sb["PH"][0:64, 0:512]
            cs = sb["PH"][0:64, 520:1032]
            scp = sb["PH"][0:64, 1040:1552]
            scm = sb["PH"][0:64, 1560:2072]
            ti = sb["tint"][0:64, :]

            def wrap(dst, src, rb, wb):
                kb.op("vector", lambda e: e.tensor_copy(ti, src), reads=[rb], writes=[B("tint")])
                kb.op("vector", lambda e: e.tensor_copy(sc3, ti), reads=[B("tint")], writes=[B("PX")])
                kb.op("vector", lambda e: e.tensor_tensor(dst, src, sc3, op=ALU.subtract),
                      reads=[rb, B("PX")], writes=[wb])
                kb.op("vector", lambda e: e.tensor_scalar(sc3, dst, 0.5, None, op0=ALU.is_ge),
                      reads=[wb], writes=[B("PX")])
                kb.op("vector", lambda e: e.tensor_tensor(dst, dst, sc3, op=ALU.subtract),
                      reads=[wb, B("PX")], writes=[wb])
                kb.op("vector", lambda e: e.tensor_scalar(sc3, dst, -0.5, None, op0=ALU.is_lt),
                      reads=[wb], writes=[B("PX")])
                kb.op("vector", lambda e: e.tensor_tensor(dst, dst, sc3, op=ALU.add),
                      reads=[wb, B("PX")], writes=[wb])

            for tc in range(8):
                sl = slice(tc * 512, (tc + 1) * 512)
                kb.op("tensor", lambda e, sl=sl: e.matmul(ps[0][0:64, :], A("invf"), trow[:, sl], start=True, stop=True),
                      reads=[B("invf"), B("PP1")], writes=[psB[0]])
                kb.op("vector", lambda e: e.tensor_scalar(u, ps[0][0:64, :], 1.0 / TWO_PI, None, op0=ALU.mult),
                      reads=[psB[0]], writes=[B("PX")])
                kb.op("tensor", lambda e, sl=sl: e.matmul(ps[1][0:64, :], A("logb"), prow[:, sl], start=True, stop=True),
                      reads=[B("logb"), B("PP2")], writes=[psB[1]])
                kb.op("scalar", lambda e: e.activation(scp, ps[1][0:64, :], AF.Exp),
                      reads=[psB[1]], writes=[B("PH")])
                kb.op("scalar", lambda e: e.activation(scm, ps[1][0:64, :], AF.Exp, scale=-1.0),
                      reads=[psB[1]], writes=[B("PH")])
                wrap(fr, u, B("PX"), B("PX"))
                kb.op("scalar", lambda e: e.activation(sn, fr, AF.Sin, scale=TWO_PI),
                      reads=[B("PX")], writes=[B("PH")])
                kb.op("vector", lambda e: e.tensor_scalar(u, fr, 0.25, None, op0=ALU.add),
                      reads=[B("PX")], writes=[B("PX")])
                wrap(fr, u, B("PX"), B("PX"))
                kb.op("scalar", lambda e: e.activation(cs, fr, AF.Sin, scale=TWO_PI),
                      reads=[B("PX")], writes=[B("PH")])
                for (ri, trg, sgn, scl) in ((0, cs, 1.0, scp), (64, sn, 1.0, scp),
                                            (128, cs, 1.0, scm), (192, sn, 1.0, scm)):
                    dst = sb["PX"][0:64, 1560:2072]
                    kb.op("vector", lambda e, trg=trg, scl=scl: e.tensor_tensor(dst, trg, scl, op=ALU.mult),
                          reads=[B("PH")], writes=[B("PX")])
                    if ri in (64, 192):
                        kb.op("vector", lambda e: e.tensor_scalar(sb["PX"][0:32, 1560:2072], sb["PX"][0:32, 1560:2072], -1.0, None, op0=ALU.mult),
                              reads=[B("PX")], writes=[B("PX")])
                    kb.dma("qg", rotm[ri:ri + 64, sl], dst,
                           reads=[B("PX")], writes=[rotmB[ri // 64]])

        s0_tables()

        def s0_w1split():
            for l in range(2):
                for jc in range(NJ):
                    wn = "WDB0" if jc % 2 == 0 else "WDB1"
                    pw = "PW0" if jc % 2 == 0 else "PW1"
                    kb.dma("qw", sb[wn][:, 0:2048], dr[f"w1_{l}"][:, jc * 2048:(jc + 1) * 2048],
                           reads=[wbuf[f"w1_{l}"]], writes=[B(wn)])
                    dbf = sb[pw][:].bitcast(mybir.dt.bfloat16)
                    kb.op("vector", lambda e, wn=wn, dbf=dbf: e.tensor_copy(
                        dbf[:, 0:2048], sb[wn][:, 0:2048]),
                        reads=[B(wn)], writes=[B(pw)])
                    kb.op("vector", lambda e, wn=wn, dbf=dbf: e.tensor_tensor(
                        dbf[:, 2048:4096], sb[wn][:, 0:2048], dbf[:, 0:2048], op=ALU.subtract),
                        reads=[B(wn), B(pw)], writes=[B(pw)])
                    kb.dma("qg", w1s[l][:, jc * 4096:(jc + 1) * 4096], dbf[:, 0:4096],
                           reads=[B(pw)], writes=[w1sB[l]])
        s0_w1split()

        def s1_chunk(c):
            t0 = c * T
            kb.dma("qw", sb["PX"][:, 0:NDC * T],
                   dr["x"][:, t0:t0 + T].rearrange("(dc p) t -> p dc t", p=128),
                   reads=[hbuf["x"][c]], writes=[B("PX")])
            for dc in range(NDC):
                xt = sb["PX"][:, dc * T:(dc + 1) * T]
                kb.op("tensor", lambda e, xt=xt, dc=dc: e.matmul(
                    ps[0][0:1, :], A("ones1"), xt, start=(dc == 0), stop=(dc == NDC - 1)),
                    reads=[B("PX"), B("ones1")], writes=[psB[0]])
            for dc in range(NDC):
                xt = sb["PX"][:, dc * T:(dc + 1) * T]
                kb.op("scalar", lambda e, xt=xt: e.activation(A("tmpA"), xt, AF.Square),
                      reads=[B("PX")], writes=[B("tmpA")])
                kb.op("tensor", lambda e, dc=dc: e.matmul(
                    ps[1][0:1, :], A("ones1"), A("tmpA"), start=(dc == 0), stop=(dc == NDC - 1)),
                    reads=[B("tmpA"), B("ones1")], writes=[psB[1]])
            r = sb["rows"]
            rm, rm2, rv, ri = r[0:1, :], r[32:33, :], r[64:65, :], r[96:97, :]
            kb.op("vector", lambda e: e.tensor_scalar(rm, ps[0][0:1, :], 1.0 / D, None, op0=ALU.mult),
                  reads=[psB[0]], writes=[B("rows")])
            kb.op("vector", lambda e: e.tensor_tensor(rm2, rm, rm, op=ALU.mult),
                  reads=[B("rows")], writes=[B("rows")])
            kb.op("vector", lambda e: e.scalar_tensor_tensor(rv, ps[1][0:1, :], 1.0 / D, rm2,
                                                             op0=ALU.mult, op1=ALU.subtract),
                  reads=[psB[1], B("rows")], writes=[B("rows")])
            kb.op("vector", lambda e: e.tensor_scalar(rv, rv, 1e-5, None, op0=ALU.add),
                  reads=[B("rows")], writes=[B("rows")])
            kb.op("scalar", lambda e: e.activation(ri, rv, AF.Ln),
                  reads=[B("rows")], writes=[B("rows")])
            kb.op("scalar", lambda e: e.activation(ri, ri, AF.Exp, scale=-0.5),
                  reads=[B("rows")], writes=[B("rows")])
            kb.op("tensor", lambda e: e.matmul(ps[2][:, :], A("sel0"), r[:], start=True, stop=True),
                  reads=[B("sel0"), B("rows")], writes=[psB[2]])
            kb.op("tensor", lambda e: e.matmul(ps[3][:, :], A("sel96"), r[:], start=True, stop=True),
                  reads=[B("sel96"), B("rows")], writes=[psB[3]])
            for dc in range(NDC):
                xt = sb["PX"][:, dc * T:(dc + 1) * T]
                ot = sb["PO"][:, dc * T:(dc + 1) * T]
                kb.op("vector", lambda e, xt=xt, ot=ot: e.tensor_tensor(ot, xt, ps[2][:, :], op=ALU.subtract),
                      reads=[B("PX"), psB[2]], writes=[B("PO")])
                kb.op("vector", lambda e, ot=ot: e.tensor_tensor(ot, ot, ps[3][:, :], op=ALU.mult),
                      reads=[B("PO"), psB[3]], writes=[B("PO")])
            kb.dma("qg", scr["xn"][:, t0:t0 + T].rearrange("(dc p) t -> p dc t", p=128),
                   sb["PO"][:, 0:NDC * T], reads=[B("PO")], writes=[hbuf["xn"][c]])

        def s2_prep():
            kb.op("vector", lambda e: e.memset(A("carry"), 0.0), reads=[], writes=[B("carry")])
            kb.op("vector", lambda e: e.memset(sb["PW1"][:, 0:8], 0.0), reads=[], writes=[B("PW1")])
            kb.dma("qw", sb["PW3"][0:16, 0:2048], dr["sel16"],
                   reads=[wbuf["sel16"]], writes=[B("PW3")])

        def s2_chunk(c):
            t0 = c * T
            PX, PH = sb["PX"], sb["PH"]
            lo = max(t0 - 1, 0)
            hi = min(t0 + T + 1, SEQ)
            ncol = hi - lo
            coff = 1 if c == 0 else 0
            rd = [hbuf["xn"][c]]
            if c > 0:
                rd.append(hbuf["xn"][c - 1])
            if c < NCH - 1:
                rd.append(hbuf["xn"][c + 1])
            kb.dma("qw", PX[:].rearrange("p (dc j) -> p dc j", dc=NDC)[:, :, coff:coff + ncol],
                   scr["xn"][:, lo:hi].rearrange("(dc p) t -> p dc t", p=128),
                   reads=rd, writes=[B("PX")])
            for ec in range(NDC):
                wb = "WDB0" if ec % 2 == 0 else "WDB1"
                half = (ec // 2) % 2
                wsl = sb[wb][:, half * 1024:(half + 1) * 1024]
                kb.dma("qw", wsl, dr["w_in"][:, ec * 1024:(ec + 1) * 1024],
                       reads=[wbuf["w_in"]], writes=[B(wb)])
                pk = ec % 2
                j0 = coff
                j1 = 514 if c < NCH - 1 else 513
                m0 = j0
                m1 = min(j0 + 512, j1)
                for dc in range(NDC):
                    kb.op("tensor", lambda e, ec=ec, dc=dc, wsl=wsl, m0=m0, m1=m1, pk=pk: e.matmul(
                        ps[pk][:, 0:m1 - m0], wsl[:, dc * 128:(dc + 1) * 128],
                        PX[:, dc * 520 + m0:dc * 520 + m1], start=(dc == 0), stop=(dc == NDC - 1)),
                        reads=[B(wb), B("PX")], writes=[psB[pk]])
                kb.op("scalar", lambda e, ec=ec, pk=pk, m0=m0, m1=m1: e.copy(
                    PH[:, ec * 520 + m0:ec * 520 + m1], ps[pk][:, 0:m1 - m0]),
                    reads=[psB[pk]], writes=[B("PH")])
                if m1 < j1:
                    for dc in range(NDC):
                        kb.op("tensor", lambda e, ec=ec, dc=dc, wsl=wsl, m1=m1, j1=j1, pk=pk: e.matmul(
                            ps[2 + pk][:, 0:j1 - m1], wsl[:, dc * 128:(dc + 1) * 128],
                            PX[:, dc * 520 + m1:dc * 520 + j1], start=(dc == 0), stop=(dc == NDC - 1)),
                            reads=[B(wb), B("PX")], writes=[psB[2 + pk]])
                    kb.op("scalar", lambda e, ec=ec, pk=pk, m1=m1, j1=j1: e.copy(
                        PH[:, ec * 520 + m1:ec * 520 + j1], ps[2 + pk][:, 0:j1 - m1]),
                        reads=[psB[2 + pk]], writes=[B("PH")])
                if c == 0:
                    kb.op("vector", lambda e, ec=ec: e.memset(PH[:, ec * 520:ec * 520 + 1], 0.0),
                          reads=[], writes=[B("PH")])
                if c == NCH - 1:
                    kb.op("vector", lambda e, ec=ec: e.memset(PH[:, ec * 520 + 513:ec * 520 + 514], 0.0),
                          reads=[], writes=[B("PH")])
            for ec in range(NDC):
                kern = "kx" if ec < 4 else "kz"
                cb = ec % 4
                h0 = PH[:, ec * 520 + 0:ec * 520 + 512]
                h1 = PH[:, ec * 520 + 1:ec * 520 + 513]
                h2 = PH[:, ec * 520 + 2:ec * 520 + 514]
                kb.op("vector", lambda e, h1=h1, kern=kern, cb=cb: e.tensor_scalar(
                    A("tmpA"), h1, sb[kern][:, cb * 3 + 1:cb * 3 + 2], None, op0=ALU.mult),
                    reads=[B("PH"), B(kern)], writes=[B("tmpA")])
                kb.op("vector", lambda e, h0=h0, kern=kern, cb=cb: e.scalar_tensor_tensor(
                    A("tmpB"), h0, sb[kern][:, cb * 3 + 0:cb * 3 + 1], A("tmpA"), op0=ALU.mult, op1=ALU.add),
                    reads=[B("PH"), B(kern), B("tmpA")], writes=[B("tmpB")])
                kb.op("vector", lambda e, h2=h2, kern=kern, cb=cb: e.scalar_tensor_tensor(
                    A("tmpA"), h2, sb[kern][:, cb * 3 + 2:cb * 3 + 3], A("tmpB"), op0=ALU.mult, op1=ALU.add),
                    reads=[B("PH"), B(kern), B("tmpB")], writes=[B("tmpA")])
                kb.op("scalar", lambda e, ec=ec: e.activation(
                    sb["PP1"][:, ec * 512:(ec + 1) * 512], A("tmpA"), AF.Silu),
                    reads=[B("tmpA")], writes=[B("PP1")])
            for cb in range(4):
                kb.op("tensor", lambda e, cb=cb: e.matmul(
                    ps[4][0:80, :], sb["w_xp"][:, cb * 80:(cb + 1) * 80],
                    sb["PP1"][:, cb * 512:(cb + 1) * 512], start=(cb == 0), stop=(cb == 3)),
                    reads=[B("w_xp"), B("PP1")], writes=[psB[4]])
            kb.op("scalar", lambda e: e.copy(sb["proj"][0:80, :], ps[4][0:80, :]),
                  reads=[psB[4]], writes=[B("proj")])
            kb.op("scalar", lambda e: e.activation(A("bct"), sb["proj"][64:80, :], AF.Tanh),
                  reads=[B("proj")], writes=[B("bct")])
            for cb in range(4):
                kb.op("tensor", lambda e, cb=cb: e.matmul(
                    ps[5][:, :], sb["w_dt"][:, cb * 128:(cb + 1) * 128], sb["proj"][0:64, :],
                    start=True, stop=True),
                    reads=[B("w_dt"), B("proj")], writes=[psB[5]])
                kb.op("vector", lambda e, cb=cb: e.tensor_scalar(
                    A("tmpA"), ps[5][:, :], sb["b_dt"][:, cb:cb + 1], 10.0, op0=ALU.add, op1=ALU.min),
                    reads=[psB[5], B("b_dt")], writes=[B("tmpA")])
                kb.op("scalar", lambda e: e.activation(A("tmpB"), A("tmpA"), AF.Exp),
                      reads=[B("tmpA")], writes=[B("tmpB")])
                kb.op("scalar", lambda e: e.activation(A("tmpA"), A("tmpB"), AF.Ln, bias=1.0),
                      reads=[B("tmpB")], writes=[B("tmpA")])
                kb.op("vector", lambda e, cb=cb: e.tensor_scalar(
                    sb["PP2"][:, cb * 512:(cb + 1) * 512], A("tmpA"), 1e-4, 1.0, op0=ALU.max, op1=ALU.min),
                    reads=[B("tmpA")], writes=[B("PP2")])
            for s in range(DS):
                pk = 6 + s % 2
                kb.op("tensor", lambda e, s=s, pk=pk: e.matmul(
                    ps[pk][:, :], sb["PW3"][0:16, s * 128:(s + 1) * 128], A("bct"),
                    start=True, stop=True),
                    reads=[B("PW3"), B("bct")], writes=[psB[pk]])
                kb.op("scalar", lambda e, s=s, pk=pk: e.copy(
                    sb["PP3"][:, s * 512:(s + 1) * 512], ps[pk][:, :]),
                    reads=[psB[pk]], writes=[B("PP3")])
                kb.op("tensor", lambda e, s=s, pk=pk: e.matmul(
                    ps[pk][:, :], sb["PW3"][0:16, (8 + s) * 128:(9 + s) * 128], A("bct"),
                    start=True, stop=True),
                    reads=[B("PW3"), B("bct")], writes=[psB[pk]])
                kb.op("scalar", lambda e, s=s, pk=pk: e.copy(
                    sb["PH"][:, s * 512:(s + 1) * 512], ps[pk][:, :]),
                    reads=[psB[pk]], writes=[B("PH")])
            for cb in range(4):
                SA, SB_, SC = sb["PW0"], sb["PW1"], sb["PW2"]
                for s in range(DS):
                    dst = SA[:, 8:4104].rearrange("p (t s) -> p t s", s=8)[:, :, s]
                    kb.op("vector", lambda e, dst=dst, cb=cb, s=s: e.tensor_scalar(
                        dst, sb["PP2"][:, cb * 512:(cb + 1) * 512],
                        sb["a_cols"][:, cb * 8 + s:cb * 8 + s + 1], None, op0=ALU.mult),
                        reads=[B("PP2"), B("a_cols")], writes=[B("PW0")])
                kb.op("scalar", lambda e: e.activation(SB_[:, 8:4104], SA[:, 8:4104], AF.Exp, scale=-1.0),
                      reads=[B("PW0")], writes=[B("PW1")])
                kb.op("vector", lambda e: e.tensor_scalar(SB_[:, 8:4104], SB_[:, 8:4104], 1e-4, 1.0, op0=ALU.max, op1=ALU.min),
                      reads=[B("PW1")], writes=[B("PW1")])
                xbrep = sb["PP1"][:, cb * 512:(cb + 1) * 512].unsqueeze(2).broadcast_to([128, 512, 8])
                bview = sb["PP3"][:, 0:4096].rearrange("p (s t) -> p t s", s=8)
                kb.op("vector", lambda e, xbrep=xbrep, bview=bview: e.tensor_tensor(
                    SA[:, 8:4104].rearrange("p (t s) -> p t s", s=8),
                    bview, xbrep, op=ALU.mult),
                    reads=[B("PP3"), B("PP1")], writes=[B("PW0")])
                kb.op("scalar", lambda e, cb=cb: e.copy(SC[:, 0:8], sb["carry"][:, cb * 8:cb * 8 + 8]),
                      reads=[B("carry")], writes=[B("PW2")])
                kb.op("vector", lambda e: e.scalar_tensor_tensor(
                    SC[:, 8:4104], SB_[:, 8:4104], 1.0, SA[:, 8:4104], op0=ALU.subtract, op1=ALU.mult),
                    reads=[B("PW1"), B("PW0")], writes=[B("PW2")])
                for s in range(DS):
                    d0 = SB_[:, 0:4104].rearrange("p (j s) -> p j s", s=8)[:, :, s]
                    d1 = SC[:, 0:4104].rearrange("p (j s) -> p j s", s=8)[:, :, s]
                    oo = SA[:, 0:4104].rearrange("p (j s) -> p j s", s=8)[:, :, s]
                    kb.op("vector", lambda e, d0=d0, d1=d1, oo=oo: e.tensor_tensor_scan(
                        oo, d0, d1, 0.0, op0=ALU.mult, op1=ALU.add),
                        reads=[B("PW1"), B("PW2")], writes=[B("PW0")])
                if debug_spills and c == 0 and cb == 0:
                    kb.dma("qg", dbg["dbg_dec"], SB_[:, 0:4104], reads=[B("PW1")], writes=[hbuf["out"][0]])
                    kb.dma("qg", dbg["dbg_drv"], SC[:, 0:4104], reads=[B("PW2")], writes=[hbuf["out"][0]])
                    kb.dma("qg", dbg["dbg_st"], SA[:, 0:4104], reads=[B("PW0")], writes=[hbuf["out"][0]])
                cview = sb["PH"][:, 0:4096].rearrange("p (s t) -> p t s", s=8)
                kb.op("vector", lambda e, cview=cview: e.tensor_tensor(
                    SC[:, 8:4104].rearrange("p (t s) -> p t s", s=8),
                    SA[:, 8:4104].rearrange("p (t s) -> p t s", s=8),
                    cview, op=ALU.mult),
                    reads=[B("PW0"), B("PH")], writes=[B("PW2")])
                kb.op("vector", lambda e, cb=cb: e.tensor_reduce(
                    A("tmpA"),
                    SC[:, 8:4104].rearrange("p (t s) -> p t s", s=8),
                    mybir.AxisListType.X, ALU.add, negate=True),
                    reads=[B("PW2")], writes=[B("tmpA")])
                kb.op("scalar", lambda e, cb=cb: e.copy(
                    sb["carry"][:, cb * 8:cb * 8 + 8], SA[:, 8 + 511 * 8: 8 + 512 * 8]),
                    reads=[B("PW0")], writes=[B("carry")])
                kb.op("vector", lambda e, cb=cb: e.scalar_tensor_tensor(
                    sb["PP2"][:, 2048 + cb * 512:2048 + (cb + 1) * 512],
                    sb["PP1"][:, cb * 512:(cb + 1) * 512], sb["d_col"][:, cb:cb + 1],
                    A("tmpA"), op0=ALU.mult, op1=ALU.add),
                    reads=[B("PP1"), B("d_col"), B("tmpA")], writes=[B("PP2")])
            for ec in range(NDC):
                wb = "WDB0" if ec % 2 == 0 else "WDB1"
                half = (ec // 2) % 2
                wsl = sb[wb][:, half * 1024:(half + 1) * 1024]
                kb.dma("qw", wsl, dr["w_out"][:, ec * 1024:(ec + 1) * 1024],
                       reads=[wbuf["w_out"]], writes=[B(wb)])
                pk = ec % 2
                for dc in range(NDC):
                    if dc < 4:
                        rhs = sb["PP2"][:, 2048 + dc * 512:2048 + (dc + 1) * 512]
                        rb = B("PP2")
                    else:
                        rhs = sb["PP1"][:, 2048 + (dc - 4) * 512:2048 + (dc - 3) * 512]
                        rb = B("PP1")
                    kb.op("tensor", lambda e, wsl=wsl, dc=dc, rhs=rhs, pk=pk: e.matmul(
                        ps[pk][:, :], wsl[:, dc * 128:(dc + 1) * 128], rhs,
                        start=(dc == 0), stop=(dc == NDC - 1)),
                        reads=[B(wb), rb], writes=[psB[pk]])
                kb.op("scalar", lambda e, ec=ec, pk=pk: e.copy(
                    sb["PO"][:, ec * 512:(ec + 1) * 512], ps[pk][:, :]),
                    reads=[psB[pk]], writes=[B("PO")])
            kb.dma("qg", scr["attnb"][:, t0:t0 + T].rearrange("(dc p) t -> p dc t", p=128),
                   sb["PO"][:, 0:NDC * T], reads=[B("PO")], writes=[hbuf["attnb"][c]])
            if debug_spills:
                kb.dma("qg", dbg["dbg_xz"][:, t0:t0 + T].rearrange("(dc p) t -> p dc t", p=128),
                       sb["PP1"][:, 0:4096], reads=[B("PP1")], writes=[hbuf["out"][c]])
                kb.dma("qg", dbg["dbg_dy"][:, t0:t0 + T].rearrange("(dc p) t -> p dc t", p=128),
                       sb["PP2"][:, 0:4096], reads=[B("PP2")], writes=[hbuf["out"][c]])
                kb.dma("qg", dbg["dbg_h"][:, t0:t0 + T].rearrange("(dc p) t -> p dc t", p=128),
                       sb["PH"][:].rearrange("p (dc j) -> p dc j", dc=NDC)[:, :, 1:513],
                       reads=[B("PH")], writes=[hbuf["out"][c]])
                kb.dma("qg", dbg["dbg_bc"][:, t0:t0 + T].rearrange("(dc p) t -> p dc t", p=128),
                       sb["PP3"][:, 0:4096], reads=[B("PP3")], writes=[hbuf["out"][c]])

        def s3_prep(l):
            for q in range(3):
                kb.dma("qw", sb[f"PW{q}"][:, 0:4096], dr[f"wqkv{l}"][:, q * 4096:(q + 1) * 4096],
                       reads=[wbuf[f"wqkv{l}"]], writes=[B(f"PW{q}")])
            kb.dma("qw", sb["PW3"][:, 0:4096], dr[f"wo{l}"],
                   reads=[wbuf[f"wo{l}"]], writes=[B("PW3")])

        def _emit_pv(item, kt0):
            kt, vA, vB, eA, eB = item
            kb.op("tensor", lambda e, vA=vA, eA=eA, kt=kt: e.matmul(
                ps[4][0:65, :], vA, eA, start=(kt == kt0), stop=(kt == 7)),
                reads=[B("PP3"), B("PO")], writes=[psB[4]])
            kb.op("tensor", lambda e, vB=vB, eB=eB, kt=kt: e.matmul(
                ps[5][0:65, :], vB, eB, start=(kt == kt0), stop=(kt == 7)),
                reads=[B("PP3"), B("PO")], writes=[psB[5]])

        def wqkv_tile(es, dc):
            q = es // 4
            off = (es % 4) * 1024 + dc * 128
            return sb[f"PW{q}"][:, off:off + 128], B(f"PW{q}")

        def s3_chunk(l, c, src, dst):
            t0 = c * T
            PX, PH, PO = sb["PX"], sb["PH"], sb["PO"]
            qsn, ksn = f"qs{l}", f"ks{l}"
            vb_cur = (c % 2) * 2080
            vb_prv = ((c + 1) % 2) * 2080
            kp_cur = "PP1" if c % 2 == 0 else "PP2"
            kp_prv = "PP2" if c % 2 == 0 else "PP1"
            lo = max(t0 - 1, 0)
            ncol = t0 + T - lo
            coff = 1 if c == 0 else 0
            rd = [hbuf[src][c]] + ([hbuf[src][c - 1]] if c > 0 else [])
            kb.dma("qw", PX[:].rearrange("p (dc j) -> p dc j", dc=NDC)[:, :, coff:coff + ncol],
                   scr[src][:, lo:t0 + T].rearrange("(dc p) t -> p dc t", p=128),
                   reads=rd, writes=[B("PX")])
            if c == 0:
                kb.op("vector", lambda e: e.memset(
                    PX[:].rearrange("p (dc j) -> p dc j", dc=NDC)[:, :, 0:1], 0.0),
                    reads=[], writes=[B("PX")])
            wk = {"wkcsq": sb["PP1"][:, 2048:2560], "wksnq": sb["PP1"][:, 2560:3072],
                  "wkcsk": sb["PP1"][:, 3072:3584], "wksnk": sb["PP1"][:, 3584:4096]}
            for (nm, ri) in (("wkcsq", 0), ("wksnq", 64), ("wkcsk", 128), ("wksnk", 192)):
                kb.dma("qw", wk[nm][0:64, :], rotm[ri:ri + 64, t0:t0 + T],
                       reads=[rotmB[ri // 64]], writes=[B("PP1")])
                kb.dma("qw", wk[nm][64:128, :], rotm[ri:ri + 64, t0:t0 + T],
                       reads=[rotmB[ri // 64]], writes=[B("PP1")])

            def xsl(dc, j0, width):
                base = dc * 520 + (j0 + (1 if dc < 4 else 0))
                return PX[:, base:base + width]

            for es in range(8):
                pk = es % 2
                for dc in range(NDC):
                    w, wB = wqkv_tile(es, dc)
                    kb.op("tensor", lambda e, w=w, dc=dc, pk=pk: e.matmul(
                        ps[pk][:, :], w, xsl(dc, 0, 512), start=(dc == 0), stop=(dc == NDC - 1)),
                        reads=[wB, B("PX")], writes=[psB[pk]])
                kb.op("scalar", lambda e, es=es, pk=pk: e.copy(PH[:, es * 512:(es + 1) * 512], ps[pk][:, :]),
                      reads=[psB[pk]], writes=[B("PH")])
            for tt in range(4):
                pk = 4 + tt % 2
                first = True
                for dc in range(NDC):
                    for esv in range(4):
                        w, wB = wqkv_tile(8 + esv, dc)
                        kb.op("tensor", lambda e, w=w, dc=dc, tt=tt, esv=esv, pk=pk, first=first: e.matmul(
                            ps[pk][:, esv * 128:(esv + 1) * 128],
                            xsl(dc, tt * 128, 128), w,
                            start=first, stop=(dc == NDC - 1 and esv == 3)),
                            reads=[wB, B("PX")], writes=[psB[pk]])
                        first = False
                dstv = sb["PP3"][:, vb_cur + tt * 520: vb_cur + tt * 520 + 520]
                kb.op("scalar", lambda e, dstv=dstv, pk=pk: e.copy(
                    dstv.rearrange("p (h d) -> p h d", d=65)[:, :, 0:64], ps[pk][:, :]),
                    reads=[psB[pk]], writes=[B("PP3")])
            kb.op("vector", lambda e: e.memset(
                sb["PP3"][:, vb_cur:vb_cur + 2080].rearrange("p (tt h d) -> p tt h d", h=8, d=65)[:, :, :, 64:65], 1.0),
                reads=[], writes=[B("PP3")])
            for es in range(8):
                phs = PH[:, es * 512:(es + 1) * 512]
                kb.op("scalar", lambda e, phs=phs: e.activation(A("tmpA"), phs, AF.Square),
                      reads=[B("PH")], writes=[B("tmpA")])
                kb.op("tensor", lambda e: e.matmul(ps[6][0:2, :], A("ones2"), A("tmpA"),
                                                   start=True, stop=True),
                      reads=[B("ones2"), B("tmpA")], writes=[psB[6]])
                r = sb["rows"]
                kb.op("vector", lambda e: e.tensor_scalar(r[0:2, :], ps[6][0:2, :], 1e-24, None, op0=ALU.max),
                      reads=[psB[6]], writes=[B("rows")])
                kb.op("scalar", lambda e: e.activation(r[0:2, :], r[0:2, :], AF.Ln),
                      reads=[B("rows")], writes=[B("rows")])
                kb.op("scalar", lambda e: e.activation(r[0:2, :], r[0:2, :], AF.Exp, scale=-0.5),
                      reads=[B("rows")], writes=[B("rows")])
                kb.op("tensor", lambda e: e.matmul(ps[6][:, :], A("selrn"), r[:],
                                                   start=True, stop=True),
                      reads=[B("selrn"), B("rows")], writes=[psB[6]])
                scn = qsn if es < 4 else ksn
                kb.op("vector", lambda e, phs=phs, scn=scn: e.scalar_tensor_tensor(
                    phs, phs, sb[scn][:, 0:1], ps[6][:, :], op0=ALU.mult, op1=ALU.mult),
                    reads=[B("PH"), B(scn), psB[6]], writes=[B("PH")])
                cs = wk["wkcsq"] if es < 4 else wk["wkcsk"]
                sn = wk["wksnq"] if es < 4 else wk["wksnk"]
                kb.op("tensor", lambda e, phs=phs: e.matmul(ps[7][:, :], A("perm"), phs,
                                                            start=True, stop=True),
                      reads=[B("perm"), B("PH")], writes=[psB[7]])
                kb.op("vector", lambda e, sn=sn: e.tensor_tensor(A("tmpA"), ps[7][:, :], sn, op=ALU.mult),
                      reads=[psB[7], B("PP1")], writes=[B("tmpA")])
                kb.op("vector", lambda e, phs=phs, cs=cs: e.tensor_tensor(A("tmpB"), phs, cs, op=ALU.mult),
                      reads=[B("PH"), B("PP1")], writes=[B("tmpB")])
                kb.op("vector", lambda e, phs=phs: e.tensor_tensor(phs, A("tmpA"), A("tmpB"), op=ALU.add),
                      reads=[B("tmpA"), B("tmpB")], writes=[B("PH")])
            kt0 = 4 if c == 0 else 0
            for hp in range(4):
                qes = PH[:, hp * 512:(hp + 1) * 512]
                pvq = []
                for kt in range(kt0, 8):
                    if kt < 4:
                        kbase, kB = sb[kp_prv][:, kt * 512:], B(kp_prv)
                        ksA = kbase[0:64, hp * 0:0 + 128] if False else sb[kp_prv][0:64, hp * 512 + kt * 128: hp * 512 + kt * 128 + 128]
                        ksB = sb[kp_prv][64:128, hp * 512 + kt * 128: hp * 512 + kt * 128 + 128]
                        vsrc = vb_prv + (kt % 4) * 520
                    else:
                        ksA = PH[0:64, (4 + hp) * 512 + (kt - 4) * 128: (4 + hp) * 512 + (kt - 4) * 128 + 128]
                        ksB = PH[64:128, (4 + hp) * 512 + (kt - 4) * 128: (4 + hp) * 512 + (kt - 4) * 128 + 128]
                        kB = B("PH")
                        vsrc = vb_cur + (kt - 4) * 520
                    bkA, bkB = ps[kt % 2], ps[2 + kt % 2]
                    bkAB_, bkBB_ = psB[kt % 2], psB[2 + kt % 2]
                    kb.op("tensor", lambda e, ksA=ksA, qes=qes, bkA=bkA: e.matmul(
                        bkA[:, :], ksA, qes[0:64, :], start=True, stop=True, tile_position=(0, 0)),
                        reads=[kB, B("PH")], writes=[bkAB_])
                    kb.op("tensor", lambda e, ksB=ksB, qes=qes, bkB=bkB: e.matmul(
                        bkB[:, :], ksB, qes[64:128, :], start=True, stop=True, tile_position=(64, 0)),
                        reads=[kB, B("PH")], writes=[bkBB_])
                    if kt >= 4:
                        qt = kt - 4
                        wdt = (qt + 1) * 128
                        msl = sb["mask_cur"][:, 512 - wdt:] if False else None
                        for bk_, bkb_ in ((bkA, bkAB_), (bkB, bkBB_)):
                            kb.op("vector", lambda e, bk_=bk_, wdt=wdt: e.tensor_tensor(
                                bk_[:, 0:wdt], bk_[:, 0:wdt], sb["mask_cur"][:, 512 - wdt:], op=ALU.add),
                                reads=[bkb_, B("mask_cur")], writes=[bkb_])
                    else:
                        wdt = 512 - kt * 128
                        for bk_, bkb_ in ((bkA, bkAB_), (bkB, bkBB_)):
                            kb.op("vector", lambda e, bk_=bk_, kt=kt, wdt=wdt: e.tensor_tensor(
                                bk_[:, kt * 128:512], bk_[:, kt * 128:512], sb["mask_prev"][:, 0:wdt], op=ALU.add),
                                reads=[bkb_, B("mask_prev")], writes=[bkb_])
                    eA = PO[:, (kt % 2) * 512:(kt % 2) * 512 + 512]
                    eB = PO[:, 1024 + (kt % 2) * 512:1024 + (kt % 2) * 512 + 512]
                    kb.op("scalar", lambda e, bkA=bkA, eA=eA: e.activation(eA, bkA[:, :], AF.Exp, scale=8.0),
                          reads=[bkAB_], writes=[B("PO")])
                    kb.op("scalar", lambda e, bkB=bkB, eB=eB: e.activation(eB, bkB[:, :], AF.Exp, scale=8.0),
                          reads=[bkBB_], writes=[B("PO")])
                    vA = sb["PP3"][:, vsrc:vsrc + 520].rearrange("p (h d) -> p h d", d=65)[:, 2 * hp, :]
                    vB = sb["PP3"][:, vsrc:vsrc + 520].rearrange("p (h d) -> p h d", d=65)[:, 2 * hp + 1, :]
                    pvq.append((kt, vA, vB, eA, eB))
                    if len(pvq) >= 2:
                        _emit_pv(pvq.pop(0), kt0)
                while pvq:
                    _emit_pv(pvq.pop(0), kt0)
                r = sb["rows"]
                for (bank, bb_, rrow, seln, bcn, orow) in (
                        (ps[4], psB[4], r[64:65, :], "sel64", "bc0", 0),
                        (ps[5], psB[5], r[96:97, :], "sel96", "bc1", 64)):
                    kb.op("scalar", lambda e, bank=bank, rrow=rrow: e.activation(rrow, bank[64:65, :], AF.Ln),
                          reads=[bb_], writes=[B("rows")])
                    kb.op("scalar", lambda e, rrow=rrow: e.activation(rrow, rrow, AF.Exp, scale=-1.0),
                          reads=[B("rows")], writes=[B("rows")])
                    kb.op("tensor", lambda e, seln=seln: e.matmul(
                        ps[6][0:64, :], sb[seln][:, 0:64], r[:], start=True, stop=True),
                        reads=[B(seln), B("rows")], writes=[psB[6]])
                    kb.op("scalar", lambda e, bcn=bcn: e.copy(sb[bcn][0:64, :], ps[6][0:64, :]),
                          reads=[psB[6]], writes=[B(bcn)])
                    kb.op("vector", lambda e, bank=bank, bcn=bcn, hp=hp, orow=orow: e.tensor_tensor(
                        sb["PP2"][orow:orow + 64, 2048 + hp * 512:2048 + (hp + 1) * 512],
                        bank[0:64, :], sb[bcn][0:64, :], op=ALU.mult),
                        reads=[bb_, B(bcn)], writes=[B("PP2")])
            kb.op("vector", lambda e: e.tensor_copy(sb[kp_cur][:, 0:2048], PH[:, 2048:4096]),
                  reads=[B("PH")], writes=[B(kp_cur)])
            for ec in range(NDC):
                pk = ec % 2
                for cc in range(4):
                    kb.op("tensor", lambda e, ec=ec, cc=cc, pk=pk: e.matmul(
                        ps[pk][:, :], sb["PW3"][:, (ec * 4 + cc) * 128:(ec * 4 + cc) * 128 + 128],
                        sb["PP2"][:, 2048 + cc * 512:2048 + (cc + 1) * 512],
                        start=(cc == 0), stop=(cc == 3)),
                        reads=[B("PW3"), B("PP2")], writes=[psB[pk]])
                kb.op("vector", lambda e, ec=ec, pk=pk: e.tensor_tensor(
                    PO[:, ec * 512:(ec + 1) * 512], ps[pk][:, :], xsl(ec if False else ec, 0, 512) if False else PX[:, ec * 520 + 1:ec * 520 + 513],
                    op=ALU.add),
                    reads=[psB[pk], B("PX")], writes=[B("PO")])
            kb.dma("qg", scr[dst][:, t0:t0 + T].rearrange("(dc p) t -> p dc t", p=128),
                   PO[:, 0:NDC * T], reads=[B("PO")], writes=[hbuf[dst][c]])

        def s4_chunk(l, c, src, dst, final=False):
            t0 = c * T
            PX, PH, PO = sb["PX"], sb["PH"], sb["PO"]
            lo = max(t0 - 1, 0)
            ncol = t0 + T - lo
            coff = 1 if c == 0 else 0
            rd = [hbuf[src][c]] + ([hbuf[src][c - 1]] if c > 0 else [])
            kb.dma("qw", PX[:].rearrange("p (dc j) -> p dc j", dc=NDC)[:, :, coff:coff + ncol],
                   scr[src][:, lo:t0 + T].rearrange("(dc p) t -> p dc t", p=128),
                   reads=rd, writes=[B("PX")])
            if c == 0:
                kb.op("vector", lambda e: e.memset(
                    PX[:].rearrange("p (dc j) -> p dc j", dc=NDC)[:, :, 0:1], 0.0),
                    reads=[], writes=[B("PX")])

            def xsh(dc, j0=0, width=512):
                base = dc * 520 + (j0 + (1 if dc < 4 else 0))
                return PX[:, base:base + width]

            for dc in range(NDC):
                kb.op("tensor", lambda e, dc=dc: e.matmul(
                    ps[6][0:1, :], A("ones1"), xsh(dc), start=(dc == 0), stop=(dc == NDC - 1)),
                    reads=[B("ones1"), B("PX")], writes=[psB[6]])
            for dc in range(NDC):
                kb.op("scalar", lambda e, dc=dc: e.activation(A("tmpA"), xsh(dc), AF.Square),
                      reads=[B("PX")], writes=[B("tmpA")])
                kb.op("tensor", lambda e, dc=dc: e.matmul(
                    ps[7][0:1, :], A("ones1"), A("tmpA"), start=(dc == 0), stop=(dc == NDC - 1)),
                    reads=[B("ones1"), B("tmpA")], writes=[psB[7]])
            r = sb["rows"]
            rm, rm2, rv, ri = r[0:1, :], r[32:33, :], r[64:65, :], r[96:97, :]
            kb.op("vector", lambda e: e.tensor_scalar(rm, ps[6][0:1, :], 1.0 / D, None, op0=ALU.mult),
                  reads=[psB[6]], writes=[B("rows")])
            kb.op("vector", lambda e: e.tensor_tensor(rm2, rm, rm, op=ALU.mult),
                  reads=[B("rows")], writes=[B("rows")])
            kb.op("vector", lambda e: e.scalar_tensor_tensor(rv, ps[7][0:1, :], 1.0 / D, rm2,
                                                             op0=ALU.mult, op1=ALU.subtract),
                  reads=[psB[7], B("rows")], writes=[B("rows")])
            kb.op("vector", lambda e: e.tensor_scalar(rv, rv, 1e-5, None, op0=ALU.add),
                  reads=[B("rows")], writes=[B("rows")])
            kb.op("scalar", lambda e: e.activation(ri, rv, AF.Ln),
                  reads=[B("rows")], writes=[B("rows")])
            kb.op("scalar", lambda e: e.activation(ri, ri, AF.Exp, scale=-0.5),
                  reads=[B("rows")], writes=[B("rows")])
            kb.op("tensor", lambda e: e.matmul(ps[2][:, :], A("sel0"), r[:], start=True, stop=True),
                  reads=[B("sel0"), B("rows")], writes=[psB[2]])
            kb.op("tensor", lambda e: e.matmul(ps[3][:, :], A("sel96"), r[:], start=True, stop=True),
                  reads=[B("sel96"), B("rows")], writes=[psB[3]])
            PHbf = PH[:].bitcast(mybir.dt.bfloat16)
            for dc in range(NDC):
                kb.op("vector", lambda e, dc=dc: e.tensor_tensor(A("tmpA"), xsh(dc), ps[2][:, :], op=ALU.subtract),
                      reads=[B("PX"), psB[2]], writes=[B("tmpA")])
                kb.op("vector", lambda e: e.tensor_tensor(A("tmpB"), A("tmpA"), ps[3][:, :], op=ALU.mult),
                      reads=[B("tmpA"), psB[3]], writes=[B("tmpB")])
                kb.op("vector", lambda e, dc=dc: e.tensor_scalar(
                    A("tmpA"), A("tmpB"),
                    sb[f"lg{l}"][:, dc:dc + 1], sb[f"lb{l}"][:, dc:dc + 1],
                    op0=ALU.mult, op1=ALU.add),
                    reads=[B("tmpB"), B(f"lg{l}"), B(f"lb{l}")], writes=[B("tmpA")])
                kb.op("vector", lambda e, dc=dc, PHbf=PHbf: e.tensor_copy(
                    PHbf[:, dc * 512:(dc + 1) * 512], A("tmpA")),
                    reads=[B("tmpA")], writes=[B("PH")])
                kb.op("vector", lambda e, dc=dc, PHbf=PHbf: e.tensor_tensor(
                    PHbf[:, 4096 + dc * 512:4096 + (dc + 1) * 512], A("tmpA"),
                    PHbf[:, dc * 512:(dc + 1) * 512], op=ALU.subtract),
                    reads=[B("tmpA"), B("PH")], writes=[B("PH")])

            def pslot(jc):
                if jc < 8:
                    return sb["PP1"][:, jc * 512:(jc + 1) * 512], B("PP1")
                if jc < 16:
                    return sb["PP2"][:, (jc - 8) * 512:(jc - 7) * 512], B("PP2")
                return sb["PP3"][:, (jc - 16) * 512:(jc - 15) * 512], B("PP3")

            for jc in range(NJ):
                wn = "PW0" if jc % 2 == 0 else "PW1"
                pa = 0 + (jc % 2) * 4
                pg = 1 + (jc % 2) * 4
                gl = "tmpA" if jc % 2 == 0 else "tmpB"
                wbf = sb[wn][:].bitcast(mybir.dt.bfloat16)
                kb.dma("qw", wbf[:, 0:4096], w1s[l][:, jc * 4096:(jc + 1) * 4096],
                       reads=[w1sB[l]], writes=[B(wn)])
                for (po, aoff) in ((pa, 0), (pg, 128)):
                    terms = []
                    for dc in range(NDC):
                        wh = wbf[:, dc * 256 + aoff:dc * 256 + aoff + 128]
                        wl = wbf[:, 2048 + dc * 256 + aoff:2048 + dc * 256 + aoff + 128]
                        xh = PHbf[:, dc * 512:(dc + 1) * 512]
                        xl = PHbf[:, 4096 + dc * 512:4096 + (dc + 1) * 512]
                        terms += [(wh, xh), (wh, xl), (wl, xh)]
                    for ti, (wt, xt) in enumerate(terms):
                        kb.op("tensor", lambda e, wt=wt, xt=xt, po=po, ti=ti, n=len(terms): e.matmul(
                            ps[po][:, :], wt, xt,
                            start=(ti == 0), stop=(ti == n - 1)),
                            reads=[B(wn), B("PH")], writes=[psB[po]])
                kb.op("scalar", lambda e, jc=jc, pg=pg, gl=gl: e.activation(
                    A(gl), ps[pg][:, :], AF.Gelu, bias=sb[f"b1g{l}"][:, jc:jc + 1]),
                    reads=[psB[pg], B(f"b1g{l}")], writes=[B(gl)])
                pdst, pB = pslot(jc)
                kb.op("vector", lambda e, jc=jc, pdst=pdst, pa=pa, gl=gl: e.scalar_tensor_tensor(
                    pdst, ps[pa][:, :], sb[f"b1a{l}"][:, jc:jc + 1], A(gl),
                    op0=ALU.add, op1=ALU.mult),
                    reads=[psB[pa], B(f"b1a{l}"), B(gl)], writes=[pB])
            if final:
                kb.dma("qw", sb["WDB0"][:, 0:2048],
                       scr["attnb"][:, t0:t0 + T].rearrange("(dc p) t -> p dc t", p=128)[:, 0:4, :],
                       reads=[hbuf["attnb"][c]], writes=[B("WDB0")])
                kb.dma("qw", sb["WDB1"][:, 0:2048],
                       scr["attnb"][:, t0:t0 + T].rearrange("(dc p) t -> p dc t", p=128)[:, 4:8, :],
                       reads=[hbuf["attnb"][c]], writes=[B("WDB1")])
                kb.dma("qw", PH[:, 0:4096],
                       dr["x"][:, t0:t0 + T].rearrange("(dc p) t -> p dc t", p=128),
                       reads=[hbuf["x"][c]], writes=[B("PH")])
            for jc in range(NJ):
                wn = "PW2" if jc % 2 == 0 else "PW3"
                kb.dma("qw", sb[wn][:, 0:1024], dr[f"w2_{l}"][:, jc * 1024:(jc + 1) * 1024],
                       reads=[wbuf[f"w2_{l}"]], writes=[B(wn)])
                psrc, pB = pslot(jc)
                for ec in range(NDC):
                    kb.op("tensor", lambda e, wn=wn, ec=ec, psrc=psrc, jc=jc: e.matmul(
                        ps[ec][:, :], sb[wn][:, ec * 128:(ec + 1) * 128], psrc,
                        start=(jc == 0), stop=(jc == NJ - 1)),
                        reads=[B(wn), pB], writes=[psB[ec]])
            for ec in range(NDC):
                if not final:
                    kb.op("vector", lambda e, ec=ec: e.scalar_tensor_tensor(
                        PO[:, ec * 512:(ec + 1) * 512], ps[ec][:, :],
                        sb[f"b2_{l}"][:, ec:ec + 1], PX[:, ec * 520 + 1:ec * 520 + 513],
                        op0=ALU.add, op1=ALU.add),
                        reads=[psB[ec], B(f"b2_{l}"), B("PX")], writes=[B("PO")])
                else:
                    kb.op("vector", lambda e, ec=ec: e.scalar_tensor_tensor(
                        A("tmpA"), ps[ec][:, :],
                        sb[f"b2_{l}"][:, ec:ec + 1], PX[:, ec * 520 + 1:ec * 520 + 513],
                        op0=ALU.add, op1=ALU.add),
                        reads=[psB[ec], B(f"b2_{l}"), B("PX")], writes=[B("tmpA")])
                    at = (sb["WDB0"] if ec < 4 else sb["WDB1"])[:, (ec % 4) * 512:(ec % 4 + 1) * 512]
                    atB = B("WDB0") if ec < 4 else B("WDB1")
                    kb.op("vector", lambda e, at=at: e.tensor_tensor(A("tmpB"), A("tmpA"), at, op=ALU.add),
                          reads=[B("tmpA"), atB], writes=[B("tmpB")])
                    kb.op("vector", lambda e, ec=ec: e.tensor_tensor(
                        PO[:, ec * 512:(ec + 1) * 512], A("tmpB"), PH[:, ec * 512:(ec + 1) * 512],
                        op=ALU.add),
                        reads=[B("tmpB"), B("PH")], writes=[B("PO")])
            tgt = out_h if final else scr[dst]
            tb = hbuf["out"][c] if final else hbuf[dst][c]
            kb.dma("qg", tgt[:, t0:t0 + T].rearrange("(dc p) t -> p dc t", p=128),
                   PO[:, 0:NDC * T], reads=[B("PO")], writes=[tb])

        if stages >= 1:
            for c in range(NCH):
                s1_chunk(c)
        if stages >= 2:
            s2_prep()
            for c in range(NCH):
                s2_chunk(c)
        if stages >= 3:
            s3_prep(0)
            for c in range(NCH):
                s3_chunk(0, c, "xn", "xa")
        if stages >= 4:
            for c in range(NCH):
                s4_chunk(0, c, "xa", "xb")
        if stages >= 5:
            s3_prep(1)
            for c in range(NCH):
                s3_chunk(1, c, "xb", "xn")
        if stages >= 6:
            for c in range(NCH):
                s4_chunk(1, c, "xn", None, final=True)

        final_bufs = []
        if debug_spills:
            for n in ("xn", "attnb", "xa", "xb"):
                final_bufs += hbuf[n]
        final_bufs += hbuf["out"]
        final_bufs = [b for b in final_bufs if b.writer is not None]
        if final_bufs:
            kb.op("sync", lambda e: e.nop(), reads=final_bufs, writes=())

        kb.finalize_and_emit(sems)
    return nc


def kernel(**inputs):
    g = host_prep(inputs)
    x = np.asarray(inputs["L0_attn_x"], dtype=np.float32)
    nc = build_program(debug_spills=CFG["debug_spills"], stages=CFG["stages"])
    in_maps = []
    for core in range(8):
        m = dict(g)
        m["x"] = np.ascontiguousarray(x[core].T)
        in_maps.append(m)
    res = run_bass_kernel_spmd(nc, in_maps, core_ids=list(range(8)))
    outs = [r["out"] for r in res.results]
    return np.ascontiguousarray(np.stack(outs, 0).transpose(0, 2, 1))
